# version 01
# baseline (speedup 1.0000x reference)
import math
import numpy as np

import concourse.bass as bass
import concourse.mybir as mybir
from concourse.bass_utils import run_bass_kernel_spmd

F32 = mybir.dt.float32
BF16 = mybir.dt.bfloat16
AF = mybir.ActivationFunctionType
ALU = mybir.AluOpType
AX = mybir.AxisListType


class Prog:

    def __init__(self, nc: bass.Bass):
        self.nc = nc
        self.ops = []
        self.last_writer = {}
        self.readers = {}

    def _record(self, engine, emit, reads, writes, is_dma, tag):
        idx = len(self.ops)
        deps = set()
        for r in reads:
            if r in self.last_writer:
                deps.add(self.last_writer[r])
        for w in writes:
            if w in self.last_writer:
                deps.add(self.last_writer[w])
            for r in self.readers.get(w, ()):
                deps.add(r)
        deps.discard(idx)
        self.ops.append(dict(idx=idx, engine=engine, emit=emit, deps=deps,
                             is_dma=is_dma, needed=is_dma, tag=tag))
        for w in writes:
            self.last_writer[w] = idx
            self.readers[w] = []
        for r in reads:
            self.readers.setdefault(r, []).append(idx)
        return idx

    @staticmethod
    def _res(ap):
        return ap.tensor.name

    def _names(self, items):
        out = []
        for it in items:
            if it is None or isinstance(it, (int, float)):
                continue
            out.append(it if isinstance(it, str) else self._res(it))
        return out

    def dma(self, queue, out_ap, in_ap, tag="", **kw):
        def emit(eng):
            return eng.dma_start(out=out_ap, in_=in_ap, **kw)
        idx = self._record(queue, emit, self._names([in_ap]),
                           self._names([out_ap]), True, tag)
        self.ops[idx]["dest"] = self._names([out_ap])[0]
        return idx

    def tt(self, engine, out, in0, in1, aluop, tag=""):
        def emit(eng):
            return eng.tensor_tensor(out, in0, in1, aluop)
        return self._record(engine, emit, self._names([in0, in1]),
                            self._names([out]), False, tag)

    def ts(self, engine, out, in0, s1, op0, s2=None, op1=None, tag=""):
        def emit(eng):
            if op1 is None:
                return eng.tensor_scalar(out, in0, s1, None, op0)
            return eng.tensor_scalar(out, in0, s1, s2, op0, op1)
        return self._record(engine, emit, self._names([in0, s1, s2]),
                            self._names([out]), False, tag)

    def stt(self, engine, out, in0, scalar, in1, op0, op1, tag=""):
        def emit(eng):
            return eng.scalar_tensor_tensor(out, in0, scalar, in1, op0, op1)
        return self._record(engine, emit, self._names([in0, in1, scalar]),
                            self._names([out]), False, tag)

    def copy(self, engine, out, in_, tag=""):
        def emit(eng):
            if engine == "scalar":
                return eng.copy(out, in_)
            return eng.tensor_copy(out, in_)
        return self._record(engine, emit, self._names([in_]),
                            self._names([out]), False, tag)

    def op(self, engine, emit, reads=(), writes=(), tag=""):
        return self._record(engine, emit, self._names(list(reads)),
                            self._names(list(writes)), False, tag)

    def memset(self, engine, ap, const, tag=""):
        def emit(eng):
            return eng.memset(ap, const)
        return self._record(engine, emit, [], self._names([ap]), False, tag)

    def act(self, out, in_, func, bias=0.0, scale=1.0, tag="", accum_out=None):
        def emit(eng):
            kw = {} if accum_out is None else {"accum_out": accum_out}
            return eng.activation(out, in_, func, bias=bias, scale=scale, **kw)
        writes = [out] + ([accum_out] if accum_out is not None else [])
        return self._record("scalar", emit, self._names([in_, bias, scale]),
                            self._names(writes), False, tag)

    def reduce(self, engine, out, in_, axis, aluop, negate=None, tag=""):
        def emit(eng):
            kw = {}
            if negate is not None:
                kw["negate"] = negate
            return eng.tensor_reduce(out, in_, axis, aluop, **kw)
        return self._record(engine, emit, self._names([in_]),
                            self._names([out]), False, tag)

    def scan(self, engine, out, data0, data1, initial, op0, op1, tag=""):
        def emit(eng):
            return eng.tensor_tensor_scan(out, data0, data1, initial, op0, op1)
        return self._record(engine, emit, self._names([data0, data1, initial]),
                            self._names([out]), False, tag)

    def mm(self, out, pairs, start=True, stop=True, tag=""):
        n = len(pairs)

        def emit(eng):
            last = None
            for j, (lhsT, rhs) in enumerate(pairs):
                last = eng.matmul(out, lhsT, rhs,
                                  start=(start and j == 0),
                                  stop=(stop and j == n - 1))
            return last
        reads = []
        for lhsT, rhs in pairs:
            reads += [lhsT, rhs]
        return self._record("tensor", emit, self._names(reads),
                            self._names([out]), False, tag)

    def transpose(self, out, in_, identity, tag=""):
        def emit(eng):
            return eng.transpose(out, in_, identity)
        return self._record("tensor", emit, self._names([in_, identity]),
                            self._names([out]), False, tag)

    def emit(self):
        nc = self.nc
        for o in self.ops:
            for d in o["deps"]:
                self.ops[d]["needed"] = True
        streams = {}
        for o in self.ops:
            streams.setdefault(o["engine"], []).append(o)
        inc_rank = {}
        dma_keys = []
        dma_count = {}
        for eng_name, ops in streams.items():
            r = 0
            for o in ops:
                if not o["needed"]:
                    continue
                if o["is_dma"]:
                    key = "D_" + eng_name + "_" + o["dest"]
                    if key not in dma_count:
                        dma_count[key] = 0
                        dma_keys.append(key)
                    dma_count[key] += 16
                    o["semkey"] = key
                    inc_rank[o["idx"]] = dma_count[key]
                else:
                    r += 1
                    o["semkey"] = eng_name
                    inc_rank[o["idx"]] = r
        import contextlib
        stack = contextlib.ExitStack()
        sems = {e: stack.enter_context(nc.semaphore("S_" + e)) for e in streams}
        for key in dma_keys:
            sems[key] = stack.enter_context(nc.semaphore(key))
        block = stack.enter_context(nc.Block())
        totals = {}
        for o in self.ops:
            if o["needed"]:
                k = o["semkey"]
                totals[k] = max(totals.get(k, 0), inc_rank[o["idx"]])
        barrier_eng = self._barrier_engine()

        def make_section(eng_name):
            ops = streams[eng_name]

            def section(eng):
                waited = {}
                for o in ops:
                    bykey = {}
                    for d in o["deps"]:
                        dd = self.ops[d]
                        if dd["engine"] == "tensor" and eng_name == "tensor" \
                                and not dd["is_dma"]:
                            continue
                        k = dd["semkey"]
                        v = inc_rank[dd["idx"]]
                        if bykey.get(k, 0) < v:
                            bykey[k] = v
                    for k, v in sorted(bykey.items()):
                        if waited.get(k, 0) < v:
                            eng.wait_ge(sems[k], v)
                            waited[k] = v
                    inst = o["emit"](eng)
                    if o["needed"]:
                        inst.then_inc(sems[o["semkey"]],
                                      16 if o["is_dma"] else 1)
                if eng_name == barrier_eng:
                    for k, v in sorted(totals.items()):
                        if v > waited.get(k, 0) or k == eng_name:
                            if v > 0:
                                eng.wait_ge(sems[k], v)
            return section

        for eng_name in streams:
            getattr(block, eng_name)(make_section(eng_name))
        return stack

    def _barrier_engine(self):
        engs = {o["engine"] for o in self.ops}
        for e in ("gpsimd", "sync", "vector", "scalar", "tensor"):
            if e in engs:
                return e
        raise RuntimeError("empty program")


D = 1024
S = 4096
BATCH = 4
TH = 2048
NT = D // 128
LN_EPS = 1e-5


def emit_ln_chunk(P, srcs, sc, ps_mb, ps_rb, ps_a, ps_b, veng="vector",
                  d_total=D):
    n = len(srcs)
    inv = 1.0 / d_total
    P.mm(ps_a, [(sc["ones_col"][:], s) for s in srcs], tag="ln_s1")
    for j, s in enumerate(srcs):
        P.act(sc["sq2"][j % 2][:], s, AF.Square, tag="ln_sq")
        P.mm(ps_b, [(sc["ones_col"][:], sc["sq2"][j % 2][:])],
             start=(j == 0), stop=(j == n - 1), tag="ln_s2")
    P.copy(veng, sc["srow"][:], ps_a, tag="ln_ev1")
    P.copy(veng, sc["qrow"][:], ps_b, tag="ln_ev2")
    P.ts(veng, sc["srow"][:], sc["srow"][:], inv, ALU.mult, tag="ln_m")
    P.ts(veng, sc["qrow"][:], sc["qrow"][:], inv, ALU.mult, tag="ln_q")
    P.tt(veng, sc["sq2"][0][0:1, :], sc["srow"][:], sc["srow"][:], ALU.mult,
         tag="ln_m2")
    P.tt(veng, sc["qrow"][:], sc["qrow"][:], sc["sq2"][0][0:1, :],
         ALU.subtract, tag="ln_v")
    P.act(sc["qrow"][:], sc["qrow"][:], AF.Ln, bias=sc["eps1"][:], tag="ln_ln")
    P.act(sc["qrow"][:], sc["qrow"][:], AF.Exp, scale=-0.5, tag="ln_rstd")
    P.mm(ps_mb, [(sc["ones_row"][:], sc["srow"][:])], tag="ln_bm")
    P.mm(ps_rb, [(sc["ones_row"][:], sc["qrow"][:])], tag="ln_br")


def build_l4():
    nc = bass.Bass("TRN2", target_bir_lowering=False, debug=False)
    P = Prog(nc)
    M_TILES = 32
    TC = 512
    NCH = TH // TC

    x2h = nc.dram_tensor("x2h", [D, TH], F32, kind="ExternalInput").ap()
    oa = nc.dram_tensor("oa", [D, TH], F32, kind="ExternalInput").ap()
    ob = nc.dram_tensor("ob", [D, TH], F32, kind="ExternalInput").ap()
    w1t = nc.dram_tensor("w1t", [M_TILES, 128, D], F32,
                         kind="ExternalInput").ap()
    w2t = nc.dram_tensor("w2t", [NT, 128, 4096], F32,
                         kind="ExternalInput").ap()
    b1t = nc.dram_tensor("b1t", [128, M_TILES], F32, kind="ExternalInput").ap()
    b2t = nc.dram_tensor("b2t", [128, NT], F32, kind="ExternalInput").ap()
    ln2g = nc.dram_tensor("ln2g", [128, NT], F32, kind="ExternalInput").ap()
    ln2b = nc.dram_tensor("ln2b", [128, NT], F32, kind="ExternalInput").ap()
    sbias = nc.dram_tensor("sbias", [128, NT], F32, kind="ExternalInput").ap()
    m_in = nc.dram_tensor("m_in", [D, D], F32, kind="ExternalInput").ap()
    outh = nc.dram_tensor("outh", [D, TH], F32, kind="ExternalOutput").ap()
    w1s = nc.dram_tensor("w1s", [M_TILES, 128, D], BF16).ap()
    w2s = nc.dram_tensor("w2s", [NT, 128, 4096], BF16).ap()

    import contextlib
    stack = contextlib.ExitStack()
    sb = lambda name, shape, dt=F32: stack.enter_context(
        nc.sbuf_tensor(name, shape, dt))
    ps = lambda name, shape, dt=F32: stack.enter_context(
        nc.psum_tensor(name, shape, dt))

    xc = [sb(f"xc_{j}", [128, TC]) for j in range(NT)]
    ac = [sb(f"ac_{j}", [128, TC]) for j in range(NT)]
    ln2a = [sb(f"ln2a_{j}", [128, TC], BF16) for j in range(NT)]
    xn3 = [sb(f"xn3_{j}", [128, TC], BF16) for j in range(NT)]
    h1 = [sb(f"h1_{j}", [128, TC], BF16) for j in range(M_TILES)]
    Mbf = [sb(f"Mbf_{j}", [128, D], BF16) for j in range(NT)]
    wstg = [sb(f"wstg_{j}", [128, D], BF16) for j in range(2)]
    w2stg = [sb(f"w2stg_{j}", [128, 4096], BF16) for j in range(2)]
    cstg = [sb(f"cstg_{j}", [128, D]) for j in range(2)]
    pstg = [sb(f"pstg_{j}", [128, TC]) for j in range(4)]
    otile = [sb(f"otile_{j}", [128, TC]) for j in range(2)]
    tmpc = [sb(f"tmpc_{j}", [128, TC]) for j in range(2)]
    b1c = sb("b1c", [128, M_TILES])
    b2c = sb("b2c", [128, NT])
    gln = sb("gln", [128, NT])
    bln = sb("bln", [128, NT])
    sbc = sb("sbc", [128, NT])
    ones_col = sb("ones_col", [128, 1])
    ones_row = sb("ones_row", [1, 128])
    eps1 = sb("eps1", [1, 1])
    sq2 = [sb(f"sq2_{j}", [128, TC]) for j in range(2)]
    srow = sb("srow", [1, TC])
    qrow = sb("qrow", [1, TC])

    pb = [ps(f"pb{j}", [128, TC]).ap() for j in range(8)]

    sc = dict(ones_col=ones_col, ones_row=ones_row, sq2=sq2, srow=srow,
              qrow=qrow, eps1=eps1)

    with stack:
        P.memset("vector", ones_col[:], 1.0)
        P.memset("vector", ones_row[:], 1.0)
        P.memset("vector", eps1[:], LN_EPS)
        P.dma("sync", b1c[:], b1t)
        P.dma("sync", b2c[:], b2t)
        P.dma("sync", gln[:], ln2g)
        P.dma("sync", bln[:], ln2b)
        P.dma("sync", sbc[:], sbias)
        for j in range(NT):
            s = cstg[j % 2]
            P.dma("sync", s[:], m_in[j * 128:(j + 1) * 128, :])
            P.copy("gpsimd", Mbf[j][:], s[:])
        for mt in range(M_TILES):
            s = cstg[mt % 2]
            w = wstg[mt % 2]
            P.dma("gpsimd", s[:], w1t[mt])
            P.copy("gpsimd", w[:], s[:])
            P.dma("gpsimd", w1s[mt], w[:])
        for ot in range(NT):
            for h in range(4):
                s = cstg[h % 2]
                w = wstg[h % 2]
                csl = slice(h * D, (h + 1) * D)
                P.dma("gpsimd", s[:], w2t[ot][:, csl])
                P.copy("gpsimd", w[:], s[:])
                P.dma("gpsimd", w2s[ot][:, csl], w[:])

        for c in range(NCH):
            sl = slice(c * TC, (c + 1) * TC)
            for j in range(NT):
                P.dma("sync", xc[j][:], x2h[j * 128:(j + 1) * 128, sl],
                      tag="ld_x2")
            emit_ln_chunk(P, [xc[j][:] for j in range(NT)], sc,
                          pb[0], pb[1], pb[2][0:1, :], pb[3][0:1, :])
            for j in range(NT):
                P.tt("vector", tmpc[j % 2][:], xc[j][:], pb[0], ALU.subtract,
                     tag="a_sub")
                P.tt("vector", ac[j][:], tmpc[j % 2][:], pb[1], ALU.mult,
                     tag="a_mul")
            k = 0
            for j in range(NT):
                s0 = pstg[k % 4]; s1 = pstg[(k + 1) % 4]; k += 2
                P.dma("sync", s0[:], oa[j * 128:(j + 1) * 128, sl])
                P.dma("sync", s1[:], ob[j * 128:(j + 1) * 128, sl])
                P.tt("vector", ac[j][:], ac[j][:], s0[:], ALU.add)
                P.tt("vector", ac[j][:], ac[j][:], s1[:], ALU.add)
            emit_ln_chunk(P, [ac[j][:] for j in range(NT)], sc,
                          pb[4], pb[5], pb[6][0:1, :], pb[7][0:1, :])
            for j in range(NT):
                P.tt("vector", tmpc[j % 2][:], ac[j][:], pb[4], ALU.subtract,
                     tag="l2_sub")
                P.tt("vector", tmpc[j % 2][:], tmpc[j % 2][:], pb[5], ALU.mult,
                     tag="l2_mul")
                P.stt("vector", ln2a[j][:], tmpc[j % 2][:], gln[:, j:j + 1],
                      bln[:, j:j + 1].broadcast_to((128, TC)), ALU.mult,
                      ALU.add, tag="l2_aff")
                P.tt("gpsimd", xc[j][:], xc[j][:], ac[j][:], ALU.add,
                     tag="z_add")
            for mt in range(M_TILES):
                w = wstg[mt % 2]
                P.dma("gpsimd", w[:], w1s[mt], tag="w1_ld")
                pm = pb[mt % 2]
                P.mm(pm, [(w[:, j * 128:(j + 1) * 128], ln2a[j][:])
                          for j in range(NT)], tag="mm_h1")
                P.act(h1[mt][:], pm, AF.Gelu, bias=b1c[:, mt:mt + 1],
                      tag="gelu")
            for j in range(NT):
                w = w2stg[j % 2]
                P.dma("gpsimd", w[:], w2s[j], tag="w2_ld")
                pm = pb[2 + (j % 2)]
                P.mm(pm, [(w[:, mt * 128:(mt + 1) * 128], h1[mt][:])
                          for mt in range(M_TILES)], tag="mm_h2")
                P.stt("vector", xc[j][:], pm, b2c[:, j:j + 1], xc[j][:],
                      ALU.add, ALU.add, tag="x3")
            emit_ln_chunk(P, [xc[j][:] for j in range(NT)], sc,
                          pb[4], pb[5], pb[6][0:1, :], pb[7][0:1, :])
            for j in range(NT):
                P.tt("vector", tmpc[j % 2][:], xc[j][:], pb[4], ALU.subtract,
                     tag="n3_sub")
                P.tt("vector", xn3[j][:], tmpc[j % 2][:], pb[5], ALU.mult,
                     tag="n3_mul")
            for j in range(NT):
                pm = pb[6 if j % 2 == 0 else 7]
                P.mm(pm, [(Mbf[jj][:, j * 128:(j + 1) * 128], xn3[jj][:])
                          for jj in range(NT)], tag="mm_y")
                ot = otile[j % 2]
                P.stt("vector", ot[:], pm, sbc[:, j:j + 1], xc[j][:],
                      ALU.add, ALU.add, tag="spec_out")
                P.dma("gpsimd", outh[j * 128:(j + 1) * 128, sl], ot[:],
                      tag="out")

        ctx = P.emit()
        with ctx:
            pass
    return nc


def build_l2():
    nc = bass.Bass("TRN2", target_bir_lowering=False, debug=False)
    P = Prog(nc)

    xh = nc.dram_tensor("xh", [D, TH], F32, kind="ExternalInput").ap()
    pa = nc.dram_tensor("pa", [D, TH], F32, kind="ExternalInput").ap()
    pb = nc.dram_tensor("pb", [D, TH], F32, kind="ExternalInput").ap()
    halo = nc.dram_tensor("halo", [D, 3], F32, kind="ExternalInput").ap()
    w_r = nc.dram_tensor("w_r", [D, D], F32, kind="ExternalInput").ap()
    w_k = nc.dram_tensor("w_k", [D, D], F32, kind="ExternalInput").ap()
    w_v = nc.dram_tensor("w_v", [D, D], F32, kind="ExternalInput").ap()
    mu2 = nc.dram_tensor("mu2", [D, 2], F32, kind="ExternalInput").ap()
    x2o = nc.dram_tensor("x2o", [D, TH], F32, kind="ExternalOutput").ap()

    TC = 512
    NCH = TH // TC

    import contextlib
    stack = contextlib.ExitStack()
    sb = lambda name, shape, dt=F32: stack.enter_context(
        nc.sbuf_tensor(name, shape, dt))
    ps = lambda name, shape, dt=F32: stack.enter_context(
        nc.psum_tensor(name, shape, dt))

    x1 = [sb(f"x1_{j}", [128, TH]) for j in range(NT)]
    xn = [sb(f"xn_{j}", [128, 1 + TH], BF16) for j in range(NT)]
    pstg = [sb(f"pstg_{j}", [128, TC]) for j in range(4)]
    tmp_row = None
    halo_t = [sb(f"halo_{j}", [128, 3]) for j in range(NT)]
    hprev = [sb(f"hprev_{j}", [128, 1]) for j in range(NT)]
    hsq = [sb(f"hsq_{j}", [128, 1]) for j in range(NT)]
    mu_t = sb("mu_t", [128, 2 * NT])
    wR = [sb(f"wR_{j}", [128, D], BF16) for j in range(NT)]
    wK = [sb(f"wK_{j}", [128, D], BF16) for j in range(NT)]
    wV = [sb(f"wV_{j}", [128, D], BF16) for j in range(NT)]
    ones_col = sb("ones_col", [128, 1])
    ones_row = sb("ones_row", [1, 128])
    srow_t = sb("srow_t", [1, TH])
    qrow_t = sb("qrow_t", [1, TH])
    hrow = sb("hrow", [1, 2])
    eps1 = sb("eps1", [1, 1])
    hvar = sb("hvar", [1, 1])
    xr = [sb(f"xr_{j}", [128, TC], BF16) for j in range(NT)]
    xk = [sb(f"xk_{j}", [128, TC], BF16) for j in range(NT)]
    r_c = [sb(f"rc_{j}", [128, TC], BF16) for j in range(NT)]
    kact = [sb(f"kact_{j}", [128, TC], BF16) for j in range(NT)]
    dltc = sb("dltc", [128, TC], BF16)
    tmp_tc = sb("tmp_tc", [128, TC])
    tmp_tc2 = sb("tmp_tc2", [128, TC])

    ps_s1 = ps("ps_s1", [1, TC]).ap()
    ps_s2 = ps("ps_s2", [1, TC]).ap()
    ps_mb = ps("ps_mb", [128, TC]).ap()
    ps_rb = ps("ps_rb", [128, TC]).ap()
    ps_mm = [ps(f"ps_mm{j}", [128, TC]).ap() for j in range(3)]
    ps_h = ps_s1[:, 0:2]
    ps_hb = ps_mb[:, 0:2]

    srow = srow_t
    qrow = qrow_t

    with stack:
        P.memset("vector", ones_col[:], 1.0)
        P.memset("vector", ones_row[:], 1.0)
        P.memset("vector", eps1[:], LN_EPS)
        for j in range(NT):
            P.dma("sync", x1[j][:], xh[j * 128:(j + 1) * 128, :])
            P.dma("sync", halo_t[j][:], halo[j * 128:(j + 1) * 128, :])
        P.dma("sync", mu_t[:].rearrange("p (n c) -> p n c", c=2), mu2.rearrange("(n p) c -> p n c", p=128))
        k = 0
        for j in range(NT):
            rowsl = slice(j * 128, (j + 1) * 128)
            for c in range(NCH):
                sl = slice(c * TC, (c + 1) * TC)
                s0 = pstg[k % 4]; s1 = pstg[(k + 1) % 4]; k += 2
                P.dma("sync", s0[:], pa[rowsl, sl])
                P.dma("sync", s1[:], pb[rowsl, sl])
                P.tt("vector", x1[j][:, sl], x1[j][:, sl], s0[:], ALU.add)
                P.tt("vector", x1[j][:, sl], x1[j][:, sl], s1[:], ALU.add)
            P.tt("gpsimd", hprev[j][:], halo_t[j][:, 0:1], halo_t[j][:, 1:2],
                 ALU.add)
            P.tt("gpsimd", hprev[j][:], hprev[j][:], halo_t[j][:, 2:3],
                 ALU.add)
        k2 = 0
        for j in range(NT):
            rowsl = slice(j * 128, (j + 1) * 128)
            for (wsrc, wdst) in ((w_r, wR), (w_k, wK), (w_v, wV)):
                for h in range(2):
                    s = pstg[k2 % 4]; k2 += 1
                    csl = slice(h * TC, (h + 1) * TC)
                    P.dma("gpsimd", s[:], wsrc[rowsl, csl])
                    P.copy("gpsimd", wdst[j][:, csl], s[:])

        inv = 1.0 / D
        for c in range(NCH):
            sl = slice(c * TC, (c + 1) * TC)
            P.mm(ps_s1, [(ones_col[:], x1[j][:, sl]) for j in range(NT)],
                 tag="ln_s1")
            for j in range(NT):
                P.act((tmp_tc if j % 2 == 0 else tmp_tc2)[:], x1[j][:, sl], AF.Square, tag="ln_sq")
                P.mm(ps_s2, [(ones_col[:], (tmp_tc if j % 2 == 0 else tmp_tc2)[:])], start=(j == 0),
                     stop=(j == NT - 1), tag="ln_s2")
            P.copy("vector", srow[:, sl], ps_s1, tag="ln_ev1")
            P.copy("vector", qrow[:, sl], ps_s2, tag="ln_ev2")
        P.ts("vector", srow[:], srow[:], inv, ALU.mult, tag="ln_m")
        P.ts("vector", qrow[:], qrow[:], inv, ALU.mult, tag="ln_q")
        P.tt("vector", pstg[0][0:1, :], srow[:, 0:TC], srow[:, 0:TC], ALU.mult, tag="ln_m2")
        P.tt("vector", qrow[:, 0:TC], qrow[:, 0:TC], pstg[0][0:1, :], ALU.subtract, tag="ln_v")
        P.tt("vector", pstg[1][0:1, :], srow[:, TC:2 * TC], srow[:, TC:2 * TC], ALU.mult, tag="ln_m2")
        P.tt("vector", qrow[:, TC:2 * TC], qrow[:, TC:2 * TC], pstg[1][0:1, :], ALU.subtract, tag="ln_v")
        P.tt("vector", pstg[2][0:1, :], srow[:, 2 * TC:3 * TC], srow[:, 2 * TC:3 * TC], ALU.mult, tag="ln_m2")
        P.tt("vector", qrow[:, 2 * TC:3 * TC], qrow[:, 2 * TC:3 * TC], pstg[2][0:1, :], ALU.subtract, tag="ln_v")
        P.tt("vector", pstg[3][0:1, :], srow[:, 3 * TC:4 * TC], srow[:, 3 * TC:4 * TC], ALU.mult, tag="ln_m2")
        P.tt("vector", qrow[:, 3 * TC:4 * TC], qrow[:, 3 * TC:4 * TC], pstg[3][0:1, :], ALU.subtract, tag="ln_v")
        P.act(qrow[:], qrow[:], AF.Ln, bias=eps1[:], tag="ln_ln")
        P.act(qrow[:], qrow[:], AF.Exp, scale=-0.5, tag="ln_rstd")
        for c in range(NCH):
            sl = slice(c * TC, (c + 1) * TC)
            sl1 = slice(1 + c * TC, 1 + (c + 1) * TC)
            P.mm(ps_mb, [(ones_row[:], srow[:, sl])], tag="ln_bm")
            P.mm(ps_rb, [(ones_row[:], qrow[:, sl])], tag="ln_br")
            for j in range(NT):
                P.tt("vector", tmp_tc[:], x1[j][:, sl], ps_mb, ALU.subtract,
                     tag="ln_sub")
                P.tt("vector", xn[j][:, sl1], tmp_tc[:], ps_rb, ALU.mult,
                     tag="ln_mul")

        P.mm(ps_h[:, 0:1], [(ones_col[:], hprev[j][:]) for j in range(NT)])
        for j in range(NT):
            P.act(hsq[j][:], hprev[j][:], AF.Square)
        P.mm(ps_h[:, 1:2], [(ones_col[:], hsq[j][:]) for j in range(NT)])
        P.copy("vector", hrow[:], ps_h)
        P.ts("gpsimd", hrow[:], hrow[:], 1.0 / D, ALU.mult)
        P.tt("gpsimd", hvar[:], hrow[:, 0:1], hrow[:, 0:1], ALU.mult)
        P.tt("gpsimd", hrow[:, 1:2], hrow[:, 1:2], hvar[:], ALU.subtract)
        P.act(hrow[:, 1:2], hrow[:, 1:2], AF.Ln, bias=eps1[:])
        P.act(hrow[:, 1:2], hrow[:, 1:2], AF.Exp, scale=-0.5)
        P.mm(ps_hb, [(ones_row[:], hrow[:])])
        for j in range(NT):
            P.tt("vector", hsq[j][:], hprev[j][:], ps_hb[:, 0:1], ALU.subtract)
            P.tt("vector", xn[j][:, 0:1], hsq[j][:], ps_hb[:, 1:2], ALU.mult)

        for c in range(NCH):
            sl = slice(c * TC, (c + 1) * TC)
            for j in range(NT):
                cur = xn[j][:, 1 + c * TC: 1 + (c + 1) * TC]
                prev = xn[j][:, c * TC: (c + 1) * TC]
                P.tt("vector", dltc[:], cur, prev, ALU.subtract, tag="sh_d")
                P.stt("vector", xr[j][:], dltc[:], mu_t[:, 2 * j:2 * j + 1],
                      prev, ALU.mult, ALU.add, tag="sh_r")
                P.stt("vector", xk[j][:], dltc[:], mu_t[:, 2 * j + 1:2 * j + 2],
                      prev, ALU.mult, ALU.add, tag="sh_k")
            for j2 in range(NT):
                cols = slice(j2 * 128, (j2 + 1) * 128)
                pm = ps_mm[j2 % 2]
                P.mm(pm, [(wR[j][:, cols], xr[j][:]) for j in range(NT)],
                     tag="mm_r")
                P.act(r_c[j2][:], pm, AF.Sigmoid, tag="sig")
                pm2 = ps_mm[(j2 + 1) % 2]
                P.mm(pm2, [(wK[j][:, cols], xk[j][:]) for j in range(NT)],
                     tag="mm_k")
                P.ts("vector", tmp_tc[:], pm2, 0.0, ALU.max, tag="relu")
                P.act(kact[j2][:], tmp_tc[:], AF.Square, tag="ksq")
            for j2 in range(NT):
                cols = slice(j2 * 128, (j2 + 1) * 128)
                pm = ps_mm[2]
                P.mm(pm, [(wV[j][:, cols], kact[j][:]) for j in range(NT)],
                     tag="mm_v")
                P.tt("vector", tmp_tc2[:], r_c[j2][:], pm, ALU.mult,
                     tag="rmul")
                P.tt("vector", x1[j2][:, sl], x1[j2][:, sl], tmp_tc2[:],
                     ALU.add, tag="resid")
                P.dma("gpsimd", x2o[j2 * 128:(j2 + 1) * 128, sl],
                      x1[j2][:, sl], tag="out")

        ctx = P.emit()
        with ctx:
            pass
    return nc


def _fm(x):
    return np.ascontiguousarray(x.T)


def l2_in_maps(x0_fm, partials, inputs):
    maps = []
    for c in range(8):
        b, i = c // 2, c % 2
        t0 = i * TH
        pA = partials[2 * b]
        pB = partials[2 * b + 1]
        if i == 0:
            halo = np.zeros((D, 3), np.float32)
        else:
            halo = np.stack([x0_fm[b][:, t0 - 1], pA[:, t0 - 1],
                             pB[:, t0 - 1]], axis=1)
        maps.append({
            "xh": np.ascontiguousarray(x0_fm[b][:, t0:t0 + TH]),
            "pa": np.ascontiguousarray(pA[:, t0:t0 + TH]),
            "pb": np.ascontiguousarray(pB[:, t0:t0 + TH]),
            "halo": np.ascontiguousarray(halo),
            "w_r": np.ascontiguousarray(inputs["L0_ffn_W_r"].T),
            "w_k": np.ascontiguousarray(inputs["L0_ffn_W_k"].T),
            "w_v": np.ascontiguousarray(inputs["L0_ffn_W_v"].T),
            "mu2": np.ascontiguousarray(
                np.stack([inputs["L0_ffn_mu_r"], inputs["L0_ffn_mu_k"]],
                         axis=1)),
        })
    return maps


def build_l1():
    nc = bass.Bass("TRN2", target_bir_lowering=False, debug=False)
    P = Prog(nc)
    NCH = S // 512
    NKB = S // 128

    x_full = nc.dram_tensor("x_full", [D, S], F32, kind="ExternalInput").ap()
    wq_t = nc.dram_tensor("wq_t", [D, 512], F32, kind="ExternalInput").ap()
    wk_t = nc.dram_tensor("wk_t", [D, 512], F32, kind="ExternalInput").ap()
    wv_t = nc.dram_tensor("wv_t", [D, 520], F32, kind="ExternalInput").ap()
    wo_t = nc.dram_tensor("wo_t", [512, D], F32, kind="ExternalInput").ap()
    ropec = nc.dram_tensor("ropec", [128, S], BF16, kind="ExternalInput").ap()
    ropes = nc.dram_tensor("ropes", [128, S], BF16, kind="ExternalInput").ap()
    psign = nc.dram_tensor("psign", [128, 128], BF16, kind="ExternalInput").ap()
    masks = nc.dram_tensor("masks", [4, 128, 512], BF16,
                           kind="ExternalInput").ap()
    part = nc.dram_tensor("part", [D, S], F32, kind="ExternalOutput").ap()

    import contextlib
    stack = contextlib.ExitStack()
    sb = lambda name, shape, dt=F32: stack.enter_context(
        nc.sbuf_tensor(name, shape, dt))
    ps = lambda name, shape, dt=F32: stack.enter_context(
        nc.psum_tensor(name, shape, dt))

    xn = [sb(f"xn_{j}", [128, S], BF16) for j in range(NT)]
    krot = [sb(f"krot_{t}", [128, S], BF16) for t in range(4)]
    vtm = [sb(f"vtm_{t}", [128, 520], BF16) for t in range(NKB)]
    wqb = [sb(f"wqb_{j}", [128, 512], BF16) for j in range(NT)]
    wkb = wqb
    wvb = [sb(f"wvb_{j}", [128, 520], BF16) for j in range(NT)]
    wob = [sb(f"wob_{t}", [128, D], BF16) for t in range(4)]
    cos_t = [sb(f"cos_t{j}", [128, 512], BF16) for j in range(2)]
    sin_t = [sb(f"sin_t{j}", [128, 512], BF16) for j in range(2)]
    psg = sb("psg", [128, 128], BF16)
    mask_t = [sb(f"mask_{o}", [128, 512], BF16) for o in range(4)]
    xc = [sb(f"xc_{j}", [128, 512]) for j in range(NT)]
    qrot = [sb(f"qrot_{t}", [128, 512], BF16) for t in range(4)]
    qsb = sb("qsb", [128, 512], BF16)
    swsb = sb("swsb", [128, 512], BF16)
    sqk = sb("sqk", [128, 512])
    ptile = [sb(f"ptile_{j}", [128, 512], BF16) for j in range(2)]
    t1sb = ptile[0]
    t2sb = ptile[1]
    attn = [sb(f"attn_{t}", [128, 512], BF16) for t in range(4)]
    kmax = sb("kmax", [1, 8])
    qm1 = sb("qm1", [1, 1])
    mneg = [sb(f"mneg_{j}", [128, 1]) for j in range(2)]
    linv = sb("linv", [1, 512])
    linb = sb("linb", [64, 512])
    evac = [sb(f"evac_{j}", [128, 512]) for j in range(2)]
    ones_col = sb("ones_col", [128, 1])
    ones_row = sb("ones_row", [1, 128])
    eps1 = sb("eps1", [1, 1])
    sq2 = evac
    srow = sb("srow", [1, 512])
    qrow = sb("qrow", [1, 512])

    pb = [ps(f"pb{j}", [128, 512]).ap() for j in range(8)]

    sc = dict(ones_col=ones_col, ones_row=ones_row, sq2=sq2, srow=srow,
              qrow=qrow, eps1=eps1)

    with stack:
        P.memset("vector", ones_col[:], 1.0)
        P.memset("vector", ones_row[:], 1.0)
        P.memset("vector", eps1[:], LN_EPS)
        P.memset("vector", kmax[:], 0.0)
        P.dma("sync", psg[:], psign)
        for o in range(4):
            P.dma("sync", mask_t[o][:], masks[o])
        for j in range(NT):
            rows = slice(j * 128, (j + 1) * 128)
            s = evac[j % 2]
            P.dma("gpsimd", s[:], wk_t[rows, :])
            P.copy("gpsimd", wkb[j][:], s[:])
            P.dma("gpsimd", s[:], wv_t[rows, 0:512])
            P.copy("gpsimd", wvb[j][:, 0:512], s[:])
            P.dma("gpsimd", s[:, 0:8], wv_t[rows, 512:520])
            P.copy("gpsimd", wvb[j][:, 512:520], s[:, 0:8])
        for t in range(4):
            rows = slice(t * 128, (t + 1) * 128)
            s = evac[t % 2]
            P.dma("gpsimd", s[:, 0:512], wo_t[rows, 0:512])
            P.copy("gpsimd", wob[t][:, 0:512], s[:, 0:512])
            P.dma("gpsimd", s[:, 0:512], wo_t[rows, 512:1024])
            P.copy("gpsimd", wob[t][:, 512:1024], s[:, 0:512])

        for c in range(NCH):
            sl = slice(c * 512, (c + 1) * 512)
            for j in range(NT):
                P.dma("sync", xc[j][:], x_full[j * 128:(j + 1) * 128, sl],
                      tag="ld_x")
            emit_ln_chunk(P, [xc[j][:] for j in range(NT)], sc,
                          pb[0], pb[1], pb[2][0:1, :], pb[3][0:1, :])
            for j in range(NT):
                P.tt("vector", sqk[:], xc[j][:], pb[0], ALU.subtract,
                     tag="ln_sub")
                P.tt("vector", xn[j][:, sl], sqk[:], pb[1], ALU.mult,
                     tag="ln_mul")

        rope_state = {"sl": None}

        def rope_tabs(sl):
            if rope_state["sl"] != (sl.start, sl.stop):
                rope_state["sl"] = (sl.start, sl.stop)
                P.dma("sync", cos_t[0][:], ropec[:, sl], tag="rp_tc")
                P.dma("sync", sin_t[0][:], ropes[:, sl], tag="rp_ts")

        def rope(dst, src_ps, sl, veng="vector"):
            rope_tabs(sl)
            P.copy(veng, qsb[:], src_ps, tag="rp_ev")
            P.mm(pb[7], [(psg[:], qsb[:])], tag="rp_sw")
            P.copy(veng, swsb[:], pb[7], tag="rp_swev")
            P.tt(veng, t1sb[:], qsb[:], cos_t[0][:], ALU.mult, tag="rp_c")
            P.tt(veng, t2sb[:], swsb[:], sin_t[0][:], ALU.mult, tag="rp_s")
            P.tt(veng, dst, t1sb[:], t2sb[:], ALU.add, tag="rp_a")

        for c in range(NCH):
            sl = slice(c * 512, (c + 1) * 512)
            for t in range(4):
                cols = slice(t * 128, (t + 1) * 128)
                pm = pb[t % 2]
                P.mm(pm, [(wkb[j][:, cols], xn[j][:, sl]) for j in range(NT)],
                     tag="mm_k")
                rope(krot[t][:, sl], pm, sl)
                P.act(sqk[:], krot[t][:, sl], AF.Square, tag="kn_sq")
                for hh in range(2):
                    h = 2 * t + hh
                    base = 64 * hh
                    pn = pb[2 + hh][0:1, :]
                    P.mm(pn, [(ones_col[base:base + 64, :],
                               sqk[base:base + 64, :])], tag="kn_mm")
                    P.reduce("vector", qm1[:], pn, AX.X, ALU.max, tag="kn_rd")
                    P.tt("vector", kmax[:, h:h + 1], kmax[:, h:h + 1], qm1[:],
                         ALU.max, tag="kn_max")
            for tt4 in range(4):
                ti = c * 4 + tt4
                tsl = slice(ti * 128, (ti + 1) * 128)
                pv1 = pb[4 + tt4 % 2]
                pv2 = pb[6][:, 0:8] if tt4 % 2 == 0 else pb[7][:, 0:8]
                P.mm(pv1, [(xn[j][:, tsl], wvb[j][:, 0:512])
                           for j in range(NT)], tag="mm_v1")
                P.mm(pv2, [(xn[j][:, tsl], wvb[j][:, 512:520])
                           for j in range(NT)], tag="mm_v2")
                P.copy("scalar", vtm[ti][:, 0:512], pv1, tag="v_ev1")
                P.copy("scalar", vtm[ti][:, 512:520], pv2, tag="v_ev2")
                ones_view = vtm[ti][:].rearrange("p (h c) -> p h c", c=65)
                P.memset("gpsimd", ones_view[:, :, 64:65], 1.0, tag="v_ones")

        for j in range(NT):
            rows = slice(j * 128, (j + 1) * 128)
            s = evac[j % 2]
            P.dma("gpsimd", s[:], wq_t[rows, :])
            P.copy("gpsimd", wqb[j][:], s[:])

        for jq in range(NCH):
            sl = slice(jq * 512, (jq + 1) * 512)
            nkb = 4 * jq + 4
            for t in range(4):
                cols = slice(t * 128, (t + 1) * 128)
                pm = pb[t % 2]
                P.mm(pm, [(wqb[j][:, cols], xn[j][:, sl]) for j in range(NT)],
                     tag="mm_q")
                rope(qrot[t][:], pm, sl)
            for t in range(4):
                P.act(sqk[:], qrot[t][:], AF.Square, tag="qn_sq")
                for hh in range(2):
                    h = 2 * t + hh
                    base = 64 * hh
                    pn = pb[2 + hh][0:1, :]
                    P.mm(pn, [(ones_col[base:base + 64, :],
                               sqk[base:base + 64, :])], tag="qn_mm")
                    P.reduce("vector", qm1[:], pn, AX.X, ALU.max, tag="qn_rd")
                    P.tt("vector", qm1[:], qm1[:], kmax[:, h:h + 1], ALU.mult,
                         tag="qn_mul")
                    P.act(qm1[:], qm1[:], AF.Sqrt, tag="qn_sqrt")
                    P.ts("vector", qm1[:], qm1[:], -0.125, ALU.mult,
                         tag="qn_neg")
                    P.mm(pb[2 + hh][:, 0:1], [(ones_row[:], qm1[:])],
                         tag="qn_bc")
                    P.copy("vector", mneg[hh][:], pb[2 + hh][:, 0:1],
                           tag="qn_ev")
                for hh in range(2):
                    h = 2 * t + hh
                    base = 64 * hh
                    ps_s = pb[2 + hh]
                    ps_o = pb[4 + hh]
                    for kb in range(nkb):
                        ksl = slice(kb * 128, (kb + 1) * 128)
                        P.mm(ps_s, [(krot[t][base:base + 64, ksl],
                                     qrot[t][base:base + 64, :])],
                             tag="mm_s")
                        pt = ptile[hh]
                        P.act(pt[:], ps_s, AF.Exp, bias=mneg[hh][:],
                              scale=0.125, tag="exp")
                        moff = kb - 4 * jq
                        if moff >= 0:
                            P.tt("gpsimd", pt[:], pt[:], mask_t[moff][:],
                                 ALU.mult, tag="mask")
                        P.mm(ps_o[0:65, :],
                             [(vtm[kb][:, 65 * h:65 * h + 65], pt[:])],
                             start=(kb == 0), stop=(kb == nkb - 1),
                             tag="mm_pv")
                    P.copy("vector", linv[:], ps_o[64:65, :], tag="l_ev")
                    P.op("vector",
                         lambda eng, o=linv: eng.reciprocal(o[:], o[:]),
                         reads=["linv"], writes=["linv"], tag="l_rcp")
                    P.mm(pb[6][0:64, :], [(ones_row[:, 0:64], linv[:])],
                         tag="l_bc")
                    P.copy("vector", linb[:], pb[6][0:64, :], tag="l_bev")
                    P.tt("vector", attn[t][base:base + 64, :],
                         ps_o[0:64, :], linb[:], ALU.mult, tag="attn_div")
            for j in range(NT):
                dsl = slice(j * 128, (j + 1) * 128)
                pm = pb[6 + j % 2]
                P.mm(pm, [(wob[t][:, dsl], attn[t][:]) for t in range(4)],
                     tag="mm_wo")
                ev = evac[j % 2]
                P.copy("vector", ev[:], pm, tag="wo_ev")
                P.dma("gpsimd", part[dsl, sl], ev[:], tag="out")

        ctx = P.emit()
        with ctx:
            pass
    return nc


def l1_in_maps(x0_fm, inputs):
    D_K = 64
    theta = 10000.0
    pos = np.arange(S, dtype=np.float64)
    rates = theta ** (-(2.0 * np.arange(32, dtype=np.float64)) / D_K)
    ang = pos[None, :] * rates[:, None]
    import ml_dtypes
    cos32 = np.cos(ang)
    sin32 = np.sin(ang)
    ropec = np.tile(cos32, (4, 1)).astype(ml_dtypes.bfloat16)
    sgn = np.concatenate([-np.ones((32, 1)), np.ones((32, 1)),
                          -np.ones((32, 1)), np.ones((32, 1))], axis=0)
    ropes = (np.tile(sin32, (4, 1)) * sgn).astype(ml_dtypes.bfloat16)
    psign = np.zeros((128, 128), np.float64)
    for blk in range(2):
        b0 = 64 * blk
        psign[b0:b0 + 32, b0 + 32:b0 + 64] = np.eye(32)
        psign[b0 + 32:b0 + 64, b0:b0 + 32] = np.eye(32)
    psign = np.ascontiguousarray(psign.T).astype(ml_dtypes.bfloat16)
    masks = np.zeros((4, 128, 512), np.float32)
    for o in range(4):
        off = 128 * o
        for k in range(128):
            masks[o, k, off + k:] = 1.0
    masks = masks.astype(ml_dtypes.bfloat16)

    maps = []
    WqT = inputs["L0_attn_Wq"].T
    WkT = inputs["L0_attn_Wk"].T
    WvT = inputs["L0_attn_Wv"].T
    WoT = inputs["L0_attn_Wo"].T
    for c in range(8):
        b, i = c // 2, c % 2
        heads = list(range(8 * i, 8 * i + 8))
        cols = []
        for h in heads:
            cols += [64 * h + 2 * p for p in range(32)]
            cols += [64 * h + 2 * p + 1 for p in range(32)]
        cols = np.array(cols)
        wq = np.ascontiguousarray(WqT[:, cols])
        wk = np.ascontiguousarray(WkT[:, cols])
        wv = np.zeros((D, 520), np.float32)
        for hl, h in enumerate(heads):
            wv[:, 65 * hl:65 * hl + 64] = WvT[:, 64 * h:64 * h + 64]
        wo = np.ascontiguousarray(WoT[512 * i:512 * i + 512, :])
        maps.append({
            "x_full": x0_fm[b], "wq_t": wq, "wk_t": wk,
            "wv_t": np.ascontiguousarray(wv), "wo_t": wo,
            "ropec": ropec, "ropes": ropes, "psign": psign, "masks": masks,
        })
    return maps


def build_l3(core_half, nch_limit=None):
    nc = bass.Bass("TRN2", target_bir_lowering=False, debug=False)
    P = Prog(nc)
    TCC = 256
    NCHC = S // TCC
    NST = 16 * TCC

    x2f = nc.dram_tensor("x2f", [D, S], F32, kind="ExternalInput").ap()
    ln1g = nc.dram_tensor("ln1g", [128, NT], F32, kind="ExternalInput").ap()
    ln1b = nc.dram_tensor("ln1b", [128, NT], F32, kind="ExternalInput").ap()
    inx_t = nc.dram_tensor("inx_t", [D, D], F32, kind="ExternalInput").ap()
    inz_t = nc.dram_tensor("inz_t", [D, 512], F32, kind="ExternalInput").ap()
    cxw = nc.dram_tensor("cxw", [128, 32], F32, kind="ExternalInput").ap()
    cxb = nc.dram_tensor("cxb", [128, 8], F32, kind="ExternalInput").ap()
    czw = nc.dram_tensor("czw", [128, 16], F32, kind="ExternalInput").ap()
    czb = nc.dram_tensor("czb", [128, 4], F32, kind="ExternalInput").ap()
    xpj_t = nc.dram_tensor("xpj_t", [D, 96], F32, kind="ExternalInput").ap()
    dtw_t = nc.dram_tensor("dtw_t", [64, 512], F32, kind="ExternalInput").ap()
    dtb_t = nc.dram_tensor("dtb_t", [128, 4], F32, kind="ExternalInput").ap()
    alog_t = nc.dram_tensor("alog_t", [128, 64], F32, kind="ExternalInput").ap()
    dp_t = nc.dram_tensor("dp_t", [128, 4], F32, kind="ExternalInput").ap()
    outp_t = nc.dram_tensor("outp_t", [D, D], F32, kind="ExternalInput").ap()
    opart = nc.dram_tensor("opart", [D, S], F32, kind="ExternalOutput").ap()

    import contextlib
    stack = contextlib.ExitStack()
    sb = lambda name, shape, dt=F32: stack.enter_context(
        nc.sbuf_tensor(name, shape, dt))
    ps = lambda name, shape, dt=F32: stack.enter_context(
        nc.psum_tensor(name, shape, dt))

    inxb = [sb(f"inxb_{j}", [128, D], BF16) for j in range(NT)]
    inzb = [sb(f"inzb_{j}", [128, 512], BF16) for j in range(NT)]
    xpjb = [sb(f"xpjb_{j}", [128, 96], BF16) for j in range(NT)]
    dtwb = sb("dtwb", [64, 512], BF16)
    outpb = [sb(f"outpb_{j}", [128, D], BF16) for j in range(NT)]
    cxwb = sb("cxwb", [128, 32])
    cxbb = sb("cxbb", [128, 8])
    czwb = sb("czwb", [128, 16])
    czbb = sb("czbb", [128, 4])
    g1b = sb("g1b", [128, NT])
    b1b = sb("b1b", [128, NT])
    dtbb = sb("dtbb", [128, 4])
    alogb = sb("alogb", [128, 64])
    aneg = sb("aneg", [128, 64])
    dpb = sb("dpb", [128, 4])
    xc = [sb(f"xc_{j}", [128, TCC]) for j in range(NT)]
    xn2c = [sb(f"xn2c_{j}", [128, TCC], BF16) for j in range(NT)]
    xsr = [[sb(f"xsr_{j}_{s2}", [128, TCC + 3], BF16) for s2 in range(2)]
           for j in range(NT)]
    zsr = [[sb(f"zsr_{j}_{s2}", [128, TCC + 3], BF16) for s2 in range(2)]
           for j in range(4)]
    uc = [sb(f"uc_{j}", [128, TCC], BF16) for j in range(NT)]
    zcc = [sb(f"zcc_{j}", [128, TCC], BF16) for j in range(4)]
    xdb = sb("xdb", [80, TCC], BF16)
    bcf = sb("bcf", [1, 16 * TCC], BF16)
    ccf = sb("ccf", [1, 16 * TCC], BF16)
    ctmp = sb("ctmp", [16, TCC], BF16)
    delc = [sb(f"delc_{j}", [128, TCC], BF16) for j in range(4)]
    wch = sb("wch", [128, TCC], BF16)
    convt = [sb(f"convt_{j}", [128, TCC]) for j in range(2)]
    spl = sb("spl", [128, TCC])
    spm = sb("spm", [128, TCC])
    dAc = sb("dAc", [128, NST])
    argc = sb("argc", [128, NST], BF16)
    dBuc = sb("dBuc", [128, NST], BF16)
    hc = sb("hc", [128, NST], BF16)
    brep = sb("brep", [128, NST], BF16)
    crep = sb("crep", [128, NST], BF16)
    prodc = sb("prodc", [128, NST], BF16)
    hprev = [sb(f"hprev_{j}", [128, 16]) for j in range(4)]
    t16 = sb("t16", [128, 16])
    yc = [sb(f"yc_{j}", [128, TCC], BF16) for j in range(4)]
    ev = [sb(f"ev_{j}", [128, D]) for j in range(2)]
    otile = [sb(f"otile_{j}", [128, TCC]) for j in range(2)]
    zpad2 = sb("zpad2", [128, 2], BF16)
    ones_col = sb("ones_col", [128, 1])
    ones_row = sb("ones_row", [1, 128])
    ones_row_bf = sb("ones_row_bf", [1, 128], BF16)
    eps1 = sb("eps1", [1, 1])
    sq2 = [sb(f"sq2_{j}", [128, TCC]) for j in range(2)]
    srow = sb("srow", [1, TCC])
    qrow = sb("qrow", [1, TCC])
    r1row = sb("r1row", [1, TCC])
    vrow = sb("vrow", [1, TCC])

    pb = [ps(f"pb{j}", [128, 512]).ap() for j in range(8)]

    sc = dict(ones_col=ones_col, ones_row=ones_row, sq2=sq2, srow=srow,
              qrow=qrow, eps1=eps1)

    with stack:
        P.memset("vector", ones_col[:], 1.0)
        P.memset("vector", ones_row[:], 1.0)
        P.memset("vector", ones_row_bf[:], 1.0)
        P.memset("vector", eps1[:], LN_EPS)
        P.memset("vector", zpad2[:], 0.0)
        for jj in range(4):
            P.memset("vector", hprev[jj][:], 0.0)
            P.memset("vector", xsr[jj][0][:, 0:1], 0.0)
            P.memset("vector", zsr[jj][0][:, 0:1], 0.0)
        for jj in range(4, NT):
            P.memset("vector", xsr[jj][0][:, 0:1], 0.0)
        for j in range(NT):
            rows = slice(j * 128, (j + 1) * 128)
            s = ev[j % 2]
            P.dma("gpsimd", s[:], inx_t[rows, :])
            P.copy("gpsimd", inxb[j][:], s[:])
            P.dma("gpsimd", s[:, 0:512], inz_t[rows, :])
            P.copy("gpsimd", inzb[j][:], s[:, 0:512])
            P.dma("gpsimd", s[:, 0:96], xpj_t[rows, :])
            P.copy("gpsimd", xpjb[j][:], s[:, 0:96])
            P.dma("gpsimd", s[:], outp_t[rows, :])
            P.copy("gpsimd", outpb[j][:], s[:])
        P.dma("sync", ev[0][0:64, 0:512], dtw_t)
        P.copy("vector", dtwb[:], ev[0][0:64, 0:512])
        P.dma("sync", cxwb[:], cxw)
        P.dma("sync", cxbb[:], cxb)
        P.dma("sync", czwb[:], czw)
        P.dma("sync", czbb[:], czb)
        P.dma("sync", g1b[:], ln1g)
        P.dma("sync", b1b[:], ln1b)
        P.dma("sync", dtbb[:], dtb_t)
        P.dma("sync", alogb[:], alog_t)
        P.dma("sync", dpb[:], dp_t)
        P.act(aneg[:], alogb[:], AF.Exp, tag="aexp")
        P.ts("vector", aneg[:], aneg[:], -1.0, ALU.mult, tag="aneg")

        def stage_in(c):
            sl = slice(c * TCC, (c + 1) * TCC)
            for j in range(NT):
                P.dma("sync", xc[j][:], x2f[j * 128:(j + 1) * 128, sl],
                      tag="ld_x2")
            n = NT
            P.mm(pb[0][0:1, 0:TCC], [(ones_col[:], xc[j][:]) for j in range(n)],
                 tag="ln_s1")
            for j in range(n):
                P.act(sq2[j % 2][:], xc[j][:], AF.Square, tag="ln_sq")
                P.mm(pb[1][0:1, 0:TCC], [(ones_col[:], sq2[j % 2][:])],
                     start=(j == 0), stop=(j == n - 1), tag="ln_s2")
            P.copy("vector", srow[:], pb[0][0:1, 0:TCC])
            P.copy("vector", qrow[:], pb[1][0:1, 0:TCC])
            P.ts("vector", srow[:], srow[:], 1.0 / D, ALU.mult)
            P.ts("vector", qrow[:], qrow[:], 1.0 / D, ALU.mult)
            P.tt("vector", vrow[:], srow[:], srow[:], ALU.mult)
            P.tt("vector", qrow[:], qrow[:], vrow[:], ALU.subtract)
            P.act(r1row[:], qrow[:], AF.Ln, bias=eps1[:], tag="ln_l1")
            P.act(r1row[:], r1row[:], AF.Exp, scale=-0.5, tag="ln_r1")
            P.tt("vector", vrow[:], r1row[:], r1row[:], ALU.mult)
            P.tt("vector", vrow[:], qrow[:], vrow[:], ALU.mult)
            P.act(vrow[:], vrow[:], AF.Ln, bias=eps1[:], tag="ln_l2")
            P.act(vrow[:], vrow[:], AF.Exp, scale=-0.5, tag="ln_r2")
            P.tt("vector", r1row[:], r1row[:], vrow[:], ALU.mult)
            P.mm(pb[0][:, 0:TCC], [(ones_row[:], srow[:])], tag="ln_bm")
            P.mm(pb[1][:, 0:TCC], [(ones_row[:], r1row[:])], tag="ln_br")
            for j in range(NT):
                P.tt("vector", sq2[j % 2][:], xc[j][:], pb[0][:, 0:TCC],
                     ALU.subtract, tag="ln_sub")
                P.tt("vector", sq2[j % 2][:], sq2[j % 2][:], pb[1][:, 0:TCC],
                     ALU.mult, tag="ln_mul")
                P.stt("vector", xn2c[j][:], sq2[j % 2][:], g1b[:, j:j + 1],
                      b1b[:, j:j + 1].broadcast_to((128, TCC)), ALU.mult,
                      ALU.add, tag="ln_aff")
            par = c % 2
            for j2 in range(NT):
                cols = slice(j2 * 128, (j2 + 1) * 128)
                pm = pb[2 + j2 % 2][:, 0:TCC]
                P.mm(pm, [(inxb[j][:, cols], xn2c[j][:]) for j in range(NT)],
                     tag="mm_inx")
                P.copy("scalar", xsr[j2][par][:, 1:1 + TCC], pm, tag="xs_ev")
            for j2 in range(4):
                cols = slice(j2 * 128, (j2 + 1) * 128)
                pm = pb[2 + j2 % 2][:, 0:TCC]
                P.mm(pm, [(inzb[j][:, cols], xn2c[j][:]) for j in range(NT)],
                     tag="mm_inz")
                P.copy("scalar", zsr[j2][par][:, 1:1 + TCC], pm, tag="zs_ev")
            if c >= 1:
                prv = (c - 1) % 2
                for j2 in range(NT):
                    P.copy("gpsimd", xsr[j2][par][:, 0:1],
                           xsr[j2][prv][:, TCC:TCC + 1], tag="halo_l")
                    P.copy("gpsimd", xsr[j2][prv][:, 1 + TCC:3 + TCC],
                           xsr[j2][par][:, 1:3], tag="halo_r")
                for j2 in range(4):
                    P.copy("gpsimd", zsr[j2][par][:, 0:1],
                           zsr[j2][prv][:, TCC:TCC + 1], tag="halo_lz")
                    P.copy("gpsimd", zsr[j2][prv][:, 1 + TCC:3 + TCC],
                           zsr[j2][par][:, 1:3], tag="halo_rz")

        def conv_silu(ring, wcol, bcol, dst, tag):
            P.ts("vector", convt[0][:], ring[:, 0:TCC], wcol[0], ALU.mult,
                 tag=tag + "0")
            P.stt("vector", convt[0][:], ring[:, 1:1 + TCC], wcol[1],
                  convt[0][:], ALU.mult, ALU.add, tag=tag + "1")
            P.stt("vector", convt[0][:], ring[:, 2:2 + TCC], wcol[2],
                  convt[0][:], ALU.mult, ALU.add, tag=tag + "2")
            P.stt("vector", convt[0][:], ring[:, 3:3 + TCC], wcol[3],
                  convt[0][:], ALU.mult, ALU.add, tag=tag + "3")
            P.act(dst, convt[0][:], AF.Silu, bias=bcol, tag=tag + "s")

        def stage_scan(cc):
            par = cc % 2
            sl = slice(cc * TCC, (cc + 1) * TCC)
            for j2 in range(NT):
                wc = [cxwb[:, 4 * j2 + k:4 * j2 + k + 1] for k in range(4)]
                conv_silu(xsr[j2][par], wc, cxbb[:, j2:j2 + 1], uc[j2][:],
                          f"cvx{j2}_")
            for j2 in range(4):
                wc = [czwb[:, 4 * j2 + k:4 * j2 + k + 1] for k in range(4)]
                conv_silu(zsr[j2][par], wc, czbb[:, j2:j2 + 1], zcc[j2][:],
                          f"cvz{j2}_")
            P.mm(pb[4][0:80, 0:TCC], [(xpjb[j][:, 0:80], uc[j][:])
                                      for j in range(NT)], tag="mm_xd")
            P.copy("scalar", xdb[:], pb[4][0:80, 0:TCC], tag="xd_ev")
            P.mm(pb[5][0:16, 0:TCC], [(xpjb[j][:, 80:96], uc[j][:])
                                      for j in range(NT)], tag="mm_xc")
            P.dma("scalar", bcf[:].rearrange("o (n t) -> o n t", n=16),
                  xdb[64:80, :], tag="b_flat")
            P.copy("scalar", ctmp[:], pb[5][0:16, 0:TCC], tag="c_ev")
            P.dma("scalar", ccf[:].rearrange("o (n t) -> o n t", n=16),
                  ctmp[:], tag="c_flat")
            for n in range(16):
                pm = pb[6 if n % 2 == 0 else 7][:, 0:TCC]
                P.mm(pm, [(ones_row_bf[:], bcf[:, n * TCC:(n + 1) * TCC])],
                     tag="b_bc")
                P.act(brep[:, n * TCC:(n + 1) * TCC], pm, AF.Copy,
                      tag="b_bev")
                pm2 = pb[7 if n % 2 == 0 else 6][:, 0:TCC]
                P.mm(pm2, [(ones_row_bf[:], ccf[:, n * TCC:(n + 1) * TCC])],
                     tag="c_bc")
                P.act(crep[:, n * TCC:(n + 1) * TCC], pm2, AF.Copy,
                      tag="c_bev")
            for jj in range(4):
                dsl = slice(jj * 128, (jj + 1) * 128)
                pm = pb[4][:, 0:TCC]
                P.mm(pm, [(dtwb[:, dsl], xdb[0:64, :])], tag="mm_dt")
                P.copy("vector", spm[:], pm, tag="dt_ev")
                P.act(spl[:], spm[:], AF.Abs, bias=dtbb[:, jj:jj + 1],
                      tag="sp_a")
                P.act(spl[:], spl[:], AF.Exp, scale=-1.0, tag="sp_e")
                P.act(spl[:], spl[:], AF.Ln, bias=1.0, tag="sp_l")
                P.ts("vector", spm[:], spm[:], dtbb[:, jj:jj + 1], ALU.add,
                     0.0, ALU.max, tag="sp_m")
                P.tt("vector", delc[jj][:], spm[:], spl[:], ALU.add,
                     tag="sp_s")
                for n in range(16):
                    P.ts("gpsimd", argc[:, n * TCC:(n + 1) * TCC],
                         delc[jj][:], aneg[:, 16 * jj + n:16 * jj + n + 1],
                         ALU.mult, tag="arg")
                P.act(dAc[:], argc[:], AF.Exp, tag="dA")
                P.tt("gpsimd", wch[:], delc[jj][:], uc[jj][:],
                     ALU.mult, tag="w")
                wv_view = wch[:].rearrange("p (o t) -> p o t", o=1)\
                    .broadcast_to((128, 16, TCC))
                P.tt("vector", dBuc[:].rearrange("p (n t) -> p n t", n=16),
                     wv_view, brep[:].rearrange("p (n t) -> p n t", n=16),
                     ALU.mult, tag="dBu")
                dacol = dAc[:, 0:NST:TCC]
                dbcol = dBuc[:, 0:NST:TCC]
                P.tt("vector", t16[:], dacol, hprev[jj][:], ALU.mult,
                     tag="fix_m")
                P.tt("vector", dbcol, dbcol, t16[:], ALU.add, tag="fix_a")
                P.memset("vector", dacol, 0.0, tag="fix_z")
                P.scan("vector", hc[:], dAc[:], dBuc[:], 0.0, ALU.mult,
                       ALU.add, tag="scan")
                hlast = hc[:, TCC - 1:NST:TCC]
                P.copy("vector", hprev[jj][:], hlast, tag="h_keep")
                P.tt("gpsimd", prodc[:], hc[:], crep[:], ALU.mult, tag="prod")
                half = 8 * TCC
                P.tt("vector", prodc[:, 0:half], prodc[:, 0:half],
                     prodc[:, half:2 * half], ALU.add, tag="tr1")
                P.tt("vector", prodc[:, 0:half // 2], prodc[:, 0:half // 2],
                     prodc[:, half // 2:half], ALU.add, tag="tr2")
                P.tt("vector", prodc[:, 0:half // 4], prodc[:, 0:half // 4],
                     prodc[:, half // 4:half // 2], ALU.add, tag="tr3")
                P.tt("vector", prodc[:, 0:TCC], prodc[:, 0:TCC],
                     prodc[:, TCC:2 * TCC], ALU.add, tag="tr4")
                P.stt("vector", yc[jj][:], uc[jj][:],
                      dpb[:, jj:jj + 1], prodc[:, 0:TCC], ALU.mult, ALU.add,
                      tag="y")
            for od in range(NT):
                cols = slice(od * 128, (od + 1) * 128)
                pm = pb[2 + od % 2][:, 0:TCC]
                P.mm(pm, [(outpb[jj][:, cols], yc[jj][:]) for jj in range(4)] +
                     [(outpb[4 + jj][:, cols], zcc[jj][:]) for jj in range(4)],
                     tag="mm_op")
                ot = otile[od % 2]
                P.copy("vector", ot[:], pm, tag="op_ev")
                P.dma("gpsimd", opart[od * 128:(od + 1) * 128, sl], ot[:],
                      tag="out")

        nrun = NCHC if nch_limit is None else nch_limit
        for c in range(nrun):
            stage_in(c)
            if c >= 1:
                stage_scan(c - 1)
        lastpar = (nrun - 1) % 2
        for j2 in range(NT):
            P.copy("gpsimd", xsr[j2][lastpar][:, 1 + TCC:3 + TCC], zpad2[:],
                   tag="pad_x")
        for j2 in range(4):
            P.copy("gpsimd", zsr[j2][lastpar][:, 1 + TCC:3 + TCC], zpad2[:],
                   tag="pad_z")
        stage_scan(nrun - 1)

        ctx = P.emit()
        with ctx:
            pass
    return nc


def l3_in_maps(x2_fm, inputs):
    def colsN(v, n):
        return np.ascontiguousarray(v.reshape(n, 128).T)
    inT = inputs["L1_attn_in_proj_w"].T
    xpjT = inputs["L1_attn_x_proj_w"].T
    outT = inputs["L1_attn_out_proj_w"].T
    cxw_all = inputs["L1_attn_conv_x_w"][:, 0, :]
    cxb_all = inputs["L1_attn_conv_x_b"]
    maps = []
    for c in range(8):
        b, i = c // 2, c % 2
        mysl = np.r_[512 * i:512 * i + 512, 512 * (1 - i):512 * (1 - i) + 512]
        hsl = slice(512 * i, 512 * i + 512)
        inxT = np.ascontiguousarray(inT[:, 0:1024][:, mysl])
        inzT = np.ascontiguousarray(inT[:, 1024 + 512 * i:1536 + 512 * i])
        cxw = np.ascontiguousarray(
            cxw_all[mysl].reshape(8, 128, 4).transpose(1, 0, 2)
            .reshape(128, 32))
        cxb = colsN(cxb_all[mysl], 8)
        czw = np.ascontiguousarray(
            inputs["L1_attn_conv_z_w"][:, 0, :][hsl].reshape(4, 128, 4)
            .transpose(1, 0, 2).reshape(128, 16))
        czb = colsN(inputs["L1_attn_conv_z_b"][hsl], 4)
        dtwT = np.ascontiguousarray(inputs["L1_attn_dt_proj_w"].T[:, hsl])
        dtb = colsN(inputs["L1_attn_dt_proj_b"][hsl], 4)
        alog = np.ascontiguousarray(
            inputs["L1_attn_A_log"][hsl].reshape(4, 128, 16)
            .transpose(1, 0, 2).reshape(128, 64))
        dp = colsN(inputs["L1_attn_Dp"][hsl], 4)
        outp = np.ascontiguousarray(
            np.concatenate([outT[hsl, :],
                            outT[1024 + 512 * i:1536 + 512 * i, :]], axis=0))
        maps.append({
            "x2f": x2_fm[b],
            "ln1g": colsN(inputs["L1_attn_ln1_g"], 8),
            "ln1b": colsN(inputs["L1_attn_ln1_b"], 8),
            "inx_t": inxT, "inz_t": inzT,
            "cxw": cxw, "cxb": cxb, "czw": czw, "czb": czb,
            "xpj_t": np.ascontiguousarray(xpjT[mysl, :]),
            "dtw_t": dtwT, "dtb_t": dtb, "alog_t": alog, "dp_t": dp,
            "outp_t": outp,
        })
    return maps


def build_lm():
    nc = bass.Bass("TRN2", target_bir_lowering=False, debug=False)
    P = Prog(nc)

    wfr = nc.dram_tensor("wfr", [D, D], F32, kind="ExternalInput").ap()
    wfi = nc.dram_tensor("wfi", [D, D], F32, kind="ExternalInput").ap()
    cmat = nc.dram_tensor("cmat", [D, D], F32, kind="ExternalInput").ap()
    smat = nc.dram_tensor("smat", [D, D], F32, kind="ExternalInput").ap()
    ccol = nc.dram_tensor("ccol", [D, 128], F32, kind="ExternalInput").ap()
    scol = nc.dram_tensor("scol", [D, 128], F32, kind="ExternalInput").ap()
    scoln = nc.dram_tensor("scoln", [D, 128], F32, kind="ExternalInput").ap()
    mshard = nc.dram_tensor("mshard", [D, 128], F32, kind="ExternalOutput").ap()

    import contextlib
    stack = contextlib.ExitStack()
    sb = lambda name, shape, dt=F32: stack.enter_context(
        nc.sbuf_tensor(name, shape, dt))
    ps = lambda name, shape, dt=F32: stack.enter_context(
        nc.psum_tensor(name, shape, dt))

    wfr_b = [sb(f"wfr_{j}", [128, D]) for j in range(NT)]
    wfi_b = [sb(f"wfi_{j}", [128, D]) for j in range(NT)]
    cm_b = [sb(f"cm_{j}", [128, D]) for j in range(NT)]
    sm_b = [sb(f"sm_{j}", [128, D]) for j in range(NT)]
    cc_b = [sb(f"cc_{j}", [128, 128]) for j in range(NT)]
    sc_b = [sb(f"sc_{j}", [128, 128]) for j in range(NT)]
    scn_b = [sb(f"scn_{j}", [128, 128]) for j in range(NT)]
    vr_b = [sb(f"vr_{j}", [128, 128]) for j in range(NT)]
    vi_b = [sb(f"vi_{j}", [128, 128]) for j in range(NT)]
    mt = [sb(f"mt_{j}", [128, 128]) for j in range(2)]

    pbank = [ps(f"pbank{j}", [128, 128]).ap() for j in range(4)]

    with stack:
        for j in range(NT):
            rows = slice(j * 128, (j + 1) * 128)
            P.dma("sync", wfr_b[j][:], wfr[rows, :])
            P.dma("sync", wfi_b[j][:], wfi[rows, :])
            P.dma("sync", cm_b[j][:], cmat[rows, :])
            P.dma("sync", sm_b[j][:], smat[rows, :])
            P.dma("gpsimd", cc_b[j][:], ccol[rows, :])
            P.dma("gpsimd", sc_b[j][:], scol[rows, :])
            P.dma("gpsimd", scn_b[j][:], scoln[rows, :])
        for e in range(NT):
            esl = slice(e * 128, (e + 1) * 128)
            pm = pbank[e % 2]
            P.mm(pm, [(wfr_b[o][:, esl], cc_b[o][:]) for o in range(NT)] +
                 [(wfi_b[o][:, esl], scn_b[o][:]) for o in range(NT)],
                 tag="vr")
            P.copy("vector", vr_b[e][:], pm)
            pm2 = pbank[2 + e % 2]
            P.mm(pm2, [(wfr_b[o][:, esl], sc_b[o][:]) for o in range(NT)] +
                 [(wfi_b[o][:, esl], cc_b[o][:]) for o in range(NT)],
                 tag="vi")
            P.copy("vector", vi_b[e][:], pm2)
        for dtl in range(NT):
            dsl = slice(dtl * 128, (dtl + 1) * 128)
            pm = pbank[dtl % 2]
            P.mm(pm, [(cm_b[e][:, dsl], vr_b[e][:]) for e in range(NT)] +
                 [(sm_b[e][:, dsl], vi_b[e][:]) for e in range(NT)], tag="m")
            P.ts("vector", mt[dtl % 2][:], pm, 1.0 / D, ALU.mult)
            P.dma("gpsimd", mshard[dsl, :], mt[dtl % 2][:])
        ctx = P.emit()
        with ctx:
            pass
    return nc


def lm_in_maps(inputs):
    idx = np.arange(D)
    ang = 2.0 * np.pi * np.outer(idx, idx) / D
    C = np.cos(ang).astype(np.float32)
    Sm = np.sin(ang).astype(np.float32)
    wfr = np.ascontiguousarray(inputs["L1_ffn_w_freq_real"].T)
    wfi = np.ascontiguousarray(inputs["L1_ffn_w_freq_imag"].T)
    maps = []
    for c in range(8):
        cols = slice(128 * c, 128 * (c + 1))
        maps.append({
            "wfr": wfr, "wfi": wfi, "cmat": C, "smat": Sm,
            "ccol": np.ascontiguousarray(C[:, cols]),
            "scol": np.ascontiguousarray(Sm[:, cols]),
            "scoln": np.ascontiguousarray(-Sm[:, cols]),
        })
    return maps


def _col8(v):
    return np.ascontiguousarray(v.reshape(8, 128).T)


def l4_in_maps(x2_fm, oparts, M, inputs):
    w1t = np.ascontiguousarray(
        inputs["L1_attn_mlp_w1"].T.reshape(8, 128, 32, 128)
        .transpose(2, 1, 0, 3).reshape(32, 128, 1024))
    w2t = np.ascontiguousarray(
        inputs["L1_attn_mlp_w2"].T.reshape(32, 128, 8, 128)
        .transpose(2, 1, 0, 3).reshape(8, 128, 4096))
    b1t = np.ascontiguousarray(inputs["L1_attn_mlp_b1"].reshape(32, 128).T)
    common = {
        "w1t": w1t, "w2t": w2t, "b1t": b1t,
        "b2t": _col8(inputs["L1_attn_mlp_b2"]),
        "ln2g": _col8(inputs["L1_attn_ln2_g"]),
        "ln2b": _col8(inputs["L1_attn_ln2_b"]),
        "sbias": _col8(inputs["L1_ffn_bias"]),
        "m_in": np.ascontiguousarray(M.astype(np.float32)),
    }
    maps = []
    for c in range(8):
        b, i = c // 2, c % 2
        t0 = i * TH
        maps.append(dict(common,
                         x2h=np.ascontiguousarray(x2_fm[b][:, t0:t0 + TH]),
                         oa=np.ascontiguousarray(oparts[2 * b][:, t0:t0 + TH]),
                         ob=np.ascontiguousarray(
                             oparts[2 * b + 1][:, t0:t0 + TH])))
    return maps


_NC_CACHE = {}


def _get_nc(name, builder):
    if name not in _NC_CACHE:
        _NC_CACHE[name] = builder()
    return _NC_CACHE[name]


def kernel(**inputs):
    cores = list(range(8))
    x0_fm = [np.ascontiguousarray(inputs["L0_attn_x"][b].T) for b in range(4)]

    res = run_bass_kernel_spmd(_get_nc("lm", build_lm), lm_in_maps(inputs),
                               core_ids=cores)
    M = np.concatenate([res.results[c]["mshard"] for c in range(8)], axis=1)

    res = run_bass_kernel_spmd(_get_nc("l1", build_l1),
                               l1_in_maps(x0_fm, inputs), core_ids=cores)
    partials = [res.results[c]["part"] for c in range(8)]

    res = run_bass_kernel_spmd(_get_nc("l2", build_l2),
                               l2_in_maps(x0_fm, partials, inputs),
                               core_ids=cores)
    x2_fm = [np.ascontiguousarray(
        np.concatenate([res.results[2 * b]["x2o"],
                        res.results[2 * b + 1]["x2o"]], axis=1))
        for b in range(4)]

    res = run_bass_kernel_spmd(_get_nc("l3", lambda: build_l3(0)),
                               l3_in_maps(x2_fm, inputs), core_ids=cores)
    oparts = [res.results[c]["opart"] for c in range(8)]

    res = run_bass_kernel_spmd(_get_nc("l4", build_l4),
                               l4_in_maps(x2_fm, oparts, M, inputs),
                               core_ids=cores)
    out = np.empty((4, S, D), np.float32)
    for b in range(4):
        full = np.concatenate([res.results[2 * b]["outh"],
                               res.results[2 * b + 1]["outh"]], axis=1)
        out[b] = full.T
    return out


# version 02
# speedup vs baseline: 1.3660x; 1.3660x over previous
import math
import numpy as np

import concourse.bass as bass
import concourse.mybir as mybir
from concourse.bass_utils import run_bass_kernel_spmd

F32 = mybir.dt.float32
BF16 = mybir.dt.bfloat16
AF = mybir.ActivationFunctionType
ALU = mybir.AluOpType
AX = mybir.AxisListType


class Prog:

    def __init__(self, nc: bass.Bass):
        self.nc = nc
        self.ops = []
        self.last_writer = {}
        self.readers = {}

    def _record(self, engine, emit, reads, writes, is_dma, tag):
        idx = len(self.ops)
        deps = set()
        for r in reads:
            if r in self.last_writer:
                deps.add(self.last_writer[r])
        for w in writes:
            if w in self.last_writer:
                deps.add(self.last_writer[w])
            for r in self.readers.get(w, ()):
                deps.add(r)
        deps.discard(idx)
        self.ops.append(dict(idx=idx, engine=engine, emit=emit, deps=deps,
                             is_dma=is_dma, needed=is_dma, tag=tag))
        for w in writes:
            self.last_writer[w] = idx
            self.readers[w] = []
        for r in reads:
            self.readers.setdefault(r, []).append(idx)
        return idx

    @staticmethod
    def _res(ap):
        return ap.tensor.name

    def _names(self, items):
        out = []
        for it in items:
            if it is None or isinstance(it, (int, float)):
                continue
            out.append(it if isinstance(it, str) else self._res(it))
        return out

    def dma(self, queue, out_ap, in_ap, tag="", **kw):
        def emit(eng):
            return eng.dma_start(out=out_ap, in_=in_ap, **kw)
        idx = self._record(queue, emit, self._names([in_ap]),
                           self._names([out_ap]), True, tag)
        self.ops[idx]["dest"] = self._names([out_ap])[0]
        return idx

    def tt(self, engine, out, in0, in1, aluop, tag=""):
        def emit(eng):
            return eng.tensor_tensor(out, in0, in1, aluop)
        return self._record(engine, emit, self._names([in0, in1]),
                            self._names([out]), False, tag)

    def ts(self, engine, out, in0, s1, op0, s2=None, op1=None, tag=""):
        def emit(eng):
            if op1 is None:
                return eng.tensor_scalar(out, in0, s1, None, op0)
            return eng.tensor_scalar(out, in0, s1, s2, op0, op1)
        return self._record(engine, emit, self._names([in0, s1, s2]),
                            self._names([out]), False, tag)

    def stt(self, engine, out, in0, scalar, in1, op0, op1, tag=""):
        def emit(eng):
            return eng.scalar_tensor_tensor(out, in0, scalar, in1, op0, op1)
        return self._record(engine, emit, self._names([in0, in1, scalar]),
                            self._names([out]), False, tag)

    def copy(self, engine, out, in_, tag=""):
        def emit(eng):
            if engine == "scalar":
                return eng.copy(out, in_)
            return eng.tensor_copy(out, in_)
        return self._record(engine, emit, self._names([in_]),
                            self._names([out]), False, tag)

    def op(self, engine, emit, reads=(), writes=(), tag=""):
        return self._record(engine, emit, self._names(list(reads)),
                            self._names(list(writes)), False, tag)

    def memset(self, engine, ap, const, tag=""):
        def emit(eng):
            return eng.memset(ap, const)
        return self._record(engine, emit, [], self._names([ap]), False, tag)

    def act(self, out, in_, func, bias=0.0, scale=1.0, tag="", accum_out=None):
        def emit(eng):
            kw = {} if accum_out is None else {"accum_out": accum_out}
            return eng.activation(out, in_, func, bias=bias, scale=scale, **kw)
        writes = [out] + ([accum_out] if accum_out is not None else [])
        return self._record("scalar", emit, self._names([in_, bias, scale]),
                            self._names(writes), False, tag)

    def reduce(self, engine, out, in_, axis, aluop, negate=None, tag=""):
        def emit(eng):
            kw = {}
            if negate is not None:
                kw["negate"] = negate
            return eng.tensor_reduce(out, in_, axis, aluop, **kw)
        return self._record(engine, emit, self._names([in_]),
                            self._names([out]), False, tag)

    def scan(self, engine, out, data0, data1, initial, op0, op1, tag=""):
        def emit(eng):
            return eng.tensor_tensor_scan(out, data0, data1, initial, op0, op1)
        return self._record(engine, emit, self._names([data0, data1, initial]),
                            self._names([out]), False, tag)

    def mm(self, out, pairs, start=True, stop=True, tag=""):
        n = len(pairs)

        def emit(eng):
            last = None
            for j, (lhsT, rhs) in enumerate(pairs):
                last = eng.matmul(out, lhsT, rhs,
                                  start=(start and j == 0),
                                  stop=(stop and j == n - 1))
            return last
        reads = []
        for lhsT, rhs in pairs:
            reads += [lhsT, rhs]
        return self._record("tensor", emit, self._names(reads),
                            self._names([out]), False, tag)

    def transpose(self, out, in_, identity, tag=""):
        def emit(eng):
            return eng.transpose(out, in_, identity)
        return self._record("tensor", emit, self._names([in_, identity]),
                            self._names([out]), False, tag)

    def emit(self):
        nc = self.nc
        for o in self.ops:
            for d in o["deps"]:
                self.ops[d]["needed"] = True
        streams = {}
        for o in self.ops:
            streams.setdefault(o["engine"], []).append(o)
        inc_rank = {}
        dma_keys = []
        dma_count = {}
        for eng_name, ops in streams.items():
            r = 0
            for o in ops:
                if not o["needed"]:
                    continue
                if o["is_dma"]:
                    key = "D_" + eng_name + "_" + o["dest"]
                    if key not in dma_count:
                        dma_count[key] = 0
                        dma_keys.append(key)
                    dma_count[key] += 16
                    o["semkey"] = key
                    inc_rank[o["idx"]] = dma_count[key]
                else:
                    r += 1
                    o["semkey"] = eng_name
                    inc_rank[o["idx"]] = r
        import contextlib
        stack = contextlib.ExitStack()
        sems = {e: stack.enter_context(nc.semaphore("S_" + e)) for e in streams}
        for key in dma_keys:
            sems[key] = stack.enter_context(nc.semaphore(key))
        block = stack.enter_context(nc.Block())
        totals = {}
        for o in self.ops:
            if o["needed"]:
                k = o["semkey"]
                totals[k] = max(totals.get(k, 0), inc_rank[o["idx"]])
        barrier_eng = self._barrier_engine()

        def make_section(eng_name):
            ops = streams[eng_name]

            def section(eng):
                waited = {}
                for o in ops:
                    bykey = {}
                    for d in o["deps"]:
                        dd = self.ops[d]
                        if dd["engine"] == "tensor" and eng_name == "tensor" \
                                and not dd["is_dma"]:
                            continue
                        k = dd["semkey"]
                        v = inc_rank[dd["idx"]]
                        if bykey.get(k, 0) < v:
                            bykey[k] = v
                    for k, v in sorted(bykey.items()):
                        if waited.get(k, 0) < v:
                            eng.wait_ge(sems[k], v)
                            waited[k] = v
                    inst = o["emit"](eng)
                    if o["needed"]:
                        inst.then_inc(sems[o["semkey"]],
                                      16 if o["is_dma"] else 1)
                if eng_name == barrier_eng:
                    for k, v in sorted(totals.items()):
                        if v > waited.get(k, 0) or k == eng_name:
                            if v > 0:
                                eng.wait_ge(sems[k], v)
            return section

        for eng_name in streams:
            getattr(block, eng_name)(make_section(eng_name))
        return stack

    def _barrier_engine(self):
        engs = {o["engine"] for o in self.ops}
        for e in ("gpsimd", "sync", "vector", "scalar", "tensor"):
            if e in engs:
                return e
        raise RuntimeError("empty program")


D = 1024
S = 4096
BATCH = 4
TH = 2048
NT = D // 128
LN_EPS = 1e-5


def emit_ln_chunk(P, srcs, sc, ps_mb, ps_rb, ps_a, ps_b, veng="vector",
                  d_total=D):
    n = len(srcs)
    inv = 1.0 / d_total
    P.mm(ps_a, [(sc["ones_col"][:], s) for s in srcs], tag="ln_s1")
    for j, s in enumerate(srcs):
        P.act(sc["sq2"][j % 2][:], s, AF.Square, tag="ln_sq")
        P.mm(ps_b, [(sc["ones_col"][:], sc["sq2"][j % 2][:])],
             start=(j == 0), stop=(j == n - 1), tag="ln_s2")
    P.copy(veng, sc["srow"][:], ps_a, tag="ln_ev1")
    P.copy(veng, sc["qrow"][:], ps_b, tag="ln_ev2")
    P.ts(veng, sc["srow"][:], sc["srow"][:], inv, ALU.mult, tag="ln_m")
    P.ts(veng, sc["qrow"][:], sc["qrow"][:], inv, ALU.mult, tag="ln_q")
    P.tt(veng, sc["sq2"][0][0:1, :], sc["srow"][:], sc["srow"][:], ALU.mult,
         tag="ln_m2")
    P.tt(veng, sc["qrow"][:], sc["qrow"][:], sc["sq2"][0][0:1, :],
         ALU.subtract, tag="ln_v")
    P.act(sc["qrow"][:], sc["qrow"][:], AF.Ln, bias=sc["eps1"][:], tag="ln_ln")
    P.act(sc["qrow"][:], sc["qrow"][:], AF.Exp, scale=-0.5, tag="ln_rstd")
    P.mm(ps_mb, [(sc["ones_row"][:], sc["srow"][:])], tag="ln_bm")
    P.mm(ps_rb, [(sc["ones_row"][:], sc["qrow"][:])], tag="ln_br")


def build_l4():
    nc = bass.Bass("TRN2", target_bir_lowering=False, debug=False)
    P = Prog(nc)
    M_TILES = 32
    TC = 512
    NCH = TH // TC

    x2h = nc.dram_tensor("x2h", [D, TH], F32, kind="ExternalInput").ap()
    oa = nc.dram_tensor("oa", [D, TH], F32, kind="ExternalInput").ap()
    ob = nc.dram_tensor("ob", [D, TH], F32, kind="ExternalInput").ap()
    w1t = nc.dram_tensor("w1t", [M_TILES, 128, D], F32,
                         kind="ExternalInput").ap()
    w2t = nc.dram_tensor("w2t", [NT, 128, 4096], F32,
                         kind="ExternalInput").ap()
    b1t = nc.dram_tensor("b1t", [128, M_TILES], F32, kind="ExternalInput").ap()
    b2t = nc.dram_tensor("b2t", [128, NT], F32, kind="ExternalInput").ap()
    ln2g = nc.dram_tensor("ln2g", [128, NT], F32, kind="ExternalInput").ap()
    ln2b = nc.dram_tensor("ln2b", [128, NT], F32, kind="ExternalInput").ap()
    sbias = nc.dram_tensor("sbias", [128, NT], F32, kind="ExternalInput").ap()
    m_in = nc.dram_tensor("m_in", [D, D], F32, kind="ExternalInput").ap()
    outh = nc.dram_tensor("outh", [D, TH], F32, kind="ExternalOutput").ap()
    w1s = nc.dram_tensor("w1s", [M_TILES, 128, D], BF16).ap()
    w2s = nc.dram_tensor("w2s", [NT, 128, 4096], BF16).ap()

    import contextlib
    stack = contextlib.ExitStack()
    sb = lambda name, shape, dt=F32: stack.enter_context(
        nc.sbuf_tensor(name, shape, dt))
    ps = lambda name, shape, dt=F32: stack.enter_context(
        nc.psum_tensor(name, shape, dt))

    xc = [sb(f"xc_{j}", [128, TC]) for j in range(NT)]
    ac = [sb(f"ac_{j}", [128, TC]) for j in range(NT)]
    ln2a = [sb(f"ln2a_{j}", [128, TC], BF16) for j in range(NT)]
    xn3 = [sb(f"xn3_{j}", [128, TC], BF16) for j in range(NT)]
    h1 = [sb(f"h1_{j}", [128, TC], BF16) for j in range(M_TILES)]
    Mbf = [sb(f"Mbf_{j}", [128, D], BF16) for j in range(NT)]
    wstg = [sb(f"wstg_{j}", [128, D], BF16) for j in range(2)]
    w2stg = [sb(f"w2stg_{j}", [128, 4096], BF16) for j in range(2)]
    cstg = [sb(f"cstg_{j}", [128, D]) for j in range(2)]
    pstg = [sb(f"pstg_{j}", [128, TC]) for j in range(4)]
    otile = [sb(f"otile_{j}", [128, TC]) for j in range(2)]
    tmpc = [sb(f"tmpc_{j}", [128, TC]) for j in range(2)]
    b1c = sb("b1c", [128, M_TILES])
    b2c = sb("b2c", [128, NT])
    gln = sb("gln", [128, NT])
    bln = sb("bln", [128, NT])
    sbc = sb("sbc", [128, NT])
    ones_col = sb("ones_col", [128, 1])
    ones_row = sb("ones_row", [1, 128])
    eps1 = sb("eps1", [1, 1])
    sq2 = [sb(f"sq2_{j}", [128, TC]) for j in range(2)]
    srow = sb("srow", [1, TC])
    qrow = sb("qrow", [1, TC])

    pb = [ps(f"pb{j}", [128, TC]).ap() for j in range(8)]

    sc = dict(ones_col=ones_col, ones_row=ones_row, sq2=sq2, srow=srow,
              qrow=qrow, eps1=eps1)

    with stack:
        P.memset("vector", ones_col[:], 1.0)
        P.memset("vector", ones_row[:], 1.0)
        P.memset("vector", eps1[:], LN_EPS)
        P.dma("sync", b1c[:], b1t)
        P.dma("sync", b2c[:], b2t)
        P.dma("sync", gln[:], ln2g)
        P.dma("sync", bln[:], ln2b)
        P.dma("sync", sbc[:], sbias)
        for j in range(NT):
            s = cstg[j % 2]
            P.dma("sync", s[:], m_in[j * 128:(j + 1) * 128, :])
            P.copy("gpsimd", Mbf[j][:], s[:])
        for mt in range(M_TILES):
            s = cstg[mt % 2]
            w = wstg[mt % 2]
            P.dma("gpsimd", s[:], w1t[mt])
            P.copy("gpsimd", w[:], s[:])
            P.dma("gpsimd", w1s[mt], w[:])
        for ot in range(NT):
            for h in range(4):
                s = cstg[h % 2]
                w = wstg[h % 2]
                csl = slice(h * D, (h + 1) * D)
                P.dma("gpsimd", s[:], w2t[ot][:, csl])
                P.copy("gpsimd", w[:], s[:])
                P.dma("gpsimd", w2s[ot][:, csl], w[:])

        for c in range(NCH):
            sl = slice(c * TC, (c + 1) * TC)
            for j in range(NT):
                P.dma("sync", xc[j][:], x2h[j * 128:(j + 1) * 128, sl],
                      tag="ld_x2")
            emit_ln_chunk(P, [xc[j][:] for j in range(NT)], sc,
                          pb[0], pb[1], pb[2][0:1, :], pb[3][0:1, :])
            for j in range(NT):
                P.tt("vector", tmpc[j % 2][:], xc[j][:], pb[0], ALU.subtract,
                     tag="a_sub")
                P.tt("vector", ac[j][:], tmpc[j % 2][:], pb[1], ALU.mult,
                     tag="a_mul")
            k = 0
            for j in range(NT):
                s0 = pstg[k % 4]; s1 = pstg[(k + 1) % 4]; k += 2
                P.dma("sync", s0[:], oa[j * 128:(j + 1) * 128, sl])
                P.dma("sync", s1[:], ob[j * 128:(j + 1) * 128, sl])
                P.tt("vector", ac[j][:], ac[j][:], s0[:], ALU.add)
                P.tt("vector", ac[j][:], ac[j][:], s1[:], ALU.add)
            emit_ln_chunk(P, [ac[j][:] for j in range(NT)], sc,
                          pb[4], pb[5], pb[6][0:1, :], pb[7][0:1, :])
            for j in range(NT):
                P.tt("vector", tmpc[j % 2][:], ac[j][:], pb[4], ALU.subtract,
                     tag="l2_sub")
                P.tt("vector", tmpc[j % 2][:], tmpc[j % 2][:], pb[5], ALU.mult,
                     tag="l2_mul")
                P.stt("vector", ln2a[j][:], tmpc[j % 2][:], gln[:, j:j + 1],
                      bln[:, j:j + 1].broadcast_to((128, TC)), ALU.mult,
                      ALU.add, tag="l2_aff")
                P.tt("gpsimd", xc[j][:], xc[j][:], ac[j][:], ALU.add,
                     tag="z_add")
            for mt in range(M_TILES):
                w = wstg[mt % 2]
                P.dma("gpsimd", w[:], w1s[mt], tag="w1_ld")
                pm = pb[mt % 2]
                P.mm(pm, [(w[:, j * 128:(j + 1) * 128], ln2a[j][:])
                          for j in range(NT)], tag="mm_h1")
                P.act(h1[mt][:], pm, AF.Gelu, bias=b1c[:, mt:mt + 1],
                      tag="gelu")
            for j in range(NT):
                w = w2stg[j % 2]
                P.dma("gpsimd", w[:], w2s[j], tag="w2_ld")
                pm = pb[2 + (j % 2)]
                P.mm(pm, [(w[:, mt * 128:(mt + 1) * 128], h1[mt][:])
                          for mt in range(M_TILES)], tag="mm_h2")
                P.stt("vector", xc[j][:], pm, b2c[:, j:j + 1], xc[j][:],
                      ALU.add, ALU.add, tag="x3")
            emit_ln_chunk(P, [xc[j][:] for j in range(NT)], sc,
                          pb[4], pb[5], pb[6][0:1, :], pb[7][0:1, :])
            for j in range(NT):
                P.tt("vector", tmpc[j % 2][:], xc[j][:], pb[4], ALU.subtract,
                     tag="n3_sub")
                P.tt("vector", xn3[j][:], tmpc[j % 2][:], pb[5], ALU.mult,
                     tag="n3_mul")
            for j in range(NT):
                pm = pb[6 if j % 2 == 0 else 7]
                P.mm(pm, [(Mbf[jj][:, j * 128:(j + 1) * 128], xn3[jj][:])
                          for jj in range(NT)], tag="mm_y")
                ot = otile[j % 2]
                P.stt("vector", ot[:], pm, sbc[:, j:j + 1], xc[j][:],
                      ALU.add, ALU.add, tag="spec_out")
                P.dma("gpsimd", outh[j * 128:(j + 1) * 128, sl], ot[:],
                      tag="out")

        ctx = P.emit()
        with ctx:
            pass
    return nc


def build_l2():
    nc = bass.Bass("TRN2", target_bir_lowering=False, debug=False)
    P = Prog(nc)

    xh = nc.dram_tensor("xh", [D, TH], F32, kind="ExternalInput").ap()
    pa = nc.dram_tensor("pa", [D, TH], F32, kind="ExternalInput").ap()
    pb = nc.dram_tensor("pb", [D, TH], F32, kind="ExternalInput").ap()
    halo = nc.dram_tensor("halo", [D, 3], F32, kind="ExternalInput").ap()
    w_r = nc.dram_tensor("w_r", [D, D], F32, kind="ExternalInput").ap()
    w_k = nc.dram_tensor("w_k", [D, D], F32, kind="ExternalInput").ap()
    w_v = nc.dram_tensor("w_v", [D, D], F32, kind="ExternalInput").ap()
    mu2 = nc.dram_tensor("mu2", [D, 2], F32, kind="ExternalInput").ap()
    x2o = nc.dram_tensor("x2o", [D, TH], F32, kind="ExternalOutput").ap()

    TC = 512
    NCH = TH // TC

    import contextlib
    stack = contextlib.ExitStack()
    sb = lambda name, shape, dt=F32: stack.enter_context(
        nc.sbuf_tensor(name, shape, dt))
    ps = lambda name, shape, dt=F32: stack.enter_context(
        nc.psum_tensor(name, shape, dt))

    x1 = [sb(f"x1_{j}", [128, TH]) for j in range(NT)]
    xn = [sb(f"xn_{j}", [128, 1 + TH], BF16) for j in range(NT)]
    pstg = [sb(f"pstg_{j}", [128, TC]) for j in range(4)]
    tmp_row = None
    halo_t = [sb(f"halo_{j}", [128, 3]) for j in range(NT)]
    hprev = [sb(f"hprev_{j}", [128, 1]) for j in range(NT)]
    hsq = [sb(f"hsq_{j}", [128, 1]) for j in range(NT)]
    mu_t = sb("mu_t", [128, 2 * NT])
    wR = [sb(f"wR_{j}", [128, D], BF16) for j in range(NT)]
    wK = [sb(f"wK_{j}", [128, D], BF16) for j in range(NT)]
    wV = [sb(f"wV_{j}", [128, D], BF16) for j in range(NT)]
    ones_col = sb("ones_col", [128, 1])
    ones_row = sb("ones_row", [1, 128])
    srow_t = sb("srow_t", [1, TH])
    qrow_t = sb("qrow_t", [1, TH])
    hrow = sb("hrow", [1, 2])
    eps1 = sb("eps1", [1, 1])
    hvar = sb("hvar", [1, 1])
    xr = [sb(f"xr_{j}", [128, TC], BF16) for j in range(NT)]
    xk = [sb(f"xk_{j}", [128, TC], BF16) for j in range(NT)]
    r_c = [sb(f"rc_{j}", [128, TC], BF16) for j in range(NT)]
    kact = [sb(f"kact_{j}", [128, TC], BF16) for j in range(NT)]
    dltc = sb("dltc", [128, TC], BF16)
    tmp_tc = sb("tmp_tc", [128, TC])
    tmp_tc2 = sb("tmp_tc2", [128, TC])

    ps_s1 = ps("ps_s1", [1, TC]).ap()
    ps_s2 = ps("ps_s2", [1, TC]).ap()
    ps_mb = ps("ps_mb", [128, TC]).ap()
    ps_rb = ps("ps_rb", [128, TC]).ap()
    ps_mm = [ps(f"ps_mm{j}", [128, TC]).ap() for j in range(3)]
    ps_h = ps_s1[:, 0:2]
    ps_hb = ps_mb[:, 0:2]

    srow = srow_t
    qrow = qrow_t

    with stack:
        P.memset("vector", ones_col[:], 1.0)
        P.memset("vector", ones_row[:], 1.0)
        P.memset("vector", eps1[:], LN_EPS)
        for j in range(NT):
            P.dma("sync", x1[j][:], xh[j * 128:(j + 1) * 128, :])
            P.dma("sync", halo_t[j][:], halo[j * 128:(j + 1) * 128, :])
        P.dma("sync", mu_t[:].rearrange("p (n c) -> p n c", c=2), mu2.rearrange("(n p) c -> p n c", p=128))
        k = 0
        for j in range(NT):
            rowsl = slice(j * 128, (j + 1) * 128)
            for c in range(NCH):
                sl = slice(c * TC, (c + 1) * TC)
                s0 = pstg[k % 4]; s1 = pstg[(k + 1) % 4]; k += 2
                P.dma("sync", s0[:], pa[rowsl, sl])
                P.dma("sync", s1[:], pb[rowsl, sl])
                P.tt("vector", x1[j][:, sl], x1[j][:, sl], s0[:], ALU.add)
                P.tt("vector", x1[j][:, sl], x1[j][:, sl], s1[:], ALU.add)
            P.tt("gpsimd", hprev[j][:], halo_t[j][:, 0:1], halo_t[j][:, 1:2],
                 ALU.add)
            P.tt("gpsimd", hprev[j][:], hprev[j][:], halo_t[j][:, 2:3],
                 ALU.add)
        k2 = 0
        for j in range(NT):
            rowsl = slice(j * 128, (j + 1) * 128)
            for (wsrc, wdst) in ((w_r, wR), (w_k, wK), (w_v, wV)):
                for h in range(2):
                    s = pstg[k2 % 4]; k2 += 1
                    csl = slice(h * TC, (h + 1) * TC)
                    P.dma("gpsimd", s[:], wsrc[rowsl, csl])
                    P.copy("gpsimd", wdst[j][:, csl], s[:])

        inv = 1.0 / D
        for c in range(NCH):
            sl = slice(c * TC, (c + 1) * TC)
            P.mm(ps_s1, [(ones_col[:], x1[j][:, sl]) for j in range(NT)],
                 tag="ln_s1")
            for j in range(NT):
                P.act((tmp_tc if j % 2 == 0 else tmp_tc2)[:], x1[j][:, sl], AF.Square, tag="ln_sq")
                P.mm(ps_s2, [(ones_col[:], (tmp_tc if j % 2 == 0 else tmp_tc2)[:])], start=(j == 0),
                     stop=(j == NT - 1), tag="ln_s2")
            P.copy("vector", srow[:, sl], ps_s1, tag="ln_ev1")
            P.copy("vector", qrow[:, sl], ps_s2, tag="ln_ev2")
        P.ts("vector", srow[:], srow[:], inv, ALU.mult, tag="ln_m")
        P.ts("vector", qrow[:], qrow[:], inv, ALU.mult, tag="ln_q")
        P.tt("vector", pstg[0][0:1, :], srow[:, 0:TC], srow[:, 0:TC], ALU.mult, tag="ln_m2")
        P.tt("vector", qrow[:, 0:TC], qrow[:, 0:TC], pstg[0][0:1, :], ALU.subtract, tag="ln_v")
        P.tt("vector", pstg[1][0:1, :], srow[:, TC:2 * TC], srow[:, TC:2 * TC], ALU.mult, tag="ln_m2")
        P.tt("vector", qrow[:, TC:2 * TC], qrow[:, TC:2 * TC], pstg[1][0:1, :], ALU.subtract, tag="ln_v")
        P.tt("vector", pstg[2][0:1, :], srow[:, 2 * TC:3 * TC], srow[:, 2 * TC:3 * TC], ALU.mult, tag="ln_m2")
        P.tt("vector", qrow[:, 2 * TC:3 * TC], qrow[:, 2 * TC:3 * TC], pstg[2][0:1, :], ALU.subtract, tag="ln_v")
        P.tt("vector", pstg[3][0:1, :], srow[:, 3 * TC:4 * TC], srow[:, 3 * TC:4 * TC], ALU.mult, tag="ln_m2")
        P.tt("vector", qrow[:, 3 * TC:4 * TC], qrow[:, 3 * TC:4 * TC], pstg[3][0:1, :], ALU.subtract, tag="ln_v")
        P.act(qrow[:], qrow[:], AF.Ln, bias=eps1[:], tag="ln_ln")
        P.act(qrow[:], qrow[:], AF.Exp, scale=-0.5, tag="ln_rstd")
        for c in range(NCH):
            sl = slice(c * TC, (c + 1) * TC)
            sl1 = slice(1 + c * TC, 1 + (c + 1) * TC)
            P.mm(ps_mb, [(ones_row[:], srow[:, sl])], tag="ln_bm")
            P.mm(ps_rb, [(ones_row[:], qrow[:, sl])], tag="ln_br")
            for j in range(NT):
                P.tt("vector", tmp_tc[:], x1[j][:, sl], ps_mb, ALU.subtract,
                     tag="ln_sub")
                P.tt("vector", xn[j][:, sl1], tmp_tc[:], ps_rb, ALU.mult,
                     tag="ln_mul")

        P.mm(ps_h[:, 0:1], [(ones_col[:], hprev[j][:]) for j in range(NT)])
        for j in range(NT):
            P.act(hsq[j][:], hprev[j][:], AF.Square)
        P.mm(ps_h[:, 1:2], [(ones_col[:], hsq[j][:]) for j in range(NT)])
        P.copy("vector", hrow[:], ps_h)
        P.ts("gpsimd", hrow[:], hrow[:], 1.0 / D, ALU.mult)
        P.tt("gpsimd", hvar[:], hrow[:, 0:1], hrow[:, 0:1], ALU.mult)
        P.tt("gpsimd", hrow[:, 1:2], hrow[:, 1:2], hvar[:], ALU.subtract)
        P.act(hrow[:, 1:2], hrow[:, 1:2], AF.Ln, bias=eps1[:])
        P.act(hrow[:, 1:2], hrow[:, 1:2], AF.Exp, scale=-0.5)
        P.mm(ps_hb, [(ones_row[:], hrow[:])])
        for j in range(NT):
            P.tt("vector", hsq[j][:], hprev[j][:], ps_hb[:, 0:1], ALU.subtract)
            P.tt("vector", xn[j][:, 0:1], hsq[j][:], ps_hb[:, 1:2], ALU.mult)

        for c in range(NCH):
            sl = slice(c * TC, (c + 1) * TC)
            for j in range(NT):
                cur = xn[j][:, 1 + c * TC: 1 + (c + 1) * TC]
                prev = xn[j][:, c * TC: (c + 1) * TC]
                P.tt("vector", dltc[:], cur, prev, ALU.subtract, tag="sh_d")
                P.stt("vector", xr[j][:], dltc[:], mu_t[:, 2 * j:2 * j + 1],
                      prev, ALU.mult, ALU.add, tag="sh_r")
                P.stt("vector", xk[j][:], dltc[:], mu_t[:, 2 * j + 1:2 * j + 2],
                      prev, ALU.mult, ALU.add, tag="sh_k")
            for j2 in range(NT):
                cols = slice(j2 * 128, (j2 + 1) * 128)
                pm = ps_mm[j2 % 2]
                P.mm(pm, [(wR[j][:, cols], xr[j][:]) for j in range(NT)],
                     tag="mm_r")
                P.act(r_c[j2][:], pm, AF.Sigmoid, tag="sig")
                pm2 = ps_mm[(j2 + 1) % 2]
                P.mm(pm2, [(wK[j][:, cols], xk[j][:]) for j in range(NT)],
                     tag="mm_k")
                P.ts("vector", tmp_tc[:], pm2, 0.0, ALU.max, tag="relu")
                P.act(kact[j2][:], tmp_tc[:], AF.Square, tag="ksq")
            for j2 in range(NT):
                cols = slice(j2 * 128, (j2 + 1) * 128)
                pm = ps_mm[2]
                P.mm(pm, [(wV[j][:, cols], kact[j][:]) for j in range(NT)],
                     tag="mm_v")
                P.tt("vector", tmp_tc2[:], r_c[j2][:], pm, ALU.mult,
                     tag="rmul")
                P.tt("vector", x1[j2][:, sl], x1[j2][:, sl], tmp_tc2[:],
                     ALU.add, tag="resid")
                P.dma("gpsimd", x2o[j2 * 128:(j2 + 1) * 128, sl],
                      x1[j2][:, sl], tag="out")

        ctx = P.emit()
        with ctx:
            pass
    return nc


def _fm(x):
    return np.ascontiguousarray(x.T)


def l2_in_maps(x0_fm, partials, inputs):
    maps = []
    for c in range(8):
        b, i = c // 2, c % 2
        t0 = i * TH
        pA = partials[2 * b]
        pB = partials[2 * b + 1]
        if i == 0:
            halo = np.zeros((D, 3), np.float32)
        else:
            halo = np.stack([x0_fm[b][:, t0 - 1], pA[:, t0 - 1],
                             pB[:, t0 - 1]], axis=1)
        maps.append({
            "xh": np.ascontiguousarray(x0_fm[b][:, t0:t0 + TH]),
            "pa": np.ascontiguousarray(pA[:, t0:t0 + TH]),
            "pb": np.ascontiguousarray(pB[:, t0:t0 + TH]),
            "halo": np.ascontiguousarray(halo),
            "w_r": np.ascontiguousarray(inputs["L0_ffn_W_r"].T),
            "w_k": np.ascontiguousarray(inputs["L0_ffn_W_k"].T),
            "w_v": np.ascontiguousarray(inputs["L0_ffn_W_v"].T),
            "mu2": np.ascontiguousarray(
                np.stack([inputs["L0_ffn_mu_r"], inputs["L0_ffn_mu_k"]],
                         axis=1)),
        })
    return maps


def build_l1():
    nc = bass.Bass("TRN2", target_bir_lowering=False, debug=False)
    P = Prog(nc)
    NCH = S // 512
    NKB = S // 128

    x_full = nc.dram_tensor("x_full", [D, S], F32, kind="ExternalInput").ap()
    wq_t = nc.dram_tensor("wq_t", [D, 512], F32, kind="ExternalInput").ap()
    wk_t = nc.dram_tensor("wk_t", [D, 512], F32, kind="ExternalInput").ap()
    wv_t = nc.dram_tensor("wv_t", [D, 520], F32, kind="ExternalInput").ap()
    wo_t = nc.dram_tensor("wo_t", [512, D], F32, kind="ExternalInput").ap()
    ropec = nc.dram_tensor("ropec", [128, S], BF16, kind="ExternalInput").ap()
    ropes = nc.dram_tensor("ropes", [128, S], BF16, kind="ExternalInput").ap()
    psign = nc.dram_tensor("psign", [128, 128], BF16, kind="ExternalInput").ap()
    masks = nc.dram_tensor("masks", [4, 128, 512], BF16,
                           kind="ExternalInput").ap()
    part = nc.dram_tensor("part", [D, S], F32, kind="ExternalOutput").ap()

    import contextlib
    stack = contextlib.ExitStack()
    sb = lambda name, shape, dt=F32: stack.enter_context(
        nc.sbuf_tensor(name, shape, dt))
    ps = lambda name, shape, dt=F32: stack.enter_context(
        nc.psum_tensor(name, shape, dt))

    xn = [sb(f"xn_{j}", [128, S], BF16) for j in range(NT)]
    krot = [sb(f"krot_{t}", [128, S], BF16) for t in range(4)]
    vtm = [sb(f"vtm_{t}", [128, 520], BF16) for t in range(NKB)]
    wqb = [sb(f"wqb_{j}", [128, 512], BF16) for j in range(NT)]
    wkb = wqb
    wvb = [sb(f"wvb_{j}", [128, 520], BF16) for j in range(NT)]
    wob = [sb(f"wob_{t}", [128, D], BF16) for t in range(4)]
    cos_t = [sb(f"cos_t{j}", [128, 512], BF16) for j in range(2)]
    sin_t = [sb(f"sin_t{j}", [128, 512], BF16) for j in range(2)]
    psg = sb("psg", [128, 128], BF16)
    mask_t = [sb(f"mask_{o}", [128, 512], BF16) for o in range(4)]
    xc = [sb(f"xc_{j}", [128, 512]) for j in range(NT)]
    qrot = [sb(f"qrot_{t}", [128, 512], BF16) for t in range(4)]
    qsb = sb("qsb", [128, 512], BF16)
    swsb = sb("swsb", [128, 512], BF16)
    sqk = sb("sqk", [128, 512])
    ptile = [sb(f"ptile_{j}", [128, 512], BF16) for j in range(2)]
    t1sb = ptile[0]
    t2sb = ptile[1]
    attn = [sb(f"attn_{t}", [128, 512], BF16) for t in range(4)]
    kmax = sb("kmax", [1, 8])
    qm1 = sb("qm1", [1, 1])
    mneg = [sb(f"mneg_{j}", [128, 1]) for j in range(2)]
    linv = sb("linv", [1, 512])
    linb = sb("linb", [64, 512])
    evac = [sb(f"evac_{j}", [128, 512]) for j in range(2)]
    ones_col = sb("ones_col", [128, 1])
    ones_row = sb("ones_row", [1, 128])
    eps1 = sb("eps1", [1, 1])
    sq2 = evac
    srow = sb("srow", [1, 512])
    qrow = sb("qrow", [1, 512])

    pb = [ps(f"pb{j}", [128, 512]).ap() for j in range(8)]

    sc = dict(ones_col=ones_col, ones_row=ones_row, sq2=sq2, srow=srow,
              qrow=qrow, eps1=eps1)

    with stack:
        P.memset("vector", ones_col[:], 1.0)
        P.memset("vector", ones_row[:], 1.0)
        P.memset("vector", eps1[:], LN_EPS)
        P.memset("vector", kmax[:], 0.0)
        P.dma("sync", psg[:], psign)
        for o in range(4):
            P.dma("sync", mask_t[o][:], masks[o])
        for j in range(NT):
            rows = slice(j * 128, (j + 1) * 128)
            s = evac[j % 2]
            P.dma("gpsimd", s[:], wk_t[rows, :])
            P.copy("gpsimd", wkb[j][:], s[:])
            P.dma("gpsimd", s[:], wv_t[rows, 0:512])
            P.copy("gpsimd", wvb[j][:, 0:512], s[:])
            P.dma("gpsimd", s[:, 0:8], wv_t[rows, 512:520])
            P.copy("gpsimd", wvb[j][:, 512:520], s[:, 0:8])
        for t in range(4):
            rows = slice(t * 128, (t + 1) * 128)
            s = evac[t % 2]
            P.dma("gpsimd", s[:, 0:512], wo_t[rows, 0:512])
            P.copy("gpsimd", wob[t][:, 0:512], s[:, 0:512])
            P.dma("gpsimd", s[:, 0:512], wo_t[rows, 512:1024])
            P.copy("gpsimd", wob[t][:, 512:1024], s[:, 0:512])

        for c in range(NCH):
            sl = slice(c * 512, (c + 1) * 512)
            for j in range(NT):
                P.dma("sync", xc[j][:], x_full[j * 128:(j + 1) * 128, sl],
                      tag="ld_x")
            emit_ln_chunk(P, [xc[j][:] for j in range(NT)], sc,
                          pb[0], pb[1], pb[2][0:1, :], pb[3][0:1, :])
            for j in range(NT):
                P.tt("vector", sqk[:], xc[j][:], pb[0], ALU.subtract,
                     tag="ln_sub")
                P.tt("vector", xn[j][:, sl], sqk[:], pb[1], ALU.mult,
                     tag="ln_mul")

        rope_state = {"sl": None}

        def rope_tabs(sl):
            if rope_state["sl"] != (sl.start, sl.stop):
                rope_state["sl"] = (sl.start, sl.stop)
                P.dma("sync", cos_t[0][:], ropec[:, sl], tag="rp_tc")
                P.dma("sync", sin_t[0][:], ropes[:, sl], tag="rp_ts")

        def rope(dst, src_ps, sl, veng="vector"):
            rope_tabs(sl)
            P.copy(veng, qsb[:], src_ps, tag="rp_ev")
            P.mm(pb[7], [(psg[:], qsb[:])], tag="rp_sw")
            P.copy(veng, swsb[:], pb[7], tag="rp_swev")
            P.tt(veng, t1sb[:], qsb[:], cos_t[0][:], ALU.mult, tag="rp_c")
            P.tt(veng, t2sb[:], swsb[:], sin_t[0][:], ALU.mult, tag="rp_s")
            P.tt(veng, dst, t1sb[:], t2sb[:], ALU.add, tag="rp_a")

        for c in range(NCH):
            sl = slice(c * 512, (c + 1) * 512)
            for t in range(4):
                cols = slice(t * 128, (t + 1) * 128)
                pm = pb[t % 2]
                P.mm(pm, [(wkb[j][:, cols], xn[j][:, sl]) for j in range(NT)],
                     tag="mm_k")
                rope(krot[t][:, sl], pm, sl)
                P.act(sqk[:], krot[t][:, sl], AF.Square, tag="kn_sq")
                for hh in range(2):
                    h = 2 * t + hh
                    base = 64 * hh
                    pn = pb[2 + hh][0:1, :]
                    P.mm(pn, [(ones_col[base:base + 64, :],
                               sqk[base:base + 64, :])], tag="kn_mm")
                    P.reduce("vector", qm1[:], pn, AX.X, ALU.max, tag="kn_rd")
                    P.tt("vector", kmax[:, h:h + 1], kmax[:, h:h + 1], qm1[:],
                         ALU.max, tag="kn_max")
            for tt4 in range(4):
                ti = c * 4 + tt4
                tsl = slice(ti * 128, (ti + 1) * 128)
                pv1 = pb[4 + tt4 % 2]
                pv2 = pb[6][:, 0:8] if tt4 % 2 == 0 else pb[7][:, 0:8]
                P.mm(pv1, [(xn[j][:, tsl], wvb[j][:, 0:512])
                           for j in range(NT)], tag="mm_v1")
                P.mm(pv2, [(xn[j][:, tsl], wvb[j][:, 512:520])
                           for j in range(NT)], tag="mm_v2")
                P.copy("scalar", vtm[ti][:, 0:512], pv1, tag="v_ev1")
                P.copy("scalar", vtm[ti][:, 512:520], pv2, tag="v_ev2")
                ones_view = vtm[ti][:].rearrange("p (h c) -> p h c", c=65)
                P.memset("gpsimd", ones_view[:, :, 64:65], 1.0, tag="v_ones")

        for j in range(NT):
            rows = slice(j * 128, (j + 1) * 128)
            s = evac[j % 2]
            P.dma("gpsimd", s[:], wq_t[rows, :])
            P.copy("gpsimd", wqb[j][:], s[:])

        for jq in range(NCH):
            sl = slice(jq * 512, (jq + 1) * 512)
            nkb = 4 * jq + 4
            for t in range(4):
                cols = slice(t * 128, (t + 1) * 128)
                pm = pb[t % 2]
                P.mm(pm, [(wqb[j][:, cols], xn[j][:, sl]) for j in range(NT)],
                     tag="mm_q")
                rope(qrot[t][:], pm, sl)
            for t in range(4):
                P.act(sqk[:], qrot[t][:], AF.Square, tag="qn_sq")
                for hh in range(2):
                    h = 2 * t + hh
                    base = 64 * hh
                    pn = pb[2 + hh][0:1, :]
                    P.mm(pn, [(ones_col[base:base + 64, :],
                               sqk[base:base + 64, :])], tag="qn_mm")
                    P.reduce("vector", qm1[:], pn, AX.X, ALU.max, tag="qn_rd")
                    P.tt("vector", qm1[:], qm1[:], kmax[:, h:h + 1], ALU.mult,
                         tag="qn_mul")
                    P.act(qm1[:], qm1[:], AF.Sqrt, tag="qn_sqrt")
                    P.ts("vector", qm1[:], qm1[:], -0.125, ALU.mult,
                         tag="qn_neg")
                    P.mm(pb[2 + hh][:, 0:1], [(ones_row[:], qm1[:])],
                         tag="qn_bc")
                    P.copy("vector", mneg[hh][:], pb[2 + hh][:, 0:1],
                           tag="qn_ev")
                for hh in range(2):
                    h = 2 * t + hh
                    base = 64 * hh
                    ps_s = pb[2 + hh]
                    ps_o = pb[4 + hh]
                    for kb in range(nkb):
                        ksl = slice(kb * 128, (kb + 1) * 128)
                        P.mm(ps_s, [(krot[t][base:base + 64, ksl],
                                     qrot[t][base:base + 64, :])],
                             tag="mm_s")
                        pt = ptile[hh]
                        P.act(pt[:], ps_s, AF.Exp, bias=mneg[hh][:],
                              scale=0.125, tag="exp")
                        moff = kb - 4 * jq
                        if moff >= 0:
                            P.tt("gpsimd", pt[:], pt[:], mask_t[moff][:],
                                 ALU.mult, tag="mask")
                        P.mm(ps_o[0:65, :],
                             [(vtm[kb][:, 65 * h:65 * h + 65], pt[:])],
                             start=(kb == 0), stop=(kb == nkb - 1),
                             tag="mm_pv")
                    P.copy("vector", linv[:], ps_o[64:65, :], tag="l_ev")
                    P.op("vector",
                         lambda eng, o=linv: eng.reciprocal(o[:], o[:]),
                         reads=["linv"], writes=["linv"], tag="l_rcp")
                    P.mm(pb[6][0:64, :], [(ones_row[:, 0:64], linv[:])],
                         tag="l_bc")
                    P.copy("vector", linb[:], pb[6][0:64, :], tag="l_bev")
                    P.tt("vector", attn[t][base:base + 64, :],
                         ps_o[0:64, :], linb[:], ALU.mult, tag="attn_div")
            for j in range(NT):
                dsl = slice(j * 128, (j + 1) * 128)
                pm = pb[6 + j % 2]
                P.mm(pm, [(wob[t][:, dsl], attn[t][:]) for t in range(4)],
                     tag="mm_wo")
                ev = evac[j % 2]
                P.copy("vector", ev[:], pm, tag="wo_ev")
                P.dma("gpsimd", part[dsl, sl], ev[:], tag="out")

        ctx = P.emit()
        with ctx:
            pass
    return nc


def l1_in_maps(x0_fm, inputs):
    D_K = 64
    theta = 10000.0
    pos = np.arange(S, dtype=np.float64)
    rates = theta ** (-(2.0 * np.arange(32, dtype=np.float64)) / D_K)
    ang = pos[None, :] * rates[:, None]
    import ml_dtypes
    cos32 = np.cos(ang)
    sin32 = np.sin(ang)
    ropec = np.tile(cos32, (4, 1)).astype(ml_dtypes.bfloat16)
    sgn = np.concatenate([-np.ones((32, 1)), np.ones((32, 1)),
                          -np.ones((32, 1)), np.ones((32, 1))], axis=0)
    ropes = (np.tile(sin32, (4, 1)) * sgn).astype(ml_dtypes.bfloat16)
    psign = np.zeros((128, 128), np.float64)
    for blk in range(2):
        b0 = 64 * blk
        psign[b0:b0 + 32, b0 + 32:b0 + 64] = np.eye(32)
        psign[b0 + 32:b0 + 64, b0:b0 + 32] = np.eye(32)
    psign = np.ascontiguousarray(psign.T).astype(ml_dtypes.bfloat16)
    masks = np.zeros((4, 128, 512), np.float32)
    for o in range(4):
        off = 128 * o
        for k in range(128):
            masks[o, k, off + k:] = 1.0
    masks = masks.astype(ml_dtypes.bfloat16)

    maps = []
    WqT = inputs["L0_attn_Wq"].T
    WkT = inputs["L0_attn_Wk"].T
    WvT = inputs["L0_attn_Wv"].T
    WoT = inputs["L0_attn_Wo"].T
    for c in range(8):
        b, i = c // 2, c % 2
        heads = list(range(8 * i, 8 * i + 8))
        cols = []
        for h in heads:
            cols += [64 * h + 2 * p for p in range(32)]
            cols += [64 * h + 2 * p + 1 for p in range(32)]
        cols = np.array(cols)
        wq = np.ascontiguousarray(WqT[:, cols])
        wk = np.ascontiguousarray(WkT[:, cols])
        wv = np.zeros((D, 520), np.float32)
        for hl, h in enumerate(heads):
            wv[:, 65 * hl:65 * hl + 64] = WvT[:, 64 * h:64 * h + 64]
        wo = np.ascontiguousarray(WoT[512 * i:512 * i + 512, :])
        maps.append({
            "x_full": x0_fm[b], "wq_t": wq, "wk_t": wk,
            "wv_t": np.ascontiguousarray(wv), "wo_t": wo,
            "ropec": ropec, "ropes": ropes, "psign": psign, "masks": masks,
        })
    return maps


def build_l3(core_half, nch_limit=None):
    nc = bass.Bass("TRN2", target_bir_lowering=False, debug=False)
    P = Prog(nc)
    TCC = 256
    NCHC = S // TCC
    NST = 16 * TCC

    x2f = nc.dram_tensor("x2f", [D, S], F32, kind="ExternalInput").ap()
    ln1g = nc.dram_tensor("ln1g", [128, NT], F32, kind="ExternalInput").ap()
    ln1b = nc.dram_tensor("ln1b", [128, NT], F32, kind="ExternalInput").ap()
    inx_t = nc.dram_tensor("inx_t", [D, D], F32, kind="ExternalInput").ap()
    inz_t = nc.dram_tensor("inz_t", [D, 512], F32, kind="ExternalInput").ap()
    cxw = nc.dram_tensor("cxw", [128, 32], F32, kind="ExternalInput").ap()
    cxb = nc.dram_tensor("cxb", [128, 8], F32, kind="ExternalInput").ap()
    czw = nc.dram_tensor("czw", [128, 16], F32, kind="ExternalInput").ap()
    czb = nc.dram_tensor("czb", [128, 4], F32, kind="ExternalInput").ap()
    xpj_t = nc.dram_tensor("xpj_t", [D, 96], F32, kind="ExternalInput").ap()
    dtw_t = nc.dram_tensor("dtw_t", [64, 512], F32, kind="ExternalInput").ap()
    dtb_t = nc.dram_tensor("dtb_t", [128, 4], F32, kind="ExternalInput").ap()
    alog_t = nc.dram_tensor("alog_t", [128, 64], F32, kind="ExternalInput").ap()
    dp_t = nc.dram_tensor("dp_t", [128, 4], F32, kind="ExternalInput").ap()
    outp_t = nc.dram_tensor("outp_t", [D, D], F32, kind="ExternalInput").ap()
    opart = nc.dram_tensor("opart", [D, S], F32, kind="ExternalOutput").ap()

    import contextlib
    stack = contextlib.ExitStack()
    sb = lambda name, shape, dt=F32: stack.enter_context(
        nc.sbuf_tensor(name, shape, dt))
    ps = lambda name, shape, dt=F32: stack.enter_context(
        nc.psum_tensor(name, shape, dt))

    inxb = [sb(f"inxb_{j}", [128, D], BF16) for j in range(NT)]
    inzb = [sb(f"inzb_{j}", [128, 512], BF16) for j in range(NT)]
    xpjb = [sb(f"xpjb_{j}", [128, 96], BF16) for j in range(NT)]
    dtwb = sb("dtwb", [64, 512], BF16)
    outpb = [sb(f"outpb_{j}", [128, D], BF16) for j in range(NT)]
    cxwb = sb("cxwb", [128, 32])
    cxbb = sb("cxbb", [128, 8])
    czwb = sb("czwb", [128, 16])
    czbb = sb("czbb", [128, 4])
    g1b = sb("g1b", [128, NT])
    b1b = sb("b1b", [128, NT])
    dtbb = sb("dtbb", [128, 4])
    alogb = sb("alogb", [128, 64])
    aneg = sb("aneg", [128, 64])
    dpb = sb("dpb", [128, 4])
    xc = [sb(f"xc_{j}", [128, TCC]) for j in range(NT)]
    xn2c = [sb(f"xn2c_{j}", [128, TCC], BF16) for j in range(NT)]
    xsr = [[sb(f"xsr_{j}_{s2}", [128, TCC + 3], BF16) for s2 in range(2)]
           for j in range(NT)]
    zsr = [[sb(f"zsr_{j}_{s2}", [128, TCC + 3], BF16) for s2 in range(2)]
           for j in range(4)]
    uc = [sb(f"uc_{j}", [128, TCC], BF16) for j in range(NT)]
    zcc = [sb(f"zcc_{j}", [128, TCC], BF16) for j in range(4)]
    xdb = sb("xdb", [80, TCC], BF16)
    bcf = sb("bcf", [1, 16 * TCC], BF16)
    ccf = sb("ccf", [1, 16 * TCC], BF16)
    ctmp = sb("ctmp", [16, TCC], BF16)
    delc = [sb(f"delc_{j}", [128, TCC], BF16) for j in range(4)]
    wch = sb("wch", [128, TCC], BF16)
    convt = [sb(f"convt_{k}", [128, TCC], BF16) for k in range(4)]
    spl = [sb("spl0", [128, TCC])] * 2
    spm = [sb("spm0", [128, TCC])] * 2
    dAc = [sb(f"dAc{k}", [128, NST]) for k in range(2)]
    argc = [sb("argc0", [128, NST], BF16)] * 2
    dBuc = [sb(f"dBuc{k}", [128, NST], BF16) for k in range(2)]
    hc = [sb(f"hc{k}", [128, NST], BF16) for k in range(2)]
    brep = sb("brep", [128, NST], BF16)
    crep = sb("crep", [128, NST], BF16)
    prodc = [sb("prodc0", [128, NST], BF16)] * 2
    hprev = [sb(f"hprev_{j}", [128, 16]) for j in range(4)]
    t16 = sb("t16", [128, 16])
    t16x = sb("t16x", [128, TCC], BF16)
    yc = [uc[4 + j] for j in range(4)]
    ev = [sb(f"ev_{j}", [128, 512]) for j in range(2)]
    otile = [sb(f"otile_{j}", [128, TCC]) for j in range(2)]
    zpad2 = sb("zpad2", [128, 2], BF16)
    ones_col = sb("ones_col", [128, 1])
    ones_row = sb("ones_row", [1, 128])
    ones_row_bf = sb("ones_row_bf", [1, 128], BF16)
    eps1 = sb("eps1", [1, 1])
    sq2 = [spl[0], spm[0]]
    srow = sb("srow", [1, TCC])
    qrow = sb("qrow", [1, TCC])
    r1row = sb("r1row", [1, TCC])
    pb = [ps(f"pb{j}", [128, 512]).ap() for j in range(8)]
    vrow = otile[1][0:1, 0:TCC]

    sc = dict(ones_col=ones_col, ones_row=ones_row, sq2=sq2, srow=srow,
              qrow=qrow, eps1=eps1)

    with stack:
        P.memset("vector", ones_col[:], 1.0)
        P.memset("vector", ones_row[:], 1.0)
        P.memset("vector", ones_row_bf[:], 1.0)
        P.memset("vector", eps1[:], LN_EPS)
        P.memset("vector", zpad2[:], 0.0)
        for jj in range(4):
            P.memset("vector", hprev[jj][:], 0.0)
            P.memset("vector", xsr[jj][0][:, 0:1], 0.0)
            P.memset("vector", zsr[jj][0][:, 0:1], 0.0)
        for jj in range(4, NT):
            P.memset("vector", xsr[jj][0][:, 0:1], 0.0)
        for j in range(NT):
            rows = slice(j * 128, (j + 1) * 128)
            s = ev[j % 2]
            for hh in range(2):
                csl = slice(hh * 512, (hh + 1) * 512)
                P.dma("gpsimd", s[:], inx_t[rows, csl])
                P.copy("gpsimd", inxb[j][:, csl], s[:])
                P.dma("gpsimd", s[:], outp_t[rows, csl])
                P.copy("gpsimd", outpb[j][:, csl], s[:])
            P.dma("gpsimd", s[:], inz_t[rows, :])
            P.copy("gpsimd", inzb[j][:], s[:])
            P.dma("gpsimd", s[:, 0:96], xpj_t[rows, :])
            P.copy("gpsimd", xpjb[j][:], s[:, 0:96])
        P.dma("sync", ev[0][0:64, 0:512], dtw_t)
        P.copy("vector", dtwb[:], ev[0][0:64, 0:512])
        P.dma("sync", cxwb[:], cxw)
        P.dma("sync", cxbb[:], cxb)
        P.dma("sync", czwb[:], czw)
        P.dma("sync", czbb[:], czb)
        P.dma("sync", g1b[:], ln1g)
        P.dma("sync", b1b[:], ln1b)
        P.dma("sync", dtbb[:], dtb_t)
        P.dma("sync", alogb[:], alog_t)
        P.dma("sync", dpb[:], dp_t)
        P.act(aneg[:], alogb[:], AF.Exp, tag="aexp")
        P.ts("vector", aneg[:], aneg[:], -1.0, ALU.mult, tag="aneg")

        def stage_in(c):
            sl = slice(c * TCC, (c + 1) * TCC)
            for j in range(NT):
                P.dma("sync", xc[j][:], x2f[j * 128:(j + 1) * 128, sl],
                      tag="ld_x2")
            n = NT
            P.mm(pb[0][0:1, 0:TCC], [(ones_col[:], xc[j][:]) for j in range(n)],
                 tag="ln_s1")
            for j in range(n):
                P.act(sq2[j % 2][:], xc[j][:], AF.Square, tag="ln_sq")
                P.mm(pb[1][0:1, 0:TCC], [(ones_col[:], sq2[j % 2][:])],
                     start=(j == 0), stop=(j == n - 1), tag="ln_s2")
            P.copy("vector", srow[:], pb[0][0:1, 0:TCC])
            P.copy("vector", qrow[:], pb[1][0:1, 0:TCC])
            P.ts("vector", srow[:], srow[:], 1.0 / D, ALU.mult)
            P.ts("vector", qrow[:], qrow[:], 1.0 / D, ALU.mult)
            P.tt("vector", vrow, srow[:], srow[:], ALU.mult)
            P.tt("vector", qrow[:], qrow[:], vrow, ALU.subtract)
            P.act(r1row[:], qrow[:], AF.Ln, bias=eps1[:], tag="ln_l1")
            P.act(r1row[:], r1row[:], AF.Exp, scale=-0.5, tag="ln_r1")
            P.tt("vector", vrow, r1row[:], r1row[:], ALU.mult)
            P.tt("vector", vrow, qrow[:], vrow, ALU.mult)
            P.act(vrow, vrow, AF.Ln, bias=eps1[:], tag="ln_l2")
            P.act(vrow, vrow, AF.Exp, scale=-0.5, tag="ln_r2")
            P.tt("vector", r1row[:], r1row[:], vrow, ALU.mult)
            P.mm(pb[0][:, 0:TCC], [(ones_row[:], srow[:])], tag="ln_bm")
            P.mm(pb[1][:, 0:TCC], [(ones_row[:], r1row[:])], tag="ln_br")
            for j in range(NT):
                P.tt("vector", sq2[j % 2][:], xc[j][:], pb[0][:, 0:TCC],
                     ALU.subtract, tag="ln_sub")
                P.tt("vector", sq2[j % 2][:], sq2[j % 2][:], pb[1][:, 0:TCC],
                     ALU.mult, tag="ln_mul")
                P.stt("vector", xn2c[j][:], sq2[j % 2][:], g1b[:, j:j + 1],
                      b1b[:, j:j + 1].broadcast_to((128, TCC)), ALU.mult,
                      ALU.add, tag="ln_aff")
            par = c % 2
            for j2 in range(NT):
                cols = slice(j2 * 128, (j2 + 1) * 128)
                pm = pb[2 + j2 % 2][:, 0:TCC]
                P.mm(pm, [(inxb[j][:, cols], xn2c[j][:]) for j in range(NT)],
                     tag="mm_inx")
                P.copy("scalar", xsr[j2][par][:, 1:1 + TCC], pm, tag="xs_ev")
            for j2 in range(4):
                cols = slice(j2 * 128, (j2 + 1) * 128)
                pm = pb[2 + j2 % 2][:, 0:TCC]
                P.mm(pm, [(inzb[j][:, cols], xn2c[j][:]) for j in range(NT)],
                     tag="mm_inz")
                P.copy("scalar", zsr[j2][par][:, 1:1 + TCC], pm, tag="zs_ev")
            if c >= 1:
                prv = (c - 1) % 2
                for j2 in range(NT):
                    P.copy("gpsimd", xsr[j2][par][:, 0:1],
                           xsr[j2][prv][:, TCC:TCC + 1], tag="halo_l")
                    P.copy("gpsimd", xsr[j2][prv][:, 1 + TCC:3 + TCC],
                           xsr[j2][par][:, 1:3], tag="halo_r")
                for j2 in range(4):
                    P.copy("gpsimd", zsr[j2][par][:, 0:1],
                           zsr[j2][prv][:, TCC:TCC + 1], tag="halo_lz")
                    P.copy("gpsimd", zsr[j2][prv][:, 1 + TCC:3 + TCC],
                           zsr[j2][par][:, 1:3], tag="halo_rz")

        def conv_silu(ring, wcol, bcol, dst, tag, k=0):
            for kk in range(4):
                P.ts("vector", convt[kk][:], ring[:, kk:kk + TCC], wcol[kk],
                     ALU.mult, tag=tag + str(kk))
            P.tt("vector", convt[0][:], convt[0][:], convt[1][:], ALU.add,
                 tag=tag + "a0")
            P.tt("vector", convt[2][:], convt[2][:], convt[3][:], ALU.add,
                 tag=tag + "a1")
            P.tt("vector", convt[0][:], convt[0][:], convt[2][:], ALU.add,
                 tag=tag + "a2")
            P.act(dst, convt[0][:], AF.Silu, bias=bcol, tag=tag + "s")

        def stage_scan(cc):
            par = cc % 2
            sl = slice(cc * TCC, (cc + 1) * TCC)
            for j2 in range(NT):
                wc = [cxwb[:, 4 * j2 + k:4 * j2 + k + 1] for k in range(4)]
                conv_silu(xsr[j2][par], wc, cxbb[:, j2:j2 + 1], uc[j2][:],
                          f"cvx{j2}_", k=j2)
            P.mm(pb[4][0:80, 0:TCC], [(xpjb[j][:, 0:80], uc[j][:])
                                      for j in range(NT)], tag="mm_xd")
            P.copy("scalar", xdb[:], pb[4][0:80, 0:TCC], tag="xd_ev")
            P.mm(pb[5][0:16, 0:TCC], [(xpjb[j][:, 80:96], uc[j][:])
                                      for j in range(NT)], tag="mm_xc")
            P.dma("scalar", bcf[:].rearrange("o (n t) -> o n t", n=16),
                  xdb[64:80, :], tag="b_flat")
            P.copy("scalar", ctmp[:], pb[5][0:16, 0:TCC], tag="c_ev")
            P.dma("scalar", ccf[:].rearrange("o (n t) -> o n t", n=16),
                  ctmp[:], tag="c_flat")
            for n2 in range(8):
                slb = slice(n2 * 2 * TCC, (n2 + 1) * 2 * TCC)
                pm = pb[6 if n2 % 2 == 0 else 7]
                P.mm(pm, [(ones_row_bf[:], bcf[:, slb])], tag="b_bc")
                P.act(brep[:, slb], pm, AF.Copy, tag="b_bev")
                pm2 = pb[7 if n2 % 2 == 0 else 6]
                P.mm(pm2, [(ones_row_bf[:], ccf[:, slb])], tag="c_bc")
                P.act(crep[:, slb], pm2, AF.Copy, tag="c_bev")
            def prep(jj):
                par2 = jj % 2
                dA_, arg_, dBu_ = dAc[par2], argc[par2], dBuc[par2]
                sp_l_, sp_m_ = spl[par2], spm[par2]
                dsl = slice(jj * 128, (jj + 1) * 128)
                pm = pb[4 + par2][:, 0:TCC]
                P.mm(pm, [(dtwb[:, dsl], xdb[0:64, :])], tag="mm_dt")
                P.copy("vector", sp_m_[:], pm, tag="dt_ev")
                P.act(sp_l_[:], sp_m_[:], AF.Abs, bias=dtbb[:, jj:jj + 1],
                      tag="sp_a")
                P.act(sp_l_[:], sp_l_[:], AF.Exp, scale=-1.0, tag="sp_e")
                P.act(sp_l_[:], sp_l_[:], AF.Ln, bias=1.0, tag="sp_l")
                P.ts("vector", sp_m_[:], sp_m_[:], dtbb[:, jj:jj + 1],
                     ALU.add, 0.0, ALU.max, tag="sp_m")
                P.tt("vector", delc[jj][:], sp_m_[:], sp_l_[:], ALU.add,
                     tag="sp_s")
                for n in range(16):
                    P.ts("vector", arg_[:, n * TCC:(n + 1) * TCC],
                         delc[jj][:], aneg[:, 16 * jj + n:16 * jj + n + 1],
                         ALU.mult, tag="arg")
                P.act(dA_[:], arg_[:], AF.Exp, tag="dA")
                P.tt("vector", wch[:], delc[jj][:], uc[jj][:],
                     ALU.mult, tag="w")
                wv_view = wch[:].rearrange("p (o t) -> p o t", o=1)\
                    .broadcast_to((128, 16, TCC))
                P.tt("vector", dBu_[:].rearrange("p (n t) -> p n t", n=16),
                     wv_view, brep[:].rearrange("p (n t) -> p n t", n=16),
                     ALU.mult, tag="dBu")
                dacol = dA_[:, 0:NST:TCC]
                dbcol = dBu_[:, 0:NST:TCC]
                P.tt("vector", t16[:], dacol, hprev[jj][:], ALU.mult,
                     tag="fix_m")
                P.tt("vector", dbcol, dbcol, t16[:], ALU.add, tag="fix_a")
                P.memset("vector", dacol, 0.0, tag="fix_z")

            def scan_jj(jj):
                par2 = jj % 2
                P.scan("vector", hc[par2][:], dAc[par2][:], dBuc[par2][:],
                       0.0, ALU.mult, ALU.add, tag="scan")
                hlast = hc[par2][:, TCC - 1:NST:TCC]
                P.copy("scalar", hprev[jj][:], hlast, tag="h_keep")

            def post(jj):
                par2 = jj % 2
                prod_ = prodc[par2]
                P.tt("vector", prod_[:], hc[par2][:], crep[:], ALU.mult,
                     tag="prod")
                half = 8 * TCC
                P.tt("vector", prod_[:, 0:half], prod_[:, 0:half],
                     prod_[:, half:2 * half], ALU.add, tag="tr1")
                P.tt("vector", prod_[:, 0:half // 2], prod_[:, 0:half // 2],
                     prod_[:, half // 2:half], ALU.add, tag="tr2")
                P.tt("vector", prod_[:, 0:half // 4], prod_[:, 0:half // 4],
                     prod_[:, half // 4:half // 2], ALU.add, tag="tr3")
                P.tt("vector", prod_[:, 0:TCC], prod_[:, 0:TCC],
                     prod_[:, TCC:2 * TCC], ALU.add, tag="tr4")
                P.ts("vector", t16x[:], uc[jj][:], dpb[:, jj:jj + 1],
                     ALU.mult, tag="y0")
                P.tt("vector", yc[jj][:], t16x[:], prod_[:, 0:TCC], ALU.add,
                     tag="y1")
                wcz = [czwb[:, 4 * jj + k:4 * jj + k + 1] for k in range(4)]
                conv_silu(zsr[jj][par], wcz, czbb[:, jj:jj + 1], zcc[jj][:],
                          f"cvz{jj}_", k=jj)

            prep(0)
            prep(1)
            scan_jj(0)
            prep(2)
            post(0)
            scan_jj(1)
            prep(3)
            post(1)
            scan_jj(2)
            post(2)
            scan_jj(3)
            post(3)
            for od in range(NT):
                cols = slice(od * 128, (od + 1) * 128)
                pm = pb[6 + od % 2][:, 0:TCC]
                P.mm(pm, [(outpb[jj][:, cols], yc[jj][:]) for jj in range(4)] +
                     [(outpb[4 + jj][:, cols], zcc[jj][:]) for jj in range(4)],
                     tag="mm_op")
                ot = otile[od % 2]
                P.copy("vector", ot[:], pm, tag="op_ev")
                P.dma("gpsimd", opart[od * 128:(od + 1) * 128, sl], ot[:],
                      tag="out")

        nrun = NCHC if nch_limit is None else nch_limit
        for c in range(nrun):
            stage_in(c)
            if c >= 1:
                stage_scan(c - 1)
        lastpar = (nrun - 1) % 2
        for j2 in range(NT):
            P.copy("gpsimd", xsr[j2][lastpar][:, 1 + TCC:3 + TCC], zpad2[:],
                   tag="pad_x")
        for j2 in range(4):
            P.copy("gpsimd", zsr[j2][lastpar][:, 1 + TCC:3 + TCC], zpad2[:],
                   tag="pad_z")
        stage_scan(nrun - 1)

        ctx = P.emit()
        with ctx:
            pass
    return nc


def l3_in_maps(x2_fm, inputs):
    def colsN(v, n):
        return np.ascontiguousarray(v.reshape(n, 128).T)
    inT = inputs["L1_attn_in_proj_w"].T
    xpjT = inputs["L1_attn_x_proj_w"].T
    outT = inputs["L1_attn_out_proj_w"].T
    cxw_all = inputs["L1_attn_conv_x_w"][:, 0, :]
    cxb_all = inputs["L1_attn_conv_x_b"]
    maps = []
    for c in range(8):
        b, i = c // 2, c % 2
        mysl = np.r_[512 * i:512 * i + 512, 512 * (1 - i):512 * (1 - i) + 512]
        hsl = slice(512 * i, 512 * i + 512)
        inxT = np.ascontiguousarray(inT[:, 0:1024][:, mysl])
        inzT = np.ascontiguousarray(inT[:, 1024 + 512 * i:1536 + 512 * i])
        cxw = np.ascontiguousarray(
            cxw_all[mysl].reshape(8, 128, 4).transpose(1, 0, 2)
            .reshape(128, 32))
        cxb = colsN(cxb_all[mysl], 8)
        czw = np.ascontiguousarray(
            inputs["L1_attn_conv_z_w"][:, 0, :][hsl].reshape(4, 128, 4)
            .transpose(1, 0, 2).reshape(128, 16))
        czb = colsN(inputs["L1_attn_conv_z_b"][hsl], 4)
        dtwT = np.ascontiguousarray(inputs["L1_attn_dt_proj_w"].T[:, hsl])
        dtb = colsN(inputs["L1_attn_dt_proj_b"][hsl], 4)
        alog = np.ascontiguousarray(
            inputs["L1_attn_A_log"][hsl].reshape(4, 128, 16)
            .transpose(1, 0, 2).reshape(128, 64))
        dp = colsN(inputs["L1_attn_Dp"][hsl], 4)
        outp = np.ascontiguousarray(
            np.concatenate([outT[hsl, :],
                            outT[1024 + 512 * i:1536 + 512 * i, :]], axis=0))
        maps.append({
            "x2f": x2_fm[b],
            "ln1g": colsN(inputs["L1_attn_ln1_g"], 8),
            "ln1b": colsN(inputs["L1_attn_ln1_b"], 8),
            "inx_t": inxT, "inz_t": inzT,
            "cxw": cxw, "cxb": cxb, "czw": czw, "czb": czb,
            "xpj_t": np.ascontiguousarray(xpjT[mysl, :]),
            "dtw_t": dtwT, "dtb_t": dtb, "alog_t": alog, "dp_t": dp,
            "outp_t": outp,
        })
    return maps


def build_lm():
    nc = bass.Bass("TRN2", target_bir_lowering=False, debug=False)
    P = Prog(nc)

    wfr = nc.dram_tensor("wfr", [D, D], F32, kind="ExternalInput").ap()
    wfi = nc.dram_tensor("wfi", [D, D], F32, kind="ExternalInput").ap()
    cmat = nc.dram_tensor("cmat", [D, D], F32, kind="ExternalInput").ap()
    smat = nc.dram_tensor("smat", [D, D], F32, kind="ExternalInput").ap()
    ccol = nc.dram_tensor("ccol", [D, 128], F32, kind="ExternalInput").ap()
    scol = nc.dram_tensor("scol", [D, 128], F32, kind="ExternalInput").ap()
    scoln = nc.dram_tensor("scoln", [D, 128], F32, kind="ExternalInput").ap()
    mshard = nc.dram_tensor("mshard", [D, 128], F32, kind="ExternalOutput").ap()

    import contextlib
    stack = contextlib.ExitStack()
    sb = lambda name, shape, dt=F32: stack.enter_context(
        nc.sbuf_tensor(name, shape, dt))
    ps = lambda name, shape, dt=F32: stack.enter_context(
        nc.psum_tensor(name, shape, dt))

    wfr_b = [sb(f"wfr_{j}", [128, D]) for j in range(NT)]
    wfi_b = [sb(f"wfi_{j}", [128, D]) for j in range(NT)]
    cm_b = [sb(f"cm_{j}", [128, D]) for j in range(NT)]
    sm_b = [sb(f"sm_{j}", [128, D]) for j in range(NT)]
    cc_b = [sb(f"cc_{j}", [128, 128]) for j in range(NT)]
    sc_b = [sb(f"sc_{j}", [128, 128]) for j in range(NT)]
    scn_b = [sb(f"scn_{j}", [128, 128]) for j in range(NT)]
    vr_b = [sb(f"vr_{j}", [128, 128]) for j in range(NT)]
    vi_b = [sb(f"vi_{j}", [128, 128]) for j in range(NT)]
    mt = [sb(f"mt_{j}", [128, 128]) for j in range(2)]

    pbank = [ps(f"pbank{j}", [128, 128]).ap() for j in range(4)]

    with stack:
        for j in range(NT):
            rows = slice(j * 128, (j + 1) * 128)
            P.dma("sync", wfr_b[j][:], wfr[rows, :])
            P.dma("sync", wfi_b[j][:], wfi[rows, :])
            P.dma("sync", cm_b[j][:], cmat[rows, :])
            P.dma("sync", sm_b[j][:], smat[rows, :])
            P.dma("gpsimd", cc_b[j][:], ccol[rows, :])
            P.dma("gpsimd", sc_b[j][:], scol[rows, :])
            P.dma("gpsimd", scn_b[j][:], scoln[rows, :])
        for e in range(NT):
            esl = slice(e * 128, (e + 1) * 128)
            pm = pbank[e % 2]
            P.mm(pm, [(wfr_b[o][:, esl], cc_b[o][:]) for o in range(NT)] +
                 [(wfi_b[o][:, esl], scn_b[o][:]) for o in range(NT)],
                 tag="vr")
            P.copy("vector", vr_b[e][:], pm)
            pm2 = pbank[2 + e % 2]
            P.mm(pm2, [(wfr_b[o][:, esl], sc_b[o][:]) for o in range(NT)] +
                 [(wfi_b[o][:, esl], cc_b[o][:]) for o in range(NT)],
                 tag="vi")
            P.copy("vector", vi_b[e][:], pm2)
        for dtl in range(NT):
            dsl = slice(dtl * 128, (dtl + 1) * 128)
            pm = pbank[dtl % 2]
            P.mm(pm, [(cm_b[e][:, dsl], vr_b[e][:]) for e in range(NT)] +
                 [(sm_b[e][:, dsl], vi_b[e][:]) for e in range(NT)], tag="m")
            P.ts("vector", mt[dtl % 2][:], pm, 1.0 / D, ALU.mult)
            P.dma("gpsimd", mshard[dsl, :], mt[dtl % 2][:])
        ctx = P.emit()
        with ctx:
            pass
    return nc


def lm_in_maps(inputs):
    idx = np.arange(D)
    ang = 2.0 * np.pi * np.outer(idx, idx) / D
    C = np.cos(ang).astype(np.float32)
    Sm = np.sin(ang).astype(np.float32)
    wfr = np.ascontiguousarray(inputs["L1_ffn_w_freq_real"].T)
    wfi = np.ascontiguousarray(inputs["L1_ffn_w_freq_imag"].T)
    maps = []
    for c in range(8):
        cols = slice(128 * c, 128 * (c + 1))
        maps.append({
            "wfr": wfr, "wfi": wfi, "cmat": C, "smat": Sm,
            "ccol": np.ascontiguousarray(C[:, cols]),
            "scol": np.ascontiguousarray(Sm[:, cols]),
            "scoln": np.ascontiguousarray(-Sm[:, cols]),
        })
    return maps


def _col8(v):
    return np.ascontiguousarray(v.reshape(8, 128).T)


def l4_in_maps(x2_fm, oparts, M, inputs):
    w1t = np.ascontiguousarray(
        inputs["L1_attn_mlp_w1"].T.reshape(8, 128, 32, 128)
        .transpose(2, 1, 0, 3).reshape(32, 128, 1024))
    w2t = np.ascontiguousarray(
        inputs["L1_attn_mlp_w2"].T.reshape(32, 128, 8, 128)
        .transpose(2, 1, 0, 3).reshape(8, 128, 4096))
    b1t = np.ascontiguousarray(inputs["L1_attn_mlp_b1"].reshape(32, 128).T)
    common = {
        "w1t": w1t, "w2t": w2t, "b1t": b1t,
        "b2t": _col8(inputs["L1_attn_mlp_b2"]),
        "ln2g": _col8(inputs["L1_attn_ln2_g"]),
        "ln2b": _col8(inputs["L1_attn_ln2_b"]),
        "sbias": _col8(inputs["L1_ffn_bias"]),
        "m_in": np.ascontiguousarray(M.astype(np.float32)),
    }
    maps = []
    for c in range(8):
        b, i = c // 2, c % 2
        t0 = i * TH
        maps.append(dict(common,
                         x2h=np.ascontiguousarray(x2_fm[b][:, t0:t0 + TH]),
                         oa=np.ascontiguousarray(oparts[2 * b][:, t0:t0 + TH]),
                         ob=np.ascontiguousarray(
                             oparts[2 * b + 1][:, t0:t0 + TH])))
    return maps


_NC_CACHE = {}


def _get_nc(name, builder):
    if name not in _NC_CACHE:
        _NC_CACHE[name] = builder()
    return _NC_CACHE[name]


def kernel(**inputs):
    cores = list(range(8))
    x0_fm = [np.ascontiguousarray(inputs["L0_attn_x"][b].T) for b in range(4)]

    res = run_bass_kernel_spmd(_get_nc("lm", build_lm), lm_in_maps(inputs),
                               core_ids=cores)
    M = np.concatenate([res.results[c]["mshard"] for c in range(8)], axis=1)

    res = run_bass_kernel_spmd(_get_nc("l1", build_l1),
                               l1_in_maps(x0_fm, inputs), core_ids=cores)
    partials = [res.results[c]["part"] for c in range(8)]

    res = run_bass_kernel_spmd(_get_nc("l2", build_l2),
                               l2_in_maps(x0_fm, partials, inputs),
                               core_ids=cores)
    x2_fm = [np.ascontiguousarray(
        np.concatenate([res.results[2 * b]["x2o"],
                        res.results[2 * b + 1]["x2o"]], axis=1))
        for b in range(4)]

    res = run_bass_kernel_spmd(_get_nc("l3", lambda: build_l3(0)),
                               l3_in_maps(x2_fm, inputs), core_ids=cores)
    oparts = [res.results[c]["opart"] for c in range(8)]

    res = run_bass_kernel_spmd(_get_nc("l4", build_l4),
                               l4_in_maps(x2_fm, oparts, M, inputs),
                               core_ids=cores)
    out = np.empty((4, S, D), np.float32)
    for b in range(4):
        full = np.concatenate([res.results[2 * b]["outh"],
                               res.results[2 * b + 1]["outh"]], axis=1)
        out[b] = full.T
    return out


# version 03
# speedup vs baseline: 1.4314x; 1.0479x over previous
import math
import numpy as np

import concourse.bass as bass
import concourse.mybir as mybir
from concourse.bass_utils import run_bass_kernel_spmd

F32 = mybir.dt.float32
BF16 = mybir.dt.bfloat16
AF = mybir.ActivationFunctionType
ALU = mybir.AluOpType
AX = mybir.AxisListType


class Prog:

    def __init__(self, nc: bass.Bass):
        self.nc = nc
        self.ops = []
        self.last_writer = {}
        self.readers = {}

    def _record(self, engine, emit, reads, writes, is_dma, tag):
        idx = len(self.ops)
        deps = set()
        for r in reads:
            if r in self.last_writer:
                deps.add(self.last_writer[r])
        for w in writes:
            if w in self.last_writer:
                deps.add(self.last_writer[w])
            for r in self.readers.get(w, ()):
                deps.add(r)
        deps.discard(idx)
        self.ops.append(dict(idx=idx, engine=engine, emit=emit, deps=deps,
                             is_dma=is_dma, needed=is_dma, tag=tag))
        for w in writes:
            self.last_writer[w] = idx
            self.readers[w] = []
        for r in reads:
            self.readers.setdefault(r, []).append(idx)
        return idx

    @staticmethod
    def _res(ap):
        return ap.tensor.name

    def _names(self, items):
        out = []
        for it in items:
            if it is None or isinstance(it, (int, float)):
                continue
            out.append(it if isinstance(it, str) else self._res(it))
        return out

    def dma(self, queue, out_ap, in_ap, tag="", **kw):
        def emit(eng):
            return eng.dma_start(out=out_ap, in_=in_ap, **kw)
        idx = self._record(queue, emit, self._names([in_ap]),
                           self._names([out_ap]), True, tag)
        self.ops[idx]["dest"] = self._names([out_ap])[0]
        return idx

    def tt(self, engine, out, in0, in1, aluop, tag=""):
        def emit(eng):
            return eng.tensor_tensor(out, in0, in1, aluop)
        return self._record(engine, emit, self._names([in0, in1]),
                            self._names([out]), False, tag)

    def ts(self, engine, out, in0, s1, op0, s2=None, op1=None, tag=""):
        def emit(eng):
            if op1 is None:
                return eng.tensor_scalar(out, in0, s1, None, op0)
            return eng.tensor_scalar(out, in0, s1, s2, op0, op1)
        return self._record(engine, emit, self._names([in0, s1, s2]),
                            self._names([out]), False, tag)

    def stt(self, engine, out, in0, scalar, in1, op0, op1, tag=""):
        def emit(eng):
            return eng.scalar_tensor_tensor(out, in0, scalar, in1, op0, op1)
        return self._record(engine, emit, self._names([in0, in1, scalar]),
                            self._names([out]), False, tag)

    def copy(self, engine, out, in_, tag=""):
        def emit(eng):
            if engine == "scalar":
                return eng.copy(out, in_)
            return eng.tensor_copy(out, in_)
        return self._record(engine, emit, self._names([in_]),
                            self._names([out]), False, tag)

    def op(self, engine, emit, reads=(), writes=(), tag=""):
        return self._record(engine, emit, self._names(list(reads)),
                            self._names(list(writes)), False, tag)

    def memset(self, engine, ap, const, tag=""):
        def emit(eng):
            return eng.memset(ap, const)
        return self._record(engine, emit, [], self._names([ap]), False, tag)

    def act(self, out, in_, func, bias=0.0, scale=1.0, tag="", accum_out=None):
        def emit(eng):
            kw = {} if accum_out is None else {"accum_out": accum_out}
            return eng.activation(out, in_, func, bias=bias, scale=scale, **kw)
        writes = [out] + ([accum_out] if accum_out is not None else [])
        return self._record("scalar", emit, self._names([in_, bias, scale]),
                            self._names(writes), False, tag)

    def reduce(self, engine, out, in_, axis, aluop, negate=None, tag=""):
        def emit(eng):
            kw = {}
            if negate is not None:
                kw["negate"] = negate
            return eng.tensor_reduce(out, in_, axis, aluop, **kw)
        return self._record(engine, emit, self._names([in_]),
                            self._names([out]), False, tag)

    def scan(self, engine, out, data0, data1, initial, op0, op1, tag=""):
        def emit(eng):
            return eng.tensor_tensor_scan(out, data0, data1, initial, op0, op1)
        return self._record(engine, emit, self._names([data0, data1, initial]),
                            self._names([out]), False, tag)

    def mm(self, out, pairs, start=True, stop=True, tag=""):
        n = len(pairs)

        def emit(eng):
            last = None
            for j, (lhsT, rhs) in enumerate(pairs):
                last = eng.matmul(out, lhsT, rhs,
                                  start=(start and j == 0),
                                  stop=(stop and j == n - 1))
            return last
        reads = []
        for lhsT, rhs in pairs:
            reads += [lhsT, rhs]
        return self._record("tensor", emit, self._names(reads),
                            self._names([out]), False, tag)

    def transpose(self, out, in_, identity, tag=""):
        def emit(eng):
            return eng.transpose(out, in_, identity)
        return self._record("tensor", emit, self._names([in_, identity]),
                            self._names([out]), False, tag)

    def emit(self):
        nc = self.nc
        for o in self.ops:
            for d in o["deps"]:
                self.ops[d]["needed"] = True
        streams = {}
        for o in self.ops:
            streams.setdefault(o["engine"], []).append(o)
        inc_rank = {}
        dma_keys = []
        dma_count = {}
        for eng_name, ops in streams.items():
            r = 0
            for o in ops:
                if not o["needed"]:
                    continue
                if o["is_dma"]:
                    key = "D_" + eng_name + "_" + o["dest"]
                    if key not in dma_count:
                        dma_count[key] = 0
                        dma_keys.append(key)
                    dma_count[key] += 16
                    o["semkey"] = key
                    inc_rank[o["idx"]] = dma_count[key]
                else:
                    r += 1
                    o["semkey"] = eng_name
                    inc_rank[o["idx"]] = r
        import contextlib
        stack = contextlib.ExitStack()
        sems = {e: stack.enter_context(nc.semaphore("S_" + e)) for e in streams}
        for key in dma_keys:
            sems[key] = stack.enter_context(nc.semaphore(key))
        block = stack.enter_context(nc.Block())
        totals = {}
        for o in self.ops:
            if o["needed"]:
                k = o["semkey"]
                totals[k] = max(totals.get(k, 0), inc_rank[o["idx"]])
        barrier_eng = self._barrier_engine()

        def make_section(eng_name):
            ops = streams[eng_name]

            def section(eng):
                waited = {}
                for o in ops:
                    bykey = {}
                    for d in o["deps"]:
                        dd = self.ops[d]
                        if dd["engine"] == "tensor" and eng_name == "tensor" \
                                and not dd["is_dma"]:
                            continue
                        k = dd["semkey"]
                        v = inc_rank[dd["idx"]]
                        if bykey.get(k, 0) < v:
                            bykey[k] = v
                    for k, v in sorted(bykey.items()):
                        if waited.get(k, 0) < v:
                            eng.wait_ge(sems[k], v)
                            waited[k] = v
                    inst = o["emit"](eng)
                    if o["needed"]:
                        inst.then_inc(sems[o["semkey"]],
                                      16 if o["is_dma"] else 1)
                if eng_name == barrier_eng:
                    for k, v in sorted(totals.items()):
                        if v > waited.get(k, 0) or k == eng_name:
                            if v > 0:
                                eng.wait_ge(sems[k], v)
            return section

        for eng_name in streams:
            getattr(block, eng_name)(make_section(eng_name))
        return stack

    def _barrier_engine(self):
        engs = {o["engine"] for o in self.ops}
        for e in ("gpsimd", "sync", "vector", "scalar", "tensor"):
            if e in engs:
                return e
        raise RuntimeError("empty program")


D = 1024
S = 4096
BATCH = 4
TH = 2048
NT = D // 128
LN_EPS = 1e-5


def emit_ln_chunk(P, srcs, sc, ps_mb, ps_rb, ps_a, ps_b, veng="vector",
                  d_total=D):
    n = len(srcs)
    inv = 1.0 / d_total
    P.mm(ps_a, [(sc["ones_col"][:], s) for s in srcs], tag="ln_s1")
    for j, s in enumerate(srcs):
        P.act(sc["sq2"][j % 2][:], s, AF.Square, tag="ln_sq")
        P.mm(ps_b, [(sc["ones_col"][:], sc["sq2"][j % 2][:])],
             start=(j == 0), stop=(j == n - 1), tag="ln_s2")
    P.copy(veng, sc["srow"][:], ps_a, tag="ln_ev1")
    P.copy(veng, sc["qrow"][:], ps_b, tag="ln_ev2")
    P.ts(veng, sc["srow"][:], sc["srow"][:], inv, ALU.mult, tag="ln_m")
    P.ts(veng, sc["qrow"][:], sc["qrow"][:], inv, ALU.mult, tag="ln_q")
    P.tt(veng, sc["sq2"][0][0:1, :], sc["srow"][:], sc["srow"][:], ALU.mult,
         tag="ln_m2")
    P.tt(veng, sc["qrow"][:], sc["qrow"][:], sc["sq2"][0][0:1, :],
         ALU.subtract, tag="ln_v")
    P.act(sc["qrow"][:], sc["qrow"][:], AF.Ln, bias=sc["eps1"][:], tag="ln_ln")
    P.act(sc["qrow"][:], sc["qrow"][:], AF.Exp, scale=-0.5, tag="ln_rstd")
    P.mm(ps_mb, [(sc["ones_row"][:], sc["srow"][:])], tag="ln_bm")
    P.mm(ps_rb, [(sc["ones_row"][:], sc["qrow"][:])], tag="ln_br")


def build_l4():
    nc = bass.Bass("TRN2", target_bir_lowering=False, debug=False)
    P = Prog(nc)
    M_TILES = 32
    TC = 512
    NCH = TH // TC

    x2h = nc.dram_tensor("x2h", [D, TH], F32, kind="ExternalInput").ap()
    oa = nc.dram_tensor("oa", [D, TH], F32, kind="ExternalInput").ap()
    ob = nc.dram_tensor("ob", [D, TH], F32, kind="ExternalInput").ap()
    w1t = nc.dram_tensor("w1t", [M_TILES, 128, D], F32,
                         kind="ExternalInput").ap()
    w2t = nc.dram_tensor("w2t", [NT, 128, 4096], F32,
                         kind="ExternalInput").ap()
    b1t = nc.dram_tensor("b1t", [128, M_TILES], F32, kind="ExternalInput").ap()
    b2t = nc.dram_tensor("b2t", [128, NT], F32, kind="ExternalInput").ap()
    ln2g = nc.dram_tensor("ln2g", [128, NT], F32, kind="ExternalInput").ap()
    ln2b = nc.dram_tensor("ln2b", [128, NT], F32, kind="ExternalInput").ap()
    sbias = nc.dram_tensor("sbias", [128, NT], F32, kind="ExternalInput").ap()
    m_in = nc.dram_tensor("m_in", [D, D], F32, kind="ExternalInput").ap()
    outh = nc.dram_tensor("outh", [D, TH], F32, kind="ExternalOutput").ap()
    w1s = nc.dram_tensor("w1s", [M_TILES, 128, D], BF16).ap()
    w2s = nc.dram_tensor("w2s", [NT, 128, 4096], BF16).ap()

    import contextlib
    stack = contextlib.ExitStack()
    sb = lambda name, shape, dt=F32: stack.enter_context(
        nc.sbuf_tensor(name, shape, dt))
    ps = lambda name, shape, dt=F32: stack.enter_context(
        nc.psum_tensor(name, shape, dt))

    xc = [sb(f"xc_{j}", [128, TC]) for j in range(NT)]
    ac = [sb(f"ac_{j}", [128, TC]) for j in range(NT)]
    ln2a = [sb(f"ln2a_{j}", [128, TC], BF16) for j in range(NT)]
    xn3 = [sb(f"xn3_{j}", [128, TC], BF16) for j in range(NT)]
    h1 = [sb(f"h1_{j}", [128, TC], BF16) for j in range(M_TILES)]
    Mbf = [sb(f"Mbf_{j}", [128, D], BF16) for j in range(NT)]
    wstg = [sb(f"wstg_{j}", [128, D], BF16) for j in range(2)]
    w2stg = [sb(f"w2stg_{j}", [128, 4096], BF16) for j in range(2)]
    cstg = [sb(f"cstg_{j}", [128, D]) for j in range(2)]
    pstg = [sb(f"pstg_{j}", [128, TC]) for j in range(4)]
    otile = [sb(f"otile_{j}", [128, TC]) for j in range(2)]
    tmpc = [sb(f"tmpc_{j}", [128, TC]) for j in range(2)]
    b1c = sb("b1c", [128, M_TILES])
    b2c = sb("b2c", [128, NT])
    gln = sb("gln", [128, NT])
    bln = sb("bln", [128, NT])
    sbc = sb("sbc", [128, NT])
    ones_col = sb("ones_col", [128, 1])
    ones_row = sb("ones_row", [1, 128])
    eps1 = sb("eps1", [1, 1])
    sq2 = [sb(f"sq2_{j}", [128, TC]) for j in range(2)]
    srow = sb("srow", [1, TC])
    qrow = sb("qrow", [1, TC])

    pb = [ps(f"pb{j}", [128, TC]).ap() for j in range(8)]

    sc = dict(ones_col=ones_col, ones_row=ones_row, sq2=sq2, srow=srow,
              qrow=qrow, eps1=eps1)

    with stack:
        P.memset("vector", ones_col[:], 1.0)
        P.memset("vector", ones_row[:], 1.0)
        P.memset("vector", eps1[:], LN_EPS)
        P.dma("sync", b1c[:], b1t)
        P.dma("sync", b2c[:], b2t)
        P.dma("sync", gln[:], ln2g)
        P.dma("sync", bln[:], ln2b)
        P.dma("sync", sbc[:], sbias)
        for j in range(NT):
            s = cstg[j % 2]
            P.dma("sync", s[:], m_in[j * 128:(j + 1) * 128, :])
            P.copy("gpsimd", Mbf[j][:], s[:])
        for mt in range(M_TILES):
            s = cstg[mt % 2]
            w = wstg[mt % 2]
            P.dma("gpsimd", s[:], w1t[mt])
            P.copy("gpsimd", w[:], s[:])
            P.dma("gpsimd", w1s[mt], w[:])
        for ot in range(NT):
            for h in range(4):
                s = cstg[h % 2]
                w = wstg[h % 2]
                csl = slice(h * D, (h + 1) * D)
                P.dma("gpsimd", s[:], w2t[ot][:, csl])
                P.copy("gpsimd", w[:], s[:])
                P.dma("gpsimd", w2s[ot][:, csl], w[:])

        for c in range(NCH):
            sl = slice(c * TC, (c + 1) * TC)
            for j in range(NT):
                P.dma("sync", xc[j][:], x2h[j * 128:(j + 1) * 128, sl],
                      tag="ld_x2")
            emit_ln_chunk(P, [xc[j][:] for j in range(NT)], sc,
                          pb[0], pb[1], pb[2][0:1, :], pb[3][0:1, :])
            for j in range(NT):
                P.tt("vector", tmpc[j % 2][:], xc[j][:], pb[0], ALU.subtract,
                     tag="a_sub")
                P.tt("vector", ac[j][:], tmpc[j % 2][:], pb[1], ALU.mult,
                     tag="a_mul")
            k = 0
            for j in range(NT):
                s0 = pstg[k % 4]; s1 = pstg[(k + 1) % 4]; k += 2
                P.dma("sync", s0[:], oa[j * 128:(j + 1) * 128, sl])
                P.dma("sync", s1[:], ob[j * 128:(j + 1) * 128, sl])
                P.tt("vector", ac[j][:], ac[j][:], s0[:], ALU.add)
                P.tt("vector", ac[j][:], ac[j][:], s1[:], ALU.add)
            emit_ln_chunk(P, [ac[j][:] for j in range(NT)], sc,
                          pb[4], pb[5], pb[6][0:1, :], pb[7][0:1, :])
            for j in range(NT):
                P.tt("vector", tmpc[j % 2][:], ac[j][:], pb[4], ALU.subtract,
                     tag="l2_sub")
                P.tt("vector", tmpc[j % 2][:], tmpc[j % 2][:], pb[5], ALU.mult,
                     tag="l2_mul")
                P.stt("vector", ln2a[j][:], tmpc[j % 2][:], gln[:, j:j + 1],
                      bln[:, j:j + 1].broadcast_to((128, TC)), ALU.mult,
                      ALU.add, tag="l2_aff")
                P.tt("gpsimd", xc[j][:], xc[j][:], ac[j][:], ALU.add,
                     tag="z_add")
            for mt in range(M_TILES):
                w = wstg[mt % 2]
                P.dma("gpsimd", w[:], w1s[mt], tag="w1_ld")
                pm = pb[mt % 2]
                P.mm(pm, [(w[:, j * 128:(j + 1) * 128], ln2a[j][:])
                          for j in range(NT)], tag="mm_h1")
                P.act(h1[mt][:], pm, AF.Gelu, bias=b1c[:, mt:mt + 1],
                      tag="gelu")
            for j in range(NT):
                w = w2stg[j % 2]
                P.dma("gpsimd", w[:], w2s[j], tag="w2_ld")
                pm = pb[2 + (j % 2)]
                P.mm(pm, [(w[:, mt * 128:(mt + 1) * 128], h1[mt][:])
                          for mt in range(M_TILES)], tag="mm_h2")
                P.stt("vector", xc[j][:], pm, b2c[:, j:j + 1], xc[j][:],
                      ALU.add, ALU.add, tag="x3")
            emit_ln_chunk(P, [xc[j][:] for j in range(NT)], sc,
                          pb[4], pb[5], pb[6][0:1, :], pb[7][0:1, :])
            for j in range(NT):
                P.tt("vector", tmpc[j % 2][:], xc[j][:], pb[4], ALU.subtract,
                     tag="n3_sub")
                P.tt("vector", xn3[j][:], tmpc[j % 2][:], pb[5], ALU.mult,
                     tag="n3_mul")
            for j in range(NT):
                pm = pb[6 if j % 2 == 0 else 7]
                P.mm(pm, [(Mbf[jj][:, j * 128:(j + 1) * 128], xn3[jj][:])
                          for jj in range(NT)], tag="mm_y")
                ot = otile[j % 2]
                P.stt("vector", ot[:], pm, sbc[:, j:j + 1], xc[j][:],
                      ALU.add, ALU.add, tag="spec_out")
                P.dma("gpsimd", outh[j * 128:(j + 1) * 128, sl], ot[:],
                      tag="out")

        ctx = P.emit()
        with ctx:
            pass
    return nc


def build_l2():
    nc = bass.Bass("TRN2", target_bir_lowering=False, debug=False)
    P = Prog(nc)

    xh = nc.dram_tensor("xh", [D, TH], F32, kind="ExternalInput").ap()
    pa = nc.dram_tensor("pa", [D, TH], F32, kind="ExternalInput").ap()
    pb = nc.dram_tensor("pb", [D, TH], F32, kind="ExternalInput").ap()
    halo = nc.dram_tensor("halo", [D, 3], F32, kind="ExternalInput").ap()
    w_r = nc.dram_tensor("w_r", [D, D], F32, kind="ExternalInput").ap()
    w_k = nc.dram_tensor("w_k", [D, D], F32, kind="ExternalInput").ap()
    w_v = nc.dram_tensor("w_v", [D, D], F32, kind="ExternalInput").ap()
    mu2 = nc.dram_tensor("mu2", [D, 2], F32, kind="ExternalInput").ap()
    x2o = nc.dram_tensor("x2o", [D, TH], F32, kind="ExternalOutput").ap()

    TC = 512
    NCH = TH // TC

    import contextlib
    stack = contextlib.ExitStack()
    sb = lambda name, shape, dt=F32: stack.enter_context(
        nc.sbuf_tensor(name, shape, dt))
    ps = lambda name, shape, dt=F32: stack.enter_context(
        nc.psum_tensor(name, shape, dt))

    x1 = [sb(f"x1_{j}", [128, TH]) for j in range(NT)]
    xn = [sb(f"xn_{j}", [128, 1 + TH], BF16) for j in range(NT)]
    pstg = [sb(f"pstg_{j}", [128, TC]) for j in range(4)]
    tmp_row = None
    halo_t = [sb(f"halo_{j}", [128, 3]) for j in range(NT)]
    hprev = [sb(f"hprev_{j}", [128, 1]) for j in range(NT)]
    hsq = [sb(f"hsq_{j}", [128, 1]) for j in range(NT)]
    mu_t = sb("mu_t", [128, 2 * NT])
    wR = [sb(f"wR_{j}", [128, D], BF16) for j in range(NT)]
    wK = [sb(f"wK_{j}", [128, D], BF16) for j in range(NT)]
    wV = [sb(f"wV_{j}", [128, D], BF16) for j in range(NT)]
    ones_col = sb("ones_col", [128, 1])
    ones_row = sb("ones_row", [1, 128])
    srow_t = sb("srow_t", [1, TH])
    qrow_t = sb("qrow_t", [1, TH])
    hrow = sb("hrow", [1, 2])
    eps1 = sb("eps1", [1, 1])
    hvar = sb("hvar", [1, 1])
    xr = [sb(f"xr_{j}", [128, TC], BF16) for j in range(NT)]
    xk = [sb(f"xk_{j}", [128, TC], BF16) for j in range(NT)]
    r_c = [sb(f"rc_{j}", [128, TC], BF16) for j in range(NT)]
    kact = [sb(f"kact_{j}", [128, TC], BF16) for j in range(NT)]
    dltc = sb("dltc", [128, TC], BF16)
    tmp_tc = sb("tmp_tc", [128, TC])
    tmp_tc2 = sb("tmp_tc2", [128, TC])

    ps_s1 = ps("ps_s1", [1, TC]).ap()
    ps_s2 = ps("ps_s2", [1, TC]).ap()
    ps_mb = ps("ps_mb", [128, TC]).ap()
    ps_rb = ps("ps_rb", [128, TC]).ap()
    ps_mm = [ps(f"ps_mm{j}", [128, TC]).ap() for j in range(3)]
    ps_h = ps_s1[:, 0:2]
    ps_hb = ps_mb[:, 0:2]

    srow = srow_t
    qrow = qrow_t

    with stack:
        P.memset("vector", ones_col[:], 1.0)
        P.memset("vector", ones_row[:], 1.0)
        P.memset("vector", eps1[:], LN_EPS)
        for j in range(NT):
            P.dma("sync", x1[j][:], xh[j * 128:(j + 1) * 128, :])
            P.dma("sync", halo_t[j][:], halo[j * 128:(j + 1) * 128, :])
        P.dma("sync", mu_t[:].rearrange("p (n c) -> p n c", c=2), mu2.rearrange("(n p) c -> p n c", p=128))
        k = 0
        for j in range(NT):
            rowsl = slice(j * 128, (j + 1) * 128)
            for c in range(NCH):
                sl = slice(c * TC, (c + 1) * TC)
                s0 = pstg[k % 4]; s1 = pstg[(k + 1) % 4]; k += 2
                P.dma("sync", s0[:], pa[rowsl, sl])
                P.dma("sync", s1[:], pb[rowsl, sl])
                P.tt("vector", x1[j][:, sl], x1[j][:, sl], s0[:], ALU.add)
                P.tt("vector", x1[j][:, sl], x1[j][:, sl], s1[:], ALU.add)
            P.tt("gpsimd", hprev[j][:], halo_t[j][:, 0:1], halo_t[j][:, 1:2],
                 ALU.add)
            P.tt("gpsimd", hprev[j][:], hprev[j][:], halo_t[j][:, 2:3],
                 ALU.add)
        k2 = 0
        for j in range(NT):
            rowsl = slice(j * 128, (j + 1) * 128)
            for (wsrc, wdst) in ((w_r, wR), (w_k, wK), (w_v, wV)):
                for h in range(2):
                    s = pstg[k2 % 4]; k2 += 1
                    csl = slice(h * TC, (h + 1) * TC)
                    P.dma("gpsimd", s[:], wsrc[rowsl, csl])
                    P.copy("gpsimd", wdst[j][:, csl], s[:])

        inv = 1.0 / D
        for c in range(NCH):
            sl = slice(c * TC, (c + 1) * TC)
            P.mm(ps_s1, [(ones_col[:], x1[j][:, sl]) for j in range(NT)],
                 tag="ln_s1")
            for j in range(NT):
                P.act((tmp_tc if j % 2 == 0 else tmp_tc2)[:], x1[j][:, sl], AF.Square, tag="ln_sq")
                P.mm(ps_s2, [(ones_col[:], (tmp_tc if j % 2 == 0 else tmp_tc2)[:])], start=(j == 0),
                     stop=(j == NT - 1), tag="ln_s2")
            P.copy("vector", srow[:, sl], ps_s1, tag="ln_ev1")
            P.copy("vector", qrow[:, sl], ps_s2, tag="ln_ev2")
        P.ts("vector", srow[:], srow[:], inv, ALU.mult, tag="ln_m")
        P.ts("vector", qrow[:], qrow[:], inv, ALU.mult, tag="ln_q")
        P.tt("vector", pstg[0][0:1, :], srow[:, 0:TC], srow[:, 0:TC], ALU.mult, tag="ln_m2")
        P.tt("vector", qrow[:, 0:TC], qrow[:, 0:TC], pstg[0][0:1, :], ALU.subtract, tag="ln_v")
        P.tt("vector", pstg[1][0:1, :], srow[:, TC:2 * TC], srow[:, TC:2 * TC], ALU.mult, tag="ln_m2")
        P.tt("vector", qrow[:, TC:2 * TC], qrow[:, TC:2 * TC], pstg[1][0:1, :], ALU.subtract, tag="ln_v")
        P.tt("vector", pstg[2][0:1, :], srow[:, 2 * TC:3 * TC], srow[:, 2 * TC:3 * TC], ALU.mult, tag="ln_m2")
        P.tt("vector", qrow[:, 2 * TC:3 * TC], qrow[:, 2 * TC:3 * TC], pstg[2][0:1, :], ALU.subtract, tag="ln_v")
        P.tt("vector", pstg[3][0:1, :], srow[:, 3 * TC:4 * TC], srow[:, 3 * TC:4 * TC], ALU.mult, tag="ln_m2")
        P.tt("vector", qrow[:, 3 * TC:4 * TC], qrow[:, 3 * TC:4 * TC], pstg[3][0:1, :], ALU.subtract, tag="ln_v")
        P.act(qrow[:], qrow[:], AF.Ln, bias=eps1[:], tag="ln_ln")
        P.act(qrow[:], qrow[:], AF.Exp, scale=-0.5, tag="ln_rstd")
        for c in range(NCH):
            sl = slice(c * TC, (c + 1) * TC)
            sl1 = slice(1 + c * TC, 1 + (c + 1) * TC)
            P.mm(ps_mb, [(ones_row[:], srow[:, sl])], tag="ln_bm")
            P.mm(ps_rb, [(ones_row[:], qrow[:, sl])], tag="ln_br")
            for j in range(NT):
                P.tt("vector", tmp_tc[:], x1[j][:, sl], ps_mb, ALU.subtract,
                     tag="ln_sub")
                P.tt("vector", xn[j][:, sl1], tmp_tc[:], ps_rb, ALU.mult,
                     tag="ln_mul")

        P.mm(ps_h[:, 0:1], [(ones_col[:], hprev[j][:]) for j in range(NT)])
        for j in range(NT):
            P.act(hsq[j][:], hprev[j][:], AF.Square)
        P.mm(ps_h[:, 1:2], [(ones_col[:], hsq[j][:]) for j in range(NT)])
        P.copy("vector", hrow[:], ps_h)
        P.ts("gpsimd", hrow[:], hrow[:], 1.0 / D, ALU.mult)
        P.tt("gpsimd", hvar[:], hrow[:, 0:1], hrow[:, 0:1], ALU.mult)
        P.tt("gpsimd", hrow[:, 1:2], hrow[:, 1:2], hvar[:], ALU.subtract)
        P.act(hrow[:, 1:2], hrow[:, 1:2], AF.Ln, bias=eps1[:])
        P.act(hrow[:, 1:2], hrow[:, 1:2], AF.Exp, scale=-0.5)
        P.mm(ps_hb, [(ones_row[:], hrow[:])])
        for j in range(NT):
            P.tt("vector", hsq[j][:], hprev[j][:], ps_hb[:, 0:1], ALU.subtract)
            P.tt("vector", xn[j][:, 0:1], hsq[j][:], ps_hb[:, 1:2], ALU.mult)

        for c in range(NCH):
            sl = slice(c * TC, (c + 1) * TC)
            for j in range(NT):
                cur = xn[j][:, 1 + c * TC: 1 + (c + 1) * TC]
                prev = xn[j][:, c * TC: (c + 1) * TC]
                P.tt("vector", dltc[:], cur, prev, ALU.subtract, tag="sh_d")
                P.stt("vector", xr[j][:], dltc[:], mu_t[:, 2 * j:2 * j + 1],
                      prev, ALU.mult, ALU.add, tag="sh_r")
                P.stt("vector", xk[j][:], dltc[:], mu_t[:, 2 * j + 1:2 * j + 2],
                      prev, ALU.mult, ALU.add, tag="sh_k")
            for j2 in range(NT):
                cols = slice(j2 * 128, (j2 + 1) * 128)
                pm = ps_mm[j2 % 2]
                P.mm(pm, [(wR[j][:, cols], xr[j][:]) for j in range(NT)],
                     tag="mm_r")
                P.act(r_c[j2][:], pm, AF.Sigmoid, tag="sig")
                pm2 = ps_mm[(j2 + 1) % 2]
                P.mm(pm2, [(wK[j][:, cols], xk[j][:]) for j in range(NT)],
                     tag="mm_k")
                P.ts("vector", tmp_tc[:], pm2, 0.0, ALU.max, tag="relu")
                P.act(kact[j2][:], tmp_tc[:], AF.Square, tag="ksq")
            for j2 in range(NT):
                cols = slice(j2 * 128, (j2 + 1) * 128)
                pm = ps_mm[2]
                P.mm(pm, [(wV[j][:, cols], kact[j][:]) for j in range(NT)],
                     tag="mm_v")
                P.tt("vector", tmp_tc2[:], r_c[j2][:], pm, ALU.mult,
                     tag="rmul")
                P.tt("vector", x1[j2][:, sl], x1[j2][:, sl], tmp_tc2[:],
                     ALU.add, tag="resid")
                P.dma("gpsimd", x2o[j2 * 128:(j2 + 1) * 128, sl],
                      x1[j2][:, sl], tag="out")

        ctx = P.emit()
        with ctx:
            pass
    return nc


def _fm(x):
    return np.ascontiguousarray(x.T)


def l2_in_maps(x0_fm, partials, inputs):
    maps = []
    for c in range(8):
        b, i = c // 2, c % 2
        t0 = i * TH
        pA = partials[2 * b]
        pB = partials[2 * b + 1]
        if i == 0:
            halo = np.zeros((D, 3), np.float32)
        else:
            halo = np.stack([x0_fm[b][:, t0 - 1], pA[:, t0 - 1],
                             pB[:, t0 - 1]], axis=1)
        maps.append({
            "xh": np.ascontiguousarray(x0_fm[b][:, t0:t0 + TH]),
            "pa": np.ascontiguousarray(pA[:, t0:t0 + TH]),
            "pb": np.ascontiguousarray(pB[:, t0:t0 + TH]),
            "halo": np.ascontiguousarray(halo),
            "w_r": np.ascontiguousarray(inputs["L0_ffn_W_r"].T),
            "w_k": np.ascontiguousarray(inputs["L0_ffn_W_k"].T),
            "w_v": np.ascontiguousarray(inputs["L0_ffn_W_v"].T),
            "mu2": np.ascontiguousarray(
                np.stack([inputs["L0_ffn_mu_r"], inputs["L0_ffn_mu_k"]],
                         axis=1)),
        })
    return maps


def build_l1():
    nc = bass.Bass("TRN2", target_bir_lowering=False, debug=False)
    P = Prog(nc)
    NCH = S // 512
    NKB = S // 128

    x_full = nc.dram_tensor("x_full", [D, S], F32, kind="ExternalInput").ap()
    wq_t = nc.dram_tensor("wq_t", [D, 512], F32, kind="ExternalInput").ap()
    wk_t = nc.dram_tensor("wk_t", [D, 512], F32, kind="ExternalInput").ap()
    wv_t = nc.dram_tensor("wv_t", [D, 520], F32, kind="ExternalInput").ap()
    wo_t = nc.dram_tensor("wo_t", [512, D], F32, kind="ExternalInput").ap()
    ropec = nc.dram_tensor("ropec", [128, S], BF16, kind="ExternalInput").ap()
    ropes = nc.dram_tensor("ropes", [128, S], BF16, kind="ExternalInput").ap()
    psign = nc.dram_tensor("psign", [128, 128], BF16, kind="ExternalInput").ap()
    masks = nc.dram_tensor("masks", [4, 128, 512], BF16,
                           kind="ExternalInput").ap()
    part = nc.dram_tensor("part", [D, S], F32, kind="ExternalOutput").ap()

    import contextlib
    stack = contextlib.ExitStack()
    sb = lambda name, shape, dt=F32: stack.enter_context(
        nc.sbuf_tensor(name, shape, dt))
    ps = lambda name, shape, dt=F32: stack.enter_context(
        nc.psum_tensor(name, shape, dt))

    xn = [sb(f"xn_{j}", [128, S], BF16) for j in range(NT)]
    krot = [sb(f"krot_{t}", [128, S], BF16) for t in range(4)]
    vtm = [sb(f"vtm_{t}", [128, 520], BF16) for t in range(NKB)]
    wqb = [sb(f"wqb_{j}", [128, 512], BF16) for j in range(NT)]
    wkb = wqb
    wvb = [sb(f"wvb_{j}", [128, 520], BF16) for j in range(NT)]
    wob = [sb(f"wob_{t}", [128, D], BF16) for t in range(4)]
    cos_t = [sb(f"cos_t{j}", [128, 512], BF16) for j in range(2)]
    sin_t = [sb(f"sin_t{j}", [128, 512], BF16) for j in range(2)]
    psg = sb("psg", [128, 128], BF16)
    mask_t = [sb(f"mask_{o}", [128, 512], BF16) for o in range(4)]
    xc = [sb(f"xc_{j}", [128, 512]) for j in range(NT)]
    qrot = [sb(f"qrot_{t}", [128, 512], BF16) for t in range(4)]
    qsb = sb("qsb", [128, 512], BF16)
    swsb = sb("swsb", [128, 512], BF16)
    sqk = sb("sqk", [128, 512])
    ptile = [sb(f"ptile_{j}", [128, 512], BF16) for j in range(4)]
    t1sb = ptile[0]
    t2sb = ptile[1]
    attn = [sb(f"attn_{t}", [128, 512], BF16) for t in range(4)]
    kmax = sb("kmax", [1, 8])
    qm1 = sb("qm1", [1, 1])
    qm8 = sb("qm8", [1, 8])
    mneg8 = sb("mneg8", [128, 8])
    sqk2 = None
    linv = sb("linv", [1, 512])
    linb = sb("linb", [64, 512])
    evac = [sb(f"evac_{j}", [128, 512]) for j in range(2)]
    ones_col = sb("ones_col", [128, 1])
    ones_row = sb("ones_row", [1, 128])
    eps1 = sb("eps1", [1, 1])
    sq2 = evac
    srow = sb("srow", [1, 512])
    qrow = sb("qrow", [1, 512])

    pb = [ps(f"pb{j}", [128, 512]).ap() for j in range(8)]
    sqk2 = [sqk, evac[0]]

    sc = dict(ones_col=ones_col, ones_row=ones_row, sq2=sq2, srow=srow,
              qrow=qrow, eps1=eps1)

    with stack:
        P.memset("vector", ones_col[:], 1.0)
        P.memset("vector", ones_row[:], 1.0)
        P.memset("vector", eps1[:], LN_EPS)
        P.memset("vector", kmax[:], 0.0)
        P.dma("sync", psg[:], psign)
        for o in range(4):
            P.dma("sync", mask_t[o][:], masks[o])
        for j in range(NT):
            rows = slice(j * 128, (j + 1) * 128)
            s = evac[j % 2]
            P.dma("gpsimd", s[:], wk_t[rows, :])
            P.copy("gpsimd", wkb[j][:], s[:])
            P.dma("gpsimd", s[:], wv_t[rows, 0:512])
            P.copy("gpsimd", wvb[j][:, 0:512], s[:])
            P.dma("gpsimd", s[:, 0:8], wv_t[rows, 512:520])
            P.copy("gpsimd", wvb[j][:, 512:520], s[:, 0:8])
        for t in range(4):
            rows = slice(t * 128, (t + 1) * 128)
            s = evac[t % 2]
            P.dma("gpsimd", s[:, 0:512], wo_t[rows, 0:512])
            P.copy("gpsimd", wob[t][:, 0:512], s[:, 0:512])
            P.dma("gpsimd", s[:, 0:512], wo_t[rows, 512:1024])
            P.copy("gpsimd", wob[t][:, 512:1024], s[:, 0:512])

        for c in range(NCH):
            sl = slice(c * 512, (c + 1) * 512)
            for j in range(NT):
                P.dma("sync", xc[j][:], x_full[j * 128:(j + 1) * 128, sl],
                      tag="ld_x")
            emit_ln_chunk(P, [xc[j][:] for j in range(NT)], sc,
                          pb[0], pb[1], pb[2][0:1, :], pb[3][0:1, :])
            for j in range(NT):
                P.tt("vector", sqk[:], xc[j][:], pb[0], ALU.subtract,
                     tag="ln_sub")
                P.tt("vector", xn[j][:, sl], sqk[:], pb[1], ALU.mult,
                     tag="ln_mul")

        rope_state = {"sl": None}

        def rope_tabs(sl):
            if rope_state["sl"] != (sl.start, sl.stop):
                rope_state["sl"] = (sl.start, sl.stop)
                P.dma("sync", cos_t[0][:], ropec[:, sl], tag="rp_tc")
                P.dma("sync", sin_t[0][:], ropes[:, sl], tag="rp_ts")

        def rope(dst, src_ps, sl, veng="vector"):
            rope_tabs(sl)
            P.copy(veng, qsb[:], src_ps, tag="rp_ev")
            P.mm(pb[7], [(psg[:], qsb[:])], tag="rp_sw")
            P.copy(veng, swsb[:], pb[7], tag="rp_swev")
            P.tt(veng, t1sb[:], qsb[:], cos_t[0][:], ALU.mult, tag="rp_c")
            P.tt(veng, t2sb[:], swsb[:], sin_t[0][:], ALU.mult, tag="rp_s")
            P.tt(veng, dst, t1sb[:], t2sb[:], ALU.add, tag="rp_a")

        for c in range(NCH):
            sl = slice(c * 512, (c + 1) * 512)
            for t in range(4):
                cols = slice(t * 128, (t + 1) * 128)
                pm = pb[t % 2]
                P.mm(pm, [(wkb[j][:, cols], xn[j][:, sl]) for j in range(NT)],
                     tag="mm_k")
                rope(krot[t][:, sl], pm, sl)
                P.act(sqk[:], krot[t][:, sl], AF.Square, tag="kn_sq")
                for hh in range(2):
                    h = 2 * t + hh
                    base = 64 * hh
                    pn = pb[2 + hh][0:1, :]
                    P.mm(pn, [(ones_col[base:base + 64, :],
                               sqk[base:base + 64, :])], tag="kn_mm")
                    P.reduce("vector", qm1[:], pn, AX.X, ALU.max, tag="kn_rd")
                    P.tt("vector", kmax[:, h:h + 1], kmax[:, h:h + 1], qm1[:],
                         ALU.max, tag="kn_max")
            for tt4 in range(4):
                ti = c * 4 + tt4
                tsl = slice(ti * 128, (ti + 1) * 128)
                pv1 = pb[4 + tt4 % 2]
                pv2 = pb[6][:, 0:8] if tt4 % 2 == 0 else pb[7][:, 0:8]
                P.mm(pv1, [(xn[j][:, tsl], wvb[j][:, 0:512])
                           for j in range(NT)], tag="mm_v1")
                P.mm(pv2, [(xn[j][:, tsl], wvb[j][:, 512:520])
                           for j in range(NT)], tag="mm_v2")
                P.copy("scalar", vtm[ti][:, 0:512], pv1, tag="v_ev1")
                P.copy("scalar", vtm[ti][:, 512:520], pv2, tag="v_ev2")
                ones_view = vtm[ti][:].rearrange("p (h c) -> p h c", c=65)
                P.memset("gpsimd", ones_view[:, :, 64:65], 1.0, tag="v_ones")

        for j in range(NT):
            rows = slice(j * 128, (j + 1) * 128)
            s = evac[j % 2]
            P.dma("gpsimd", s[:], wq_t[rows, :])
            P.copy("gpsimd", wqb[j][:], s[:])

        for jq in range(NCH):
            sl = slice(jq * 512, (jq + 1) * 512)
            nkb = 4 * jq + 4
            for t in range(4):
                cols = slice(t * 128, (t + 1) * 128)
                pm = pb[t % 2]
                P.mm(pm, [(wqb[j][:, cols], xn[j][:, sl]) for j in range(NT)],
                     tag="mm_q")
                rope(qrot[t][:], pm, sl)
            for t in range(4):
                P.act(sqk2[t % 2][:], qrot[t][:], AF.Square, tag="qn_sq")
                for hh in range(2):
                    h = 2 * t + hh
                    base = 64 * hh
                    pn = pb[6 + hh][0:1, :]
                    P.mm(pn, [(ones_col[base:base + 64, :],
                               sqk2[t % 2][base:base + 64, :])], tag="qn_mm")
                    P.reduce("vector", qm8[:, h:h + 1], pn, AX.X, ALU.max,
                             tag="qn_rd")
            P.tt("vector", qm8[:], qm8[:], kmax[:], ALU.mult, tag="qn_mul")
            P.act(qm8[:], qm8[:], AF.Sqrt, tag="qn_sqrt")
            P.ts("vector", qm8[:], qm8[:], -0.125, ALU.mult, tag="qn_neg")
            P.mm(pb[6][:, 0:8], [(ones_row[:], qm8[:])], tag="qn_bc")
            P.copy("vector", mneg8[:], pb[6][:, 0:8], tag="qn_ev")
            for t in range(4):
                hA, hB = 2 * t, 2 * t + 1

                def s_exp(kb):
                    par = kb % 2
                    ksl = slice(kb * 128, (kb + 1) * 128)
                    moff = kb - 4 * jq
                    for hh in range(2):
                        base = 64 * hh
                        ps_s = pb[2 * hh + par]
                        P.mm(ps_s, [(krot[t][base:base + 64, ksl],
                                     qrot[t][base:base + 64, :])], tag="mm_s")
                        pt = ptile[2 * hh + par]
                        P.act(pt[:], ps_s, AF.Exp,
                              bias=mneg8[:, 2 * t + hh:2 * t + hh + 1],
                              scale=0.125, tag="exp")
                        if moff >= 0:
                            P.tt("gpsimd", pt[:], pt[:], mask_t[moff][:],
                                 ALU.mult, tag="mask")

                def pv(kb):
                    par = kb % 2
                    for hh in range(2):
                        h = 2 * t + hh
                        P.mm(pb[4 + hh][0:65, :],
                             [(vtm[kb][:, 65 * h:65 * h + 65],
                               ptile[2 * hh + par][:])],
                             start=(kb == 0), stop=(kb == nkb - 1),
                             tag="mm_pv")

                s_exp(0)
                for kb in range(1, nkb):
                    s_exp(kb)
                    pv(kb - 1)
                pv(nkb - 1)
                for hh in range(2):
                    base = 64 * hh
                    ps_o = pb[4 + hh]
                    P.copy("vector", linv[:], ps_o[64:65, :], tag="l_ev")
                    P.act(linv[:], linv[:], AF.Ln, tag="l_ln")
                    P.act(linv[:], linv[:], AF.Exp, scale=-1.0, tag="l_exp")
                    P.mm(pb[6][0:64, :], [(ones_row[:, 0:64], linv[:])],
                         tag="l_bc")
                    P.copy("vector", linb[:], pb[6][0:64, :], tag="l_bev")
                    P.tt("vector", attn[t][base:base + 64, :],
                         ps_o[0:64, :], linb[:], ALU.mult, tag="attn_div")
            for j in range(NT):
                dsl = slice(j * 128, (j + 1) * 128)
                pm = pb[6 + j % 2]
                P.mm(pm, [(wob[t][:, dsl], attn[t][:]) for t in range(4)],
                     tag="mm_wo")
                ev = evac[j % 2]
                P.copy("vector", ev[:], pm, tag="wo_ev")
                P.dma("gpsimd", part[dsl, sl], ev[:], tag="out")

        ctx = P.emit()
        with ctx:
            pass
    return nc


def l1_in_maps(x0_fm, inputs):
    D_K = 64
    theta = 10000.0
    pos = np.arange(S, dtype=np.float64)
    rates = theta ** (-(2.0 * np.arange(32, dtype=np.float64)) / D_K)
    ang = pos[None, :] * rates[:, None]
    import ml_dtypes
    cos32 = np.cos(ang)
    sin32 = np.sin(ang)
    ropec = np.tile(cos32, (4, 1)).astype(ml_dtypes.bfloat16)
    sgn = np.concatenate([-np.ones((32, 1)), np.ones((32, 1)),
                          -np.ones((32, 1)), np.ones((32, 1))], axis=0)
    ropes = (np.tile(sin32, (4, 1)) * sgn).astype(ml_dtypes.bfloat16)
    psign = np.zeros((128, 128), np.float64)
    for blk in range(2):
        b0 = 64 * blk
        psign[b0:b0 + 32, b0 + 32:b0 + 64] = np.eye(32)
        psign[b0 + 32:b0 + 64, b0:b0 + 32] = np.eye(32)
    psign = np.ascontiguousarray(psign.T).astype(ml_dtypes.bfloat16)
    masks = np.zeros((4, 128, 512), np.float32)
    for o in range(4):
        off = 128 * o
        for k in range(128):
            masks[o, k, off + k:] = 1.0
    masks = masks.astype(ml_dtypes.bfloat16)

    maps = []
    WqT = inputs["L0_attn_Wq"].T
    WkT = inputs["L0_attn_Wk"].T
    WvT = inputs["L0_attn_Wv"].T
    WoT = inputs["L0_attn_Wo"].T
    for c in range(8):
        b, i = c // 2, c % 2
        heads = list(range(8 * i, 8 * i + 8))
        cols = []
        for h in heads:
            cols += [64 * h + 2 * p for p in range(32)]
            cols += [64 * h + 2 * p + 1 for p in range(32)]
        cols = np.array(cols)
        wq = np.ascontiguousarray(WqT[:, cols])
        wk = np.ascontiguousarray(WkT[:, cols])
        wv = np.zeros((D, 520), np.float32)
        for hl, h in enumerate(heads):
            wv[:, 65 * hl:65 * hl + 64] = WvT[:, 64 * h:64 * h + 64]
        wo = np.ascontiguousarray(WoT[512 * i:512 * i + 512, :])
        maps.append({
            "x_full": x0_fm[b], "wq_t": wq, "wk_t": wk,
            "wv_t": np.ascontiguousarray(wv), "wo_t": wo,
            "ropec": ropec, "ropes": ropes, "psign": psign, "masks": masks,
        })
    return maps


def build_l3(core_half, nch_limit=None):
    nc = bass.Bass("TRN2", target_bir_lowering=False, debug=False)
    P = Prog(nc)
    TCC = 256
    NCHC = S // TCC
    NST = 16 * TCC

    x2f = nc.dram_tensor("x2f", [D, S], F32, kind="ExternalInput").ap()
    ln1g = nc.dram_tensor("ln1g", [128, NT], F32, kind="ExternalInput").ap()
    ln1b = nc.dram_tensor("ln1b", [128, NT], F32, kind="ExternalInput").ap()
    inx_t = nc.dram_tensor("inx_t", [D, D], F32, kind="ExternalInput").ap()
    inz_t = nc.dram_tensor("inz_t", [D, 512], F32, kind="ExternalInput").ap()
    cxw = nc.dram_tensor("cxw", [128, 32], F32, kind="ExternalInput").ap()
    cxb = nc.dram_tensor("cxb", [128, 8], F32, kind="ExternalInput").ap()
    czw = nc.dram_tensor("czw", [128, 16], F32, kind="ExternalInput").ap()
    czb = nc.dram_tensor("czb", [128, 4], F32, kind="ExternalInput").ap()
    xpj_t = nc.dram_tensor("xpj_t", [D, 96], F32, kind="ExternalInput").ap()
    dtw_t = nc.dram_tensor("dtw_t", [64, 512], F32, kind="ExternalInput").ap()
    dtb_t = nc.dram_tensor("dtb_t", [128, 4], F32, kind="ExternalInput").ap()
    alog_t = nc.dram_tensor("alog_t", [128, 64], F32, kind="ExternalInput").ap()
    dp_t = nc.dram_tensor("dp_t", [128, 4], F32, kind="ExternalInput").ap()
    outp_t = nc.dram_tensor("outp_t", [D, D], F32, kind="ExternalInput").ap()
    opart = nc.dram_tensor("opart", [D, S], F32, kind="ExternalOutput").ap()

    import contextlib
    stack = contextlib.ExitStack()
    sb = lambda name, shape, dt=F32: stack.enter_context(
        nc.sbuf_tensor(name, shape, dt))
    ps = lambda name, shape, dt=F32: stack.enter_context(
        nc.psum_tensor(name, shape, dt))

    inxb = [sb(f"inxb_{j}", [128, D], BF16) for j in range(NT)]
    inzb = [sb(f"inzb_{j}", [128, 512], BF16) for j in range(NT)]
    xpjb = [sb(f"xpjb_{j}", [128, 96], BF16) for j in range(NT)]
    dtwb = sb("dtwb", [64, 512], BF16)
    outpb = [sb(f"outpb_{j}", [128, D], BF16) for j in range(NT)]
    cxwb = sb("cxwb", [128, 32])
    cxbb = sb("cxbb", [128, 8])
    czwb = sb("czwb", [128, 16])
    czbb = sb("czbb", [128, 4])
    g1b = sb("g1b", [128, NT])
    b1b = sb("b1b", [128, NT])
    dtbb = sb("dtbb", [128, 4])
    alogb = sb("alogb", [128, 64])
    aneg = sb("aneg", [128, 64])
    dpb = sb("dpb", [128, 4])
    xc = [sb(f"xc_{j}", [128, TCC]) for j in range(NT)]
    xn2c = [sb(f"xn2c_{j}", [128, TCC], BF16) for j in range(NT)]
    xsr = [[sb(f"xsr_{j}_{s2}", [128, TCC + 3], BF16) for s2 in range(2)]
           for j in range(NT)]
    zsr = [[sb(f"zsr_{j}_{s2}", [128, TCC + 3], BF16) for s2 in range(2)]
           for j in range(4)]
    uc = [sb(f"uc_{j}", [128, TCC], BF16) for j in range(NT)]
    zcc = [sb(f"zcc_{j}", [128, TCC], BF16) for j in range(4)]
    xdb = sb("xdb", [80, TCC], BF16)
    bcf = sb("bcf", [1, 16 * TCC], BF16)
    ccf = sb("ccf", [1, 16 * TCC], BF16)
    ctmp = sb("ctmp", [16, TCC], BF16)
    delc = [sb(f"delc_{j}", [128, TCC], BF16) for j in range(4)]
    wch = sb("wch", [128, TCC], BF16)
    convt = [sb(f"convt_{k}", [128, TCC], BF16) for k in range(4)]
    spl = [sb("spl0", [128, TCC])] * 2
    spm = [sb("spm0", [128, TCC])] * 2
    dAc = [sb(f"dAc{k}", [128, NST]) for k in range(2)]
    argc = [sb("argc0", [128, NST], BF16)] * 2
    dBuc = [sb(f"dBuc{k}", [128, NST], BF16) for k in range(2)]
    hc = [sb(f"hc{k}", [128, NST], BF16) for k in range(2)]
    brep = sb("brep", [128, NST], BF16)
    crep = sb("crep", [128, NST], BF16)
    prodc = [sb("prodc0", [128, NST], BF16)] * 2
    hprev = [sb(f"hprev_{j}", [128, 16]) for j in range(4)]
    t16 = sb("t16", [128, 16])
    t16x = sb("t16x", [128, TCC], BF16)
    yc = [uc[4 + j] for j in range(4)]
    ev = [sb(f"ev_{j}", [128, 512]) for j in range(2)]
    otile = [sb(f"otile_{j}", [128, TCC]) for j in range(2)]
    zpad2 = sb("zpad2", [128, 2], BF16)
    ones_col = sb("ones_col", [128, 1])
    ones_row = sb("ones_row", [1, 128])
    ones_row_bf = sb("ones_row_bf", [1, 128], BF16)
    eps1 = sb("eps1", [1, 1])
    sq2 = [spl[0], spm[0]]
    srow = sb("srow", [1, TCC])
    qrow = sb("qrow", [1, TCC])
    r1row = sb("r1row", [1, TCC])
    pb = [ps(f"pb{j}", [128, 512]).ap() for j in range(8)]
    vrow = otile[1][0:1, 0:TCC]

    sc = dict(ones_col=ones_col, ones_row=ones_row, sq2=sq2, srow=srow,
              qrow=qrow, eps1=eps1)

    with stack:
        P.memset("vector", ones_col[:], 1.0)
        P.memset("vector", ones_row[:], 1.0)
        P.memset("vector", ones_row_bf[:], 1.0)
        P.memset("vector", eps1[:], LN_EPS)
        P.memset("vector", zpad2[:], 0.0)
        for jj in range(4):
            P.memset("vector", hprev[jj][:], 0.0)
            P.memset("vector", xsr[jj][0][:, 0:1], 0.0)
            P.memset("vector", zsr[jj][0][:, 0:1], 0.0)
        for jj in range(4, NT):
            P.memset("vector", xsr[jj][0][:, 0:1], 0.0)
        for j in range(NT):
            rows = slice(j * 128, (j + 1) * 128)
            s = ev[j % 2]
            for hh in range(2):
                csl = slice(hh * 512, (hh + 1) * 512)
                P.dma("gpsimd", s[:], inx_t[rows, csl])
                P.copy("gpsimd", inxb[j][:, csl], s[:])
                P.dma("gpsimd", s[:], outp_t[rows, csl])
                P.copy("gpsimd", outpb[j][:, csl], s[:])
            P.dma("gpsimd", s[:], inz_t[rows, :])
            P.copy("gpsimd", inzb[j][:], s[:])
            P.dma("gpsimd", s[:, 0:96], xpj_t[rows, :])
            P.copy("gpsimd", xpjb[j][:], s[:, 0:96])
        P.dma("sync", ev[0][0:64, 0:512], dtw_t)
        P.copy("vector", dtwb[:], ev[0][0:64, 0:512])
        P.dma("sync", cxwb[:], cxw)
        P.dma("sync", cxbb[:], cxb)
        P.dma("sync", czwb[:], czw)
        P.dma("sync", czbb[:], czb)
        P.dma("sync", g1b[:], ln1g)
        P.dma("sync", b1b[:], ln1b)
        P.dma("sync", dtbb[:], dtb_t)
        P.dma("sync", alogb[:], alog_t)
        P.dma("sync", dpb[:], dp_t)
        P.act(aneg[:], alogb[:], AF.Exp, tag="aexp")
        P.ts("vector", aneg[:], aneg[:], -1.0, ALU.mult, tag="aneg")

        def stage_in(c):
            sl = slice(c * TCC, (c + 1) * TCC)
            for j in range(NT):
                P.dma("sync", xc[j][:], x2f[j * 128:(j + 1) * 128, sl],
                      tag="ld_x2")
            n = NT
            P.mm(pb[0][0:1, 0:TCC], [(ones_col[:], xc[j][:]) for j in range(n)],
                 tag="ln_s1")
            for j in range(n):
                P.act(sq2[j % 2][:], xc[j][:], AF.Square, tag="ln_sq")
                P.mm(pb[1][0:1, 0:TCC], [(ones_col[:], sq2[j % 2][:])],
                     start=(j == 0), stop=(j == n - 1), tag="ln_s2")
            P.copy("vector", srow[:], pb[0][0:1, 0:TCC])
            P.copy("vector", qrow[:], pb[1][0:1, 0:TCC])
            P.ts("vector", srow[:], srow[:], 1.0 / D, ALU.mult)
            P.ts("vector", qrow[:], qrow[:], 1.0 / D, ALU.mult)
            P.tt("vector", vrow, srow[:], srow[:], ALU.mult)
            P.tt("vector", qrow[:], qrow[:], vrow, ALU.subtract)
            P.act(r1row[:], qrow[:], AF.Ln, bias=eps1[:], tag="ln_l1")
            P.act(r1row[:], r1row[:], AF.Exp, scale=-0.5, tag="ln_r1")
            P.tt("vector", vrow, r1row[:], r1row[:], ALU.mult)
            P.tt("vector", vrow, qrow[:], vrow, ALU.mult)
            P.act(vrow, vrow, AF.Ln, bias=eps1[:], tag="ln_l2")
            P.act(vrow, vrow, AF.Exp, scale=-0.5, tag="ln_r2")
            P.tt("vector", r1row[:], r1row[:], vrow, ALU.mult)
            P.mm(pb[0][:, 0:TCC], [(ones_row[:], srow[:])], tag="ln_bm")
            P.mm(pb[1][:, 0:TCC], [(ones_row[:], r1row[:])], tag="ln_br")
            for j in range(NT):
                P.tt("vector", sq2[j % 2][:], xc[j][:], pb[0][:, 0:TCC],
                     ALU.subtract, tag="ln_sub")
                P.tt("vector", sq2[j % 2][:], sq2[j % 2][:], pb[1][:, 0:TCC],
                     ALU.mult, tag="ln_mul")
                P.stt("vector", xn2c[j][:], sq2[j % 2][:], g1b[:, j:j + 1],
                      b1b[:, j:j + 1].broadcast_to((128, TCC)), ALU.mult,
                      ALU.add, tag="ln_aff")
            par = c % 2
            for j2 in range(NT):
                cols = slice(j2 * 128, (j2 + 1) * 128)
                pm = pb[2 + j2 % 2][:, 0:TCC]
                P.mm(pm, [(inxb[j][:, cols], xn2c[j][:]) for j in range(NT)],
                     tag="mm_inx")
                P.copy("scalar", xsr[j2][par][:, 1:1 + TCC], pm, tag="xs_ev")
            for j2 in range(4):
                cols = slice(j2 * 128, (j2 + 1) * 128)
                pm = pb[2 + j2 % 2][:, 0:TCC]
                P.mm(pm, [(inzb[j][:, cols], xn2c[j][:]) for j in range(NT)],
                     tag="mm_inz")
                P.copy("scalar", zsr[j2][par][:, 1:1 + TCC], pm, tag="zs_ev")
            if c >= 1:
                prv = (c - 1) % 2
                for j2 in range(NT):
                    P.copy("gpsimd", xsr[j2][par][:, 0:1],
                           xsr[j2][prv][:, TCC:TCC + 1], tag="halo_l")
                    P.copy("gpsimd", xsr[j2][prv][:, 1 + TCC:3 + TCC],
                           xsr[j2][par][:, 1:3], tag="halo_r")
                for j2 in range(4):
                    P.copy("gpsimd", zsr[j2][par][:, 0:1],
                           zsr[j2][prv][:, TCC:TCC + 1], tag="halo_lz")
                    P.copy("gpsimd", zsr[j2][prv][:, 1 + TCC:3 + TCC],
                           zsr[j2][par][:, 1:3], tag="halo_rz")

        def conv_silu(ring, wcol, bcol, dst, tag, k=0):
            for kk in range(4):
                P.ts("vector", convt[kk][:], ring[:, kk:kk + TCC], wcol[kk],
                     ALU.mult, tag=tag + str(kk))
            P.tt("vector", convt[0][:], convt[0][:], convt[1][:], ALU.add,
                 tag=tag + "a0")
            P.tt("vector", convt[2][:], convt[2][:], convt[3][:], ALU.add,
                 tag=tag + "a1")
            P.tt("vector", convt[0][:], convt[0][:], convt[2][:], ALU.add,
                 tag=tag + "a2")
            P.act(dst, convt[0][:], AF.Silu, bias=bcol, tag=tag + "s")

        def stage_scan(cc):
            par = cc % 2
            sl = slice(cc * TCC, (cc + 1) * TCC)
            for j2 in range(NT):
                wc = [cxwb[:, 4 * j2 + k:4 * j2 + k + 1] for k in range(4)]
                conv_silu(xsr[j2][par], wc, cxbb[:, j2:j2 + 1], uc[j2][:],
                          f"cvx{j2}_", k=j2)
            P.mm(pb[4][0:80, 0:TCC], [(xpjb[j][:, 0:80], uc[j][:])
                                      for j in range(NT)], tag="mm_xd")
            P.copy("scalar", xdb[:], pb[4][0:80, 0:TCC], tag="xd_ev")
            P.mm(pb[5][0:16, 0:TCC], [(xpjb[j][:, 80:96], uc[j][:])
                                      for j in range(NT)], tag="mm_xc")
            P.dma("scalar", bcf[:].rearrange("o (n t) -> o n t", n=16),
                  xdb[64:80, :], tag="b_flat")
            P.copy("scalar", ctmp[:], pb[5][0:16, 0:TCC], tag="c_ev")
            P.dma("scalar", ccf[:].rearrange("o (n t) -> o n t", n=16),
                  ctmp[:], tag="c_flat")
            for n2 in range(8):
                slb = slice(n2 * 2 * TCC, (n2 + 1) * 2 * TCC)
                pm = pb[6 if n2 % 2 == 0 else 7]
                P.mm(pm, [(ones_row_bf[:], bcf[:, slb])], tag="b_bc")
                P.act(brep[:, slb], pm, AF.Copy, tag="b_bev")
                pm2 = pb[7 if n2 % 2 == 0 else 6]
                P.mm(pm2, [(ones_row_bf[:], ccf[:, slb])], tag="c_bc")
                P.act(crep[:, slb], pm2, AF.Copy, tag="c_bev")
            def prep(jj):
                par2 = jj % 2
                dA_, arg_, dBu_ = dAc[par2], argc[par2], dBuc[par2]
                sp_l_, sp_m_ = spl[par2], spm[par2]
                dsl = slice(jj * 128, (jj + 1) * 128)
                pm = pb[4 + par2][:, 0:TCC]
                P.mm(pm, [(dtwb[:, dsl], xdb[0:64, :])], tag="mm_dt")
                P.copy("vector", sp_m_[:], pm, tag="dt_ev")
                P.act(sp_l_[:], sp_m_[:], AF.Abs, bias=dtbb[:, jj:jj + 1],
                      tag="sp_a")
                P.act(sp_l_[:], sp_l_[:], AF.Exp, scale=-1.0, tag="sp_e")
                P.act(sp_l_[:], sp_l_[:], AF.Ln, bias=1.0, tag="sp_l")
                P.ts("vector", sp_m_[:], sp_m_[:], dtbb[:, jj:jj + 1],
                     ALU.add, 0.0, ALU.max, tag="sp_m")
                P.tt("vector", delc[jj][:], sp_m_[:], sp_l_[:], ALU.add,
                     tag="sp_s")
                for n in range(16):
                    P.ts("vector", arg_[:, n * TCC:(n + 1) * TCC],
                         delc[jj][:], aneg[:, 16 * jj + n:16 * jj + n + 1],
                         ALU.mult, tag="arg")
                P.act(dA_[:], arg_[:], AF.Exp, tag="dA")
                P.tt("vector", wch[:], delc[jj][:], uc[jj][:],
                     ALU.mult, tag="w")
                wv_view = wch[:].rearrange("p (o t) -> p o t", o=1)\
                    .broadcast_to((128, 16, TCC))
                P.tt("vector", dBu_[:].rearrange("p (n t) -> p n t", n=16),
                     wv_view, brep[:].rearrange("p (n t) -> p n t", n=16),
                     ALU.mult, tag="dBu")
                dacol = dA_[:, 0:NST:TCC]
                dbcol = dBu_[:, 0:NST:TCC]
                P.tt("vector", t16[:], dacol, hprev[jj][:], ALU.mult,
                     tag="fix_m")
                P.tt("vector", dbcol, dbcol, t16[:], ALU.add, tag="fix_a")
                P.memset("vector", dacol, 0.0, tag="fix_z")

            def scan_jj(jj):
                par2 = jj % 2
                P.scan("vector", hc[par2][:], dAc[par2][:], dBuc[par2][:],
                       0.0, ALU.mult, ALU.add, tag="scan")
                hlast = hc[par2][:, TCC - 1:NST:TCC]
                P.copy("scalar", hprev[jj][:], hlast, tag="h_keep")

            def post(jj):
                par2 = jj % 2
                prod_ = prodc[par2]
                P.tt("vector", prod_[:], hc[par2][:], crep[:], ALU.mult,
                     tag="prod")
                half = 8 * TCC
                P.tt("vector", prod_[:, 0:half], prod_[:, 0:half],
                     prod_[:, half:2 * half], ALU.add, tag="tr1")
                P.tt("vector", prod_[:, 0:half // 2], prod_[:, 0:half // 2],
                     prod_[:, half // 2:half], ALU.add, tag="tr2")
                P.tt("vector", prod_[:, 0:half // 4], prod_[:, 0:half // 4],
                     prod_[:, half // 4:half // 2], ALU.add, tag="tr3")
                P.tt("vector", prod_[:, 0:TCC], prod_[:, 0:TCC],
                     prod_[:, TCC:2 * TCC], ALU.add, tag="tr4")
                P.ts("vector", t16x[:], uc[jj][:], dpb[:, jj:jj + 1],
                     ALU.mult, tag="y0")
                P.tt("vector", yc[jj][:], t16x[:], prod_[:, 0:TCC], ALU.add,
                     tag="y1")
                wcz = [czwb[:, 4 * jj + k:4 * jj + k + 1] for k in range(4)]
                conv_silu(zsr[jj][par], wcz, czbb[:, jj:jj + 1], zcc[jj][:],
                          f"cvz{jj}_", k=jj)

            prep(0)
            prep(1)
            scan_jj(0)
            prep(2)
            post(0)
            scan_jj(1)
            prep(3)
            post(1)
            scan_jj(2)
            post(2)
            scan_jj(3)
            post(3)
            for od in range(NT):
                cols = slice(od * 128, (od + 1) * 128)
                pm = pb[6 + od % 2][:, 0:TCC]
                P.mm(pm, [(outpb[jj][:, cols], yc[jj][:]) for jj in range(4)] +
                     [(outpb[4 + jj][:, cols], zcc[jj][:]) for jj in range(4)],
                     tag="mm_op")
                ot = otile[od % 2]
                P.copy("vector", ot[:], pm, tag="op_ev")
                P.dma("gpsimd", opart[od * 128:(od + 1) * 128, sl], ot[:],
                      tag="out")

        nrun = NCHC if nch_limit is None else nch_limit
        for c in range(nrun):
            stage_in(c)
            if c >= 1:
                stage_scan(c - 1)
        lastpar = (nrun - 1) % 2
        for j2 in range(NT):
            P.copy("gpsimd", xsr[j2][lastpar][:, 1 + TCC:3 + TCC], zpad2[:],
                   tag="pad_x")
        for j2 in range(4):
            P.copy("gpsimd", zsr[j2][lastpar][:, 1 + TCC:3 + TCC], zpad2[:],
                   tag="pad_z")
        stage_scan(nrun - 1)

        ctx = P.emit()
        with ctx:
            pass
    return nc


def l3_in_maps(x2_fm, inputs):
    def colsN(v, n):
        return np.ascontiguousarray(v.reshape(n, 128).T)
    inT = inputs["L1_attn_in_proj_w"].T
    xpjT = inputs["L1_attn_x_proj_w"].T
    outT = inputs["L1_attn_out_proj_w"].T
    cxw_all = inputs["L1_attn_conv_x_w"][:, 0, :]
    cxb_all = inputs["L1_attn_conv_x_b"]
    maps = []
    for c in range(8):
        b, i = c // 2, c % 2
        mysl = np.r_[512 * i:512 * i + 512, 512 * (1 - i):512 * (1 - i) + 512]
        hsl = slice(512 * i, 512 * i + 512)
        inxT = np.ascontiguousarray(inT[:, 0:1024][:, mysl])
        inzT = np.ascontiguousarray(inT[:, 1024 + 512 * i:1536 + 512 * i])
        cxw = np.ascontiguousarray(
            cxw_all[mysl].reshape(8, 128, 4).transpose(1, 0, 2)
            .reshape(128, 32))
        cxb = colsN(cxb_all[mysl], 8)
        czw = np.ascontiguousarray(
            inputs["L1_attn_conv_z_w"][:, 0, :][hsl].reshape(4, 128, 4)
            .transpose(1, 0, 2).reshape(128, 16))
        czb = colsN(inputs["L1_attn_conv_z_b"][hsl], 4)
        dtwT = np.ascontiguousarray(inputs["L1_attn_dt_proj_w"].T[:, hsl])
        dtb = colsN(inputs["L1_attn_dt_proj_b"][hsl], 4)
        alog = np.ascontiguousarray(
            inputs["L1_attn_A_log"][hsl].reshape(4, 128, 16)
            .transpose(1, 0, 2).reshape(128, 64))
        dp = colsN(inputs["L1_attn_Dp"][hsl], 4)
        outp = np.ascontiguousarray(
            np.concatenate([outT[hsl, :],
                            outT[1024 + 512 * i:1536 + 512 * i, :]], axis=0))
        maps.append({
            "x2f": x2_fm[b],
            "ln1g": colsN(inputs["L1_attn_ln1_g"], 8),
            "ln1b": colsN(inputs["L1_attn_ln1_b"], 8),
            "inx_t": inxT, "inz_t": inzT,
            "cxw": cxw, "cxb": cxb, "czw": czw, "czb": czb,
            "xpj_t": np.ascontiguousarray(xpjT[mysl, :]),
            "dtw_t": dtwT, "dtb_t": dtb, "alog_t": alog, "dp_t": dp,
            "outp_t": outp,
        })
    return maps


def build_lm():
    nc = bass.Bass("TRN2", target_bir_lowering=False, debug=False)
    P = Prog(nc)

    wfr = nc.dram_tensor("wfr", [D, D], F32, kind="ExternalInput").ap()
    wfi = nc.dram_tensor("wfi", [D, D], F32, kind="ExternalInput").ap()
    cmat = nc.dram_tensor("cmat", [D, D], F32, kind="ExternalInput").ap()
    smat = nc.dram_tensor("smat", [D, D], F32, kind="ExternalInput").ap()
    ccol = nc.dram_tensor("ccol", [D, 128], F32, kind="ExternalInput").ap()
    scol = nc.dram_tensor("scol", [D, 128], F32, kind="ExternalInput").ap()
    scoln = nc.dram_tensor("scoln", [D, 128], F32, kind="ExternalInput").ap()
    mshard = nc.dram_tensor("mshard", [D, 128], F32, kind="ExternalOutput").ap()

    import contextlib
    stack = contextlib.ExitStack()
    sb = lambda name, shape, dt=F32: stack.enter_context(
        nc.sbuf_tensor(name, shape, dt))
    ps = lambda name, shape, dt=F32: stack.enter_context(
        nc.psum_tensor(name, shape, dt))

    wfr_b = [sb(f"wfr_{j}", [128, D]) for j in range(NT)]
    wfi_b = [sb(f"wfi_{j}", [128, D]) for j in range(NT)]
    cm_b = [sb(f"cm_{j}", [128, D]) for j in range(NT)]
    sm_b = [sb(f"sm_{j}", [128, D]) for j in range(NT)]
    cc_b = [sb(f"cc_{j}", [128, 128]) for j in range(NT)]
    sc_b = [sb(f"sc_{j}", [128, 128]) for j in range(NT)]
    scn_b = [sb(f"scn_{j}", [128, 128]) for j in range(NT)]
    vr_b = [sb(f"vr_{j}", [128, 128]) for j in range(NT)]
    vi_b = [sb(f"vi_{j}", [128, 128]) for j in range(NT)]
    mt = [sb(f"mt_{j}", [128, 128]) for j in range(2)]

    pbank = [ps(f"pbank{j}", [128, 128]).ap() for j in range(4)]

    with stack:
        for j in range(NT):
            rows = slice(j * 128, (j + 1) * 128)
            P.dma("sync", wfr_b[j][:], wfr[rows, :])
            P.dma("sync", wfi_b[j][:], wfi[rows, :])
            P.dma("sync", cm_b[j][:], cmat[rows, :])
            P.dma("sync", sm_b[j][:], smat[rows, :])
            P.dma("gpsimd", cc_b[j][:], ccol[rows, :])
            P.dma("gpsimd", sc_b[j][:], scol[rows, :])
            P.dma("gpsimd", scn_b[j][:], scoln[rows, :])
        for e in range(NT):
            esl = slice(e * 128, (e + 1) * 128)
            pm = pbank[e % 2]
            P.mm(pm, [(wfr_b[o][:, esl], cc_b[o][:]) for o in range(NT)] +
                 [(wfi_b[o][:, esl], scn_b[o][:]) for o in range(NT)],
                 tag="vr")
            P.copy("vector", vr_b[e][:], pm)
            pm2 = pbank[2 + e % 2]
            P.mm(pm2, [(wfr_b[o][:, esl], sc_b[o][:]) for o in range(NT)] +
                 [(wfi_b[o][:, esl], cc_b[o][:]) for o in range(NT)],
                 tag="vi")
            P.copy("vector", vi_b[e][:], pm2)
        for dtl in range(NT):
            dsl = slice(dtl * 128, (dtl + 1) * 128)
            pm = pbank[dtl % 2]
            P.mm(pm, [(cm_b[e][:, dsl], vr_b[e][:]) for e in range(NT)] +
                 [(sm_b[e][:, dsl], vi_b[e][:]) for e in range(NT)], tag="m")
            P.ts("vector", mt[dtl % 2][:], pm, 1.0 / D, ALU.mult)
            P.dma("gpsimd", mshard[dsl, :], mt[dtl % 2][:])
        ctx = P.emit()
        with ctx:
            pass
    return nc


def lm_in_maps(inputs):
    idx = np.arange(D)
    ang = 2.0 * np.pi * np.outer(idx, idx) / D
    C = np.cos(ang).astype(np.float32)
    Sm = np.sin(ang).astype(np.float32)
    wfr = np.ascontiguousarray(inputs["L1_ffn_w_freq_real"].T)
    wfi = np.ascontiguousarray(inputs["L1_ffn_w_freq_imag"].T)
    maps = []
    for c in range(8):
        cols = slice(128 * c, 128 * (c + 1))
        maps.append({
            "wfr": wfr, "wfi": wfi, "cmat": C, "smat": Sm,
            "ccol": np.ascontiguousarray(C[:, cols]),
            "scol": np.ascontiguousarray(Sm[:, cols]),
            "scoln": np.ascontiguousarray(-Sm[:, cols]),
        })
    return maps


def _col8(v):
    return np.ascontiguousarray(v.reshape(8, 128).T)


def l4_in_maps(x2_fm, oparts, M, inputs):
    w1t = np.ascontiguousarray(
        inputs["L1_attn_mlp_w1"].T.reshape(8, 128, 32, 128)
        .transpose(2, 1, 0, 3).reshape(32, 128, 1024))
    w2t = np.ascontiguousarray(
        inputs["L1_attn_mlp_w2"].T.reshape(32, 128, 8, 128)
        .transpose(2, 1, 0, 3).reshape(8, 128, 4096))
    b1t = np.ascontiguousarray(inputs["L1_attn_mlp_b1"].reshape(32, 128).T)
    common = {
        "w1t": w1t, "w2t": w2t, "b1t": b1t,
        "b2t": _col8(inputs["L1_attn_mlp_b2"]),
        "ln2g": _col8(inputs["L1_attn_ln2_g"]),
        "ln2b": _col8(inputs["L1_attn_ln2_b"]),
        "sbias": _col8(inputs["L1_ffn_bias"]),
        "m_in": np.ascontiguousarray(M.astype(np.float32)),
    }
    maps = []
    for c in range(8):
        b, i = c // 2, c % 2
        t0 = i * TH
        maps.append(dict(common,
                         x2h=np.ascontiguousarray(x2_fm[b][:, t0:t0 + TH]),
                         oa=np.ascontiguousarray(oparts[2 * b][:, t0:t0 + TH]),
                         ob=np.ascontiguousarray(
                             oparts[2 * b + 1][:, t0:t0 + TH])))
    return maps


_NC_CACHE = {}


def _get_nc(name, builder):
    if name not in _NC_CACHE:
        _NC_CACHE[name] = builder()
    return _NC_CACHE[name]


def kernel(**inputs):
    cores = list(range(8))
    x0_fm = [np.ascontiguousarray(inputs["L0_attn_x"][b].T) for b in range(4)]

    res = run_bass_kernel_spmd(_get_nc("lm", build_lm), lm_in_maps(inputs),
                               core_ids=cores)
    M = np.concatenate([res.results[c]["mshard"] for c in range(8)], axis=1)

    res = run_bass_kernel_spmd(_get_nc("l1", build_l1),
                               l1_in_maps(x0_fm, inputs), core_ids=cores)
    partials = [res.results[c]["part"] for c in range(8)]

    res = run_bass_kernel_spmd(_get_nc("l2", build_l2),
                               l2_in_maps(x0_fm, partials, inputs),
                               core_ids=cores)
    x2_fm = [np.ascontiguousarray(
        np.concatenate([res.results[2 * b]["x2o"],
                        res.results[2 * b + 1]["x2o"]], axis=1))
        for b in range(4)]

    res = run_bass_kernel_spmd(_get_nc("l3", lambda: build_l3(0)),
                               l3_in_maps(x2_fm, inputs), core_ids=cores)
    oparts = [res.results[c]["opart"] for c in range(8)]

    res = run_bass_kernel_spmd(_get_nc("l4", build_l4),
                               l4_in_maps(x2_fm, oparts, M, inputs),
                               core_ids=cores)
    out = np.empty((4, S, D), np.float32)
    for b in range(4):
        full = np.concatenate([res.results[2 * b]["outh"],
                               res.results[2 * b + 1]["outh"]], axis=1)
        out[b] = full.T
    return out
